# version 04
# baseline (speedup 1.0000x reference)
import numpy as np
import ml_dtypes
from contextlib import ExitStack

import concourse.bass as bass
import concourse.mybir as mybir
from concourse.bass_utils import run_bass_kernel_spmd

F32 = mybir.dt.float32
BF16 = mybir.dt.bfloat16
AF = mybir.ActivationFunctionType
ALU = mybir.AluOpType
AX = mybir.AxisListType

SEQ = 4096
OWN = 2048
CH = 512
NCH = OWN // CH
D = 1024
H = 16
DH = 64
NPAIR = 8
HID = 2048
NEXP = 8
EPS = 1e-5
SHIFT = 40.0

SBUF_BYTES = 207 * 1024
VTW = H * (DH + 1)


class Arena:

    def __init__(self, nc, st):
        self.words = SBUF_BYTES // 4
        self.t = st.enter_context(nc.sbuf_tensor("arena", [128, self.words], F32))
        self.off = 0
        self.marks = []

    def _alloc(self, bytes_pp):
        off = (self.off + 511) // 512 * 512
        assert off + bytes_pp <= SBUF_BYTES, f"arena overflow: {off}+{bytes_pp}"
        self.off = off + bytes_pp
        return off

    def f32(self, cols, rows=128):
        off = self._alloc(cols * 4)
        return self.t[0:rows, off // 4: off // 4 + cols]

    def bf16(self, cols, rows=128):
        off = self._alloc(cols * 2 + (cols * 2) % 4)
        nw = (cols * 2 + 3) // 4
        return self.t[0:rows, off // 4: off // 4 + nw].bitcast(BF16)[:, 0:cols]

    def mark(self):
        self.marks.append(self.off)

    def reset(self):
        self.off = self.marks.pop()


class Prog:
    QUEUES = ("sync", "gsdma", "tensor", "vector", "scalar")
    ENG = ("tensor", "vector", "scalar")

    def __init__(self, nc, st):
        self.nc = nc
        self.st = st
        self.sems = {q: st.enter_context(nc.semaphore("sem_" + q)) for q in self.ENG}
        self.sems["out"] = st.enter_context(nc.semaphore("sem_out"))
        self.ops = []
        self.labels = {}
        self.counts = {q: 0 for q in self.ENG}
        self.counts["out"] = 0
        self.watermarks = {}
        self.uid = 0

    def lab(self, tag="l"):
        self.uid += 1
        return f"{tag}{self.uid}"

    def _chan(self, name):
        if name not in self.sems:
            self.sems[name] = self.st.enter_context(self.nc.semaphore("sem_" + name))
            self.counts[name] = 0
        return name

    def op(self, queue, fn, waits=(), signal=None, out_store=False):
        if out_store:
            return self.dma("out", "sync", fn, waits=waits, signal=signal)
        assert queue in self.ENG, queue
        if signal is False:
            self.ops.append((queue, fn, tuple(w for w in waits if w), None, queue))
            return None
        if signal is None:
            signal = self.lab(queue)
        self.counts[queue] += 1
        assert signal not in self.labels, signal
        self.labels[signal] = (queue, self.counts[queue])
        self.ops.append((queue, fn, tuple(w for w in waits if w), signal, queue))
        return signal

    def dma(self, channel, queue, fn, waits=(), signal=None, serialize=False):
        assert queue in ("sync", "gsdma")
        ch = self._chan(channel)
        if signal is None:
            signal = self.lab(channel)
        self.counts[ch] += 16
        assert signal not in self.labels, signal
        self.labels[signal] = (ch, self.counts[ch])
        self.ops.append((queue, fn, tuple(w for w in waits if w), signal, ch))
        if serialize:
            self.raw_wait(queue, signal)
        return signal

    def chain(self, queue, fns, waits=()):
        lb = None
        for i, fn in enumerate(fns):
            w = list(waits) if i == 0 else [lb]
            lb = self.op(queue, fn, waits=w)
        return lb

    def raw_wait(self, queue, label):
        self.ops.append((queue, None, (label,), None, None))

    def emit_segment(self, name):
        per_q = {q: [] for q in self.QUEUES}
        for rec in self.ops:
            per_q[rec[0]].append(rec)
        self.ops = []
        active = [q for q in self.QUEUES if per_q[q]]

        with self.nc.Block(name) as blk:
            for q in active:
                def mk(qq):
                    def body(eng):
                        for (_, fn, waits, signal, semkey) in per_q[qq]:
                            for labw in waits:
                                pq, cnt = self.labels[labw]
                                key = (qq, pq)
                                if self.watermarks.get(key, 0) < cnt:
                                    eng.wait_ge(self.sems[pq], cnt)
                                    self.watermarks[key] = cnt
                            if fn is None:
                                continue
                            instr = fn(eng)
                            if signal is not None:
                                instr.then_inc(self.sems[semkey],
                                               16 if semkey not in self.ENG else 1)
                    return body
                getattr(blk, "gpsimd" if q == "gsdma" else q)(mk(q))
        for q in self.QUEUES:
            for pq in list(self.counts):
                self.watermarks[(q, pq)] = self.counts[pq]


def build_nc(debug=False):
    nc = bass.Bass("TRN2", target_bir_lowering=False)
    dt = nc.dram_tensor

    x_roll = dt("x_roll", [SEQ, D], F32, kind="ExternalInput").ap()
    gamma = dt("gamma", [1, D], F32, kind="ExternalInput").ap()
    wq = dt("wq", [D, D], F32, kind="ExternalInput").ap()
    wkv = dt("wkv", [D, 2 * D], F32, kind="ExternalInput").ap()
    wbt2 = dt("wbt2", [128, H * DH], F32, kind="ExternalInput").ap()
    wo = dt("wo", [128, 8 * D], F32, kind="ExternalInput").ap()
    w1t = dt("w1t", [NEXP, 128, 8 * HID], F32, kind="ExternalInput").ap()
    w2t = dt("w2t", [NEXP, 128, 16 * D], F32, kind="ExternalInput").ap()
    b1c = dt("b1c", [128, NEXP * (HID // 128)], F32, kind="ExternalInput").ap()
    b2 = dt("b2", [NEXP, D], F32, kind="ExternalInput").ap()
    wgt = dt("wgt", [128, 8 * NEXP], F32, kind="ExternalInput").ap()
    bg = dt("bg", [1, NEXP], F32, kind="ExternalInput").ap()
    costab = dt("costab", [128, SEQ], BF16, kind="ExternalInput").ap()
    sintab = dt("sintab", [128, SEQ], BF16, kind="ExternalInput").ap()
    identb = dt("identb", [128, 128], BF16, kind="ExternalInput").ap()
    rotmat = dt("rotmat", [128, 128], BF16, kind="ExternalInput").ap()
    ones128b = dt("ones128b", [128, 128], BF16, kind="ExternalInput").ap()
    onescol_b = dt("onescol_b", [128, 1], BF16, kind="ExternalInput").ap()
    onesrow_f = dt("onesrow_f", [1, 128], F32, kind="ExternalInput").ap()
    sel2 = dt("sel2", [2, 128], BF16, kind="ExternalInput").ap()
    identf = dt("identf", [128, 128], F32, kind="ExternalInput").ap()

    out = dt("out", [OWN, D], F32, kind="ExternalOutput").ap()
    dbg = {}
    if debug:
        for nm, shp, dtp in [
            ("kT", [128, 8 * OWN], BF16), ("qT", [128, 8 * OWN], BF16),
            ("v", [128, 16 * VTW], BF16), ("aoT", [128, 8 * OWN], BF16),
            ("res1", [OWN, D], F32), ("comb", [128, 128], F32),
            ("kTA", [128, 8 * OWN], BF16), ("vA", [128, 16 * VTW], BF16),
            ("xnT", [128, 8 * CH], BF16),
            ("sums", [128, 256], F32), ("ln2T", [128, 8 * OWN], BF16),
            ("bias", [1, 32], F32),
        ]:
            dbg[nm] = dt("dbg_" + nm, shp, dtp, kind="ExternalOutput").ap()

    with ExitStack() as st:
        P = Prog(nc, st)
        A = Arena(nc, st)
        pb = [st.enter_context(nc.psum_tensor(f"pb{i}", [128, 512], F32)) for i in range(8)]
        pbT = pb[7][:].bitcast(BF16)
        L = P.lab
        bank_owner = {}

        def dma(dst, src):
            return lambda e: e.dma_start(out=dst, in_=src)

        c_ident = A.bf16(128)
        c_rot = A.bf16(128)
        c_ones128b = A.bf16(128)
        c_onescol = A.bf16(1)
        c_onesrow = A.f32(128, rows=1)
        c_wbt = A.bf16(H * DH)
        c_b1 = A.f32(NEXP * (HID // 128))
        c_b2 = A.bf16(D, rows=NEXP)
        c_wg = A.bf16(8 * NEXP)
        c_bg = A.f32(NEXP, rows=1)
        c_sel2 = A.bf16(128, rows=2)
        c_identf = A.f32(128)
        s_bias = A.f32(32)
        s_bv = A.f32(32, rows=1)
        s_mx = A.f32(32, rows=1)

        A.mark()
        a_x = [A.f32(D) for _ in range(2)]
        a_ln1 = A.f32(D)
        a_xn = [A.bf16(D) for _ in range(2)]
        a_st = A.f32(16)
        a_mv = A.f32(4)
        a_rs = A.f32(8)
        a_xnT = A.bf16(8 * CH)
        a_wstg = [A.bf16(4 * D) for _ in range(2)]
        a_scr = A.f32(2048)
        a_kraw = [a_scr[:, 0:256].bitcast(BF16), a_scr[:, 256:512].bitcast(BF16)]
        a_rt1 = [a_scr[:, 512:768].bitcast(BF16), a_scr[:, 768:1024].bitcast(BF16)]
        a_rt2 = [a_scr[:, 1024:1280].bitcast(BF16), a_scr[:, 1280:1536].bitcast(BF16)]
        a_qhat = a_scr[:, 1536:1792].bitcast(BF16)
        a_sq = a_scr[:, 1792:2048].bitcast(BF16)
        a_gbc = A.bf16(D)
        a_kT = A.bf16(8 * OWN)
        a_v = A.bf16(16 * VTW)
        s_r64 = a_scr[:, 0:1024]
        a_qT = A.bf16(8 * OWN)
        a_aoT = A.bf16(8 * OWN)
        a_pt = [A.bf16(512) for _ in range(4)]
        a_cos = A.bf16(OWN)
        a_sin = A.bf16(OWN)
        s_sumsT = A.f32(256)
        s_sumb16 = A.bf16(512, rows=2)
        assert A.off <= SBUF_BYTES, A.off

        l_ident = P.dma("cs", "sync", dma(c_ident, identb[:]), serialize=True)
        l_rot = P.dma("cs", "sync", dma(c_rot, rotmat[:]), serialize=True)
        l_o128 = P.dma("cs", "sync", dma(c_ones128b, ones128b[:]), serialize=True)
        l_ocol = P.dma("cs", "sync", dma(c_onescol, onescol_b[:]), serialize=True)
        l_orow = P.dma("cs", "sync", dma(c_onesrow, onesrow_f[:]), serialize=True)
        l_bg = P.dma("cs", "sync", dma(c_bg, bg[:]), serialize=True)
        l_b1 = P.dma("cs", "sync", dma(c_b1, b1c[:]), serialize=True)
        l_gam = P.dma("cs", "sync", dma(a_x[0][0:1, 0:D], gamma[:]), serialize=True)
        l_wbt = P.dma("cg", "gsdma", dma(c_wbt, wbt2[:]), serialize=True)
        l_b2 = P.dma("cg", "gsdma", dma(c_b2, b2[:]), serialize=True)
        l_wg = P.dma("cg", "gsdma", dma(c_wg, wgt[:]), serialize=True)
        l_sel2 = P.dma("cg", "gsdma", dma(c_sel2, sel2[:]), serialize=True)
        l_idf = P.dma("cg", "gsdma", dma(c_identf, identf[:]), serialize=True)
        P.op("scalar", lambda e: e.activation(out=c_wbt, in_=c_wbt, func=AF.Copy, scale=0.125),
             waits=[l_wbt], signal="wbt_ok")
        for i in range(2):
            P.op("tensor", (lambda i: lambda e: e.matmul(
                pb[i][:, :], c_onesrow, a_x[0][0:1, i * 512:(i + 1) * 512],
                start=True, stop=True))(i),
                waits=[l_orow, l_gam], signal=f"gbcmm{i}")
            P.op("vector", (lambda i: lambda e: e.tensor_copy(
                a_gbc[:, i * 512:(i + 1) * 512], pb[i][:, :]))(i),
                waits=[f"gbcmm{i}"], signal=f"gbc{i}")
        P.op("vector", lambda e: e.memset(s_mx[0:1, :], -1e30), signal="mxinit")
        for tt in range(16):
            P.op("vector", (lambda tt: lambda e: e.memset(
                a_v[:, tt * VTW:(tt + 1) * VTW].rearrange(
                    "p (h u) -> p h u", u=DH + 1)[:, :, DH:DH + 1], 1.0))(tt),
                signal=f"vone{tt}")
        P.op("scalar", lambda e: e.preload_act_table(AF.Sqrt), signal=False)
        P.emit_segment("s0")

        def ln_block(src, dst, deps, tag, parity, gamma_mul):
            rsb = parity * 4
            l1 = P.chain("vector", [
                lambda e: (e.bn_stats(a_st[:, 0:6], src[:, 0:512]),
                           e.bn_stats(a_st[:, 6:12], src[:, 512:1024]))[-1],
                lambda e: e.bn_aggr(a_mv[:, 0:2], a_st[:, 0:12]),
                lambda e: e.tensor_scalar(out=a_rs[:, rsb:rsb + 1], in0=a_mv[:, 1:2],
                                          scalar1=EPS, scalar2=None, op0=ALU.add),
                lambda e: e.reciprocal(a_rs[:, rsb:rsb + 1], a_rs[:, rsb:rsb + 1]),
            ], waits=deps)
            l2 = P.op("scalar", lambda e: e.activation(
                out=a_rs[:, rsb + 1:rsb + 2], in_=a_rs[:, rsb:rsb + 1], func=AF.Sqrt),
                waits=[l1])
            l3 = P.chain("vector", [
                lambda e: e.tensor_scalar(out=a_rs[:, rsb + 2:rsb + 3], in0=a_mv[:, 0:1],
                                          scalar1=a_rs[:, rsb + 1:rsb + 2], scalar2=-1.0,
                                          op0=ALU.mult, op1=ALU.mult),
                lambda e: e.tensor_scalar(out=a_ln1[:], in0=src,
                                          scalar1=a_rs[:, rsb + 1:rsb + 2],
                                          scalar2=a_rs[:, rsb + 2:rsb + 3],
                                          op0=ALU.mult, op1=ALU.add),
                lambda e: (e.bn_stats(a_st[:, 0:6], a_ln1[:, 0:512]),
                           e.bn_stats(a_st[:, 6:12], a_ln1[:, 512:1024]))[-1],
                lambda e: e.bn_aggr(a_mv[:, 2:4], a_st[:, 0:12]),
                lambda e: e.tensor_scalar(out=a_rs[:, rsb:rsb + 1], in0=a_mv[:, 3:4],
                                          scalar1=EPS, scalar2=None, op0=ALU.add),
                lambda e: e.reciprocal(a_rs[:, rsb:rsb + 1], a_rs[:, rsb:rsb + 1]),
            ], waits=[l2])
            l4 = P.op("scalar", lambda e: e.activation(
                out=a_rs[:, rsb + 1:rsb + 2], in_=a_rs[:, rsb:rsb + 1], func=AF.Sqrt),
                waits=[l3])
            l5 = P.chain("vector", [
                lambda e: e.tensor_scalar(out=a_rs[:, rsb + 2:rsb + 3], in0=a_mv[:, 2:3],
                                          scalar1=a_rs[:, rsb + 1:rsb + 2], scalar2=-1.0,
                                          op0=ALU.mult, op1=ALU.mult),
                lambda e: e.tensor_scalar(out=dst, in0=a_ln1[:],
                                          scalar1=a_rs[:, rsb + 1:rsb + 2],
                                          scalar2=a_rs[:, rsb + 2:rsb + 3],
                                          op0=ALU.mult, op1=ALU.add),
                lambda e: e.tensor_tensor(out=dst, in0=dst, in1=a_gbc, op=ALU.mult),
            ], waits=[l4, "gbc0", "gbc1"] + (gamma_mul or []))
            return l5

        kraw_free = [None, None]

        def rope(bank, toksl, dst_ap, buf, tag, l_cos, l_sin, dst_free=None):
            kraw, rt1, rt2 = a_kraw[buf], a_rt1[buf], a_rt2[buf]
            wev = [bank_owner.get(bank)] + (kraw_free[buf] or [])
            l_ev = P.op("scalar", lambda e: e.activation(out=kraw, in_=pb[bank][:, :],
                                                         func=AF.Copy),
                        waits=wev, signal=L(tag))
            l_ro = P.op("tensor", lambda e: e.matmul(pb[bank][:, :], c_rot, kraw,
                                                     start=True, stop=True),
                        waits=[l_ev, l_rot], signal=L(tag))
            l_t1 = P.op("vector", lambda e: e.tensor_tensor(
                out=rt1, in0=kraw, in1=a_cos[:, toksl], op=ALU.mult),
                waits=[l_ev, l_cos], signal=L(tag))
            l_t2 = P.op("vector", lambda e: e.tensor_tensor(
                out=rt2, in0=pb[bank][:, :], in1=a_sin[:, toksl], op=ALU.mult),
                waits=[l_ro, l_t1, l_sin], signal=L(tag))
            l_dst = P.op("vector", lambda e: e.tensor_tensor(
                out=dst_ap, in0=rt1, in1=rt2, op=ALU.add),
                waits=[l_t2] + ([dst_free] if dst_free else []), signal=L(tag))
            kraw_free[buf] = [l_ro, l_t1]
            bank_owner[bank] = l_t2
            return l_dst

        def warm_burst(n=24):
            def f(e):
                r = None
                for _ in range(n):
                    r = e.matmul(pb[7][0:1, :], c_ones128b[0:1, 0:1],
                                 c_b2[0:1, 0:512], start=True, stop=True)
                return r
            P.op("tensor", f, signal=False)

        prev_sq = [None]
        mx_last = [None]

        def normsq(src_ap, bank, mslot, deps, tag):
            l_sq = P.op("vector", lambda e: e.tensor_tensor(
                out=a_sq, in0=src_ap, in1=src_ap, op=ALU.mult),
                waits=list(deps) + ([prev_sq[0]] if prev_sq[0] else []), signal=L(tag))
            l_mm = P.op("tensor", lambda e: e.matmul(pb[bank][0:1, :], c_onescol, a_sq,
                                                     start=True, stop=True),
                        waits=[l_sq, l_ocol, bank_owner.get(bank)], signal=L(tag))
            prev_sq[0] = l_mm
            l_mx = P.chain("vector", [
                lambda e: e.tensor_reduce(s_mx[0:1, 31:32], pb[bank][0:1, :],
                                          axis=AX.X, op=ALU.max),
                lambda e: e.tensor_tensor(out=s_mx[0:1, mslot:mslot + 1],
                                          in0=s_mx[0:1, mslot:mslot + 1],
                                          in1=s_mx[0:1, 31:32], op=ALU.max),
            ], waits=[l_mm, "mxinit"])
            bank_owner[bank] = l_mx
            mx_last[0] = l_mx
            return l_mx

        def proj_pass(pass_id):
            tag = f"P{pass_id}"
            base = pass_id * OWN
            if pass_id == 1:
                P.op("scalar", lambda e: e.preload_act_table(AF.Sqrt), signal=False)
            l_cos = P.dma("tcos", "sync", dma(a_cos, costab[:, base:base + OWN]))
            l_sin = P.dma("tsin", "sync", dma(a_sin, sintab[:, base:base + OWN]))
            warm_burst()
            wstg_free = [None, None]
            x_free = [None, None]
            xn_free = [None, None]
            xnT_free = None
            qhat_free = None
            for tc in range(NCH):
                xn_labels = []
                tr_labels = []
                for t4 in range(4):
                    tile = tc * 4 + t4
                    xb = a_x[t4 % 2]
                    l_x = P.dma(f"x{t4 % 2}", "sync",
                                dma(xb[:], x_roll[base + tile * 128: base + tile * 128 + 128, :]),
                                waits=[x_free[t4 % 2]])
                    xn = a_xn[t4 % 2]
                    l_xn = ln_block(xb[:], xn[:], [l_x], tag, t4 % 2,
                                    gamma_mul=[xn_free[t4 % 2]] if xn_free[t4 % 2] else [])
                    x_free[t4 % 2] = l_xn
                    wtr = [l_xn, l_ident, bank_owner.get(7)]
                    l_tr = P.op("tensor", (lambda xn: lambda e: [
                        e.transpose(pbT[:, d * 128:(d + 1) * 128],
                                    xn[:, d * 128:(d + 1) * 128], c_ident)
                        for d in range(8)][-1])(xn),
                        waits=wtr, signal=L(tag))
                    bank_owner[7] = l_tr
                    xn_free[t4 % 2] = l_tr
                    tr_labels.append(l_tr)
                    l_ev = P.op("vector", (lambda t4: lambda e: e.tensor_copy(
                        a_xnT.rearrange("p (d c) -> p d c", d=8)[:, :, t4 * 128:(t4 + 1) * 128],
                        pbT[:].rearrange("p (d c) -> p d c", d=8)))(t4),
                        waits=[l_tr] + ([xnT_free] if (xnT_free and t4 == 0) else []),
                        signal=L(tag))
                    bank_owner[7] = l_ev
                    xn_labels.append(l_ev)

                def wload(which, half):
                    sbuf = a_wstg[half]
                    r0 = half * 512
                    if which == "k":
                        src = wkv[r0:r0 + 512, 0:D]
                    elif which == "v":
                        src = wkv[r0:r0 + 512, D:2 * D]
                    else:
                        src = wq[r0:r0 + 512, 0:D]
                    src = src.rearrange("(c p) f -> p c f", p=128)
                    dst = sbuf.rearrange("p (c f) -> p c f", f=D)
                    l_w = P.dma(f"ws{half}", "gsdma", dma(dst, src),
                                waits=[wstg_free[half]])
                    return sbuf, l_w

                def proj_mm(which, d, dd, sbuf, l_w, first_waits):
                    if which == "k" or which == "q":
                        def f(e, sbuf=sbuf, d=d, dd=dd):
                            r = None
                            for kf in range(8):
                                r = e.matmul(pb[kf][:, :],
                                             sbuf[:, dd * D + kf * 128: dd * D + (kf + 1) * 128],
                                             a_xnT[:, d * CH:(d + 1) * CH],
                                             start=(d == 0), stop=(d == 7))
                            return r
                    else:
                        def f(e, sbuf=sbuf, d=d, dd=dd):
                            r = None
                            for tt in range(4):
                                for vh in range(2):
                                    r = e.matmul(pb[tt * 2 + vh][:, :],
                                                 a_xnT[:, d * CH + tt * 128: d * CH + (tt + 1) * 128],
                                                 sbuf[:, dd * D + vh * 512: dd * D + (vh + 1) * 512],
                                                 start=(d == 0), stop=(d == 7))
                            return r
                    return P.op("tensor", f, waits=[l_w] + first_waits, signal=L(tag))

                def proj_sweep(which):
                    last_mm = None
                    for half in range(2):
                        sbuf, l_w = wload(which, half)
                        for dd in range(4):
                            d = half * 4 + dd
                            fw = []
                            if d == 0:
                                fw = xn_labels + [bank_owner.get(b) for b in range(8)]
                            if last_mm:
                                fw = fw + [last_mm]
                            last_mm = proj_mm(which, d, dd, sbuf, l_w, fw)
                        wstg_free[half] = last_mm
                    for b in range(8):
                        bank_owner[b] = last_mm
                    return last_mm

                toksl = slice(tc * CH, (tc + 1) * CH)

                last_mm = proj_sweep("k")
                for kf in range(8):
                    dst = a_kT[:, kf * OWN + tc * CH: kf * OWN + (tc + 1) * CH]
                    l_dst = rope(kf, toksl, dst, kf % 2, tag, l_cos, l_sin)
                    normsq(dst, kf, 8 + 8 * pass_id + kf, [l_dst], tag)

                last_mm = proj_sweep("v")
                for tt in range(4):
                    for vh in range(2):
                        b = tt * 2 + vh
                        vtile = tc * 4 + tt
                        l_ev = P.op("vector", (lambda b, vtile, vh: lambda e: e.tensor_copy(
                            a_v[:, vtile * VTW + vh * 8 * (DH + 1):].rearrange(
                                "p (h u) -> p h u", u=DH + 1)[:, 0:8, 0:DH],
                            pb[b][:].rearrange("p (h d) -> p h d", d=DH)))(b, vtile, vh),
                            waits=[bank_owner[b]], signal=L(tag))
                        bank_owner[b] = l_ev

                if pass_id == 0:
                    last_mm = proj_sweep("q")
                    for qf in range(8):
                        l_dst = rope(qf, toksl, a_qhat, qf % 2, tag, l_cos, l_sin,
                                     dst_free=qhat_free)
                        h0, h1 = 2 * qf, 2 * qf + 1
                        l_bl = P.op("tensor", (lambda qf, h0, h1: lambda e: (
                            e.matmul(pb[qf][0:64, :], c_wbt[0:64, h0 * 64:(h0 + 1) * 64],
                                     a_qhat[0:64, :], start=True, stop=True),
                            e.matmul(pb[qf][64:128, :], c_wbt[64:128, h1 * 64:(h1 + 1) * 64],
                                     a_qhat[64:128, :], start=True, stop=True))[-1])(qf, h0, h1),
                            waits=[l_dst, "wbt_ok", bank_owner.get(qf)], signal=L(tag))
                        qhat_free = l_bl
                        bank_owner[qf] = l_bl
                        dstq = a_qT[:, qf * OWN + tc * CH: qf * OWN + (tc + 1) * CH]
                        l_ev = P.op("vector", (lambda dstq, qf: lambda e: e.tensor_copy(
                            dstq, pb[qf][:, :]))(dstq, qf),
                            waits=[l_bl], signal=L(tag))
                        bank_owner[qf] = l_ev
                        normsq(dstq, qf, qf, [l_ev], tag)
                xnT_free = last_mm

        def bias_build(pass_id):
            tag = f"B{pass_id}"
            if pass_id == 0:
                l1 = P.op("vector", lambda e: e.tensor_tensor(
                    out=s_bv[0:1, 0:8], in0=s_mx[0:1, 0:8], in1=s_mx[0:1, 8:16], op=ALU.mult),
                    waits=[mx_last[0]], signal=L(tag))
                l2 = P.op("scalar", lambda e: e.activation(
                    out=s_bv[0:1, 0:8], in_=s_bv[0:1, 0:8], func=AF.Sqrt),
                    waits=[l1], signal=L(tag))
                l3 = P.op("vector", lambda e: e.tensor_scalar(
                    out=s_bv[0:1, 8:16], in0=s_bv[0:1, 0:8], scalar1=-1.0, scalar2=SHIFT,
                    op0=ALU.mult, op1=ALU.add), waits=[l2], signal=L(tag))
                lm = P.op("tensor", lambda e: e.matmul(
                    pb[0][:, 0:8], c_onesrow, s_bv[0:1, 8:16], start=True, stop=True),
                    waits=[l3, bank_owner.get(0)], signal=L(tag))
                bank_owner[0] = lm
                le = P.op("vector", lambda e: e.tensor_copy(s_bias[:, 0:8], pb[0][:, 0:8]),
                          waits=[lm], signal=L(tag))
                bank_owner[0] = le
                return le
            l1 = P.op("vector", lambda e: e.tensor_tensor(
                out=s_bv[0:1, 16:24], in0=s_mx[0:1, 0:8], in1=s_mx[0:1, 16:24], op=ALU.mult),
                waits=[mx_last[0]], signal=L(tag))
            l2 = P.op("scalar", lambda e: e.activation(
                out=s_bv[0:1, 16:24], in_=s_bv[0:1, 16:24], func=AF.Sqrt),
                waits=[l1], signal=L(tag))
            l3 = P.chain("vector", [
                lambda e: e.tensor_tensor(out=s_bv[0:1, 16:24], in0=s_bv[0:1, 16:24],
                                          in1=s_bv[0:1, 0:8], op=ALU.max),
                lambda e: e.tensor_scalar(out=s_bv[0:1, 24:32], in0=s_bv[0:1, 16:24],
                                          scalar1=-1.0, scalar2=SHIFT,
                                          op0=ALU.mult, op1=ALU.add),
                lambda e: e.tensor_tensor(out=s_bv[0:1, 16:24], in0=s_bv[0:1, 24:32],
                                          in1=s_bv[0:1, 8:16], op=ALU.subtract),
            ], waits=[l2])
            P.op("scalar", lambda e: e.preload_act_table(AF.Exp), waits=[l3],
                 signal=False)
            l4 = P.op("scalar", lambda e: e.activation(
                out=s_bv[0:1, 16:24], in_=s_bv[0:1, 16:24], func=AF.Exp),
                waits=[l3], signal=L(tag))
            lm = P.op("tensor", lambda e: (
                e.matmul(pb[0][:, 0:8], c_onesrow, s_bv[0:1, 24:32], start=True, stop=True),
                e.matmul(pb[1][:, 0:8], c_onesrow, s_bv[0:1, 16:24], start=True, stop=True))[-1],
                waits=[l4, bank_owner.get(0), bank_owner.get(1)], signal=L(tag))
            le = P.op("vector", lambda e: (
                e.tensor_copy(s_bias[:, 8:16], pb[0][:, 0:8]),
                e.tensor_copy(s_bias[:, 16:24], pb[1][:, 0:8]))[-1],
                waits=[lm], signal=L(tag))
            bank_owner[0] = le
            bank_owner[1] = le
            return le

        sums_last = [None]

        def sim_pass(pass_id, l_bias):
            tag = f"S{pass_id}"
            P.op("scalar", lambda e: e.preload_act_table(AF.Exp), waits=[l_bias],
                 signal=False)
            biascol = 0 if pass_id == 0 else 8
            warm_burst()
            for pr in range(NPAIR):
                h0, h1 = 2 * pr, 2 * pr + 1
                for qc in range(4):
                    qsl = slice(pr * OWN + qc * 512, pr * OWN + (qc + 1) * 512)
                    av_done = {}
                    for js in range(16):
                        for hp, h in ((0, h0), (1, h1)):
                            sb = hp * 2 + js % 2
                            ksl = slice(pr * OWN + js * 128, pr * OWN + (js + 1) * 128)
                            l_sim = P.op("tensor", (lambda sb, hp, ksl, qsl: lambda e: e.matmul(
                                pb[sb][:, :],
                                a_kT[hp * 64:(hp + 1) * 64, ksl],
                                a_qT[hp * 64:(hp + 1) * 64, qsl],
                                start=True, stop=True))(sb, hp, ksl, qsl),
                                waits=[bank_owner.get(sb)], signal=L(tag))
                            bank_owner[sb] = l_sim
                            pt = a_pt[hp * 2 + js % 2]
                            l_exp = P.op("scalar", (lambda sb, pt, pr: lambda e: e.activation(
                                out=pt, in_=pb[sb][:, :], func=AF.Exp,
                                bias=s_bias[:, biascol + pr: biascol + pr + 1]))(sb, pt, pr),
                                waits=[l_sim, av_done.get((hp, js - 2))], signal=L(tag))
                            bank_owner[sb] = l_exp
                            ob = 4 + hp
                            wav = [l_exp]
                            if js == 0:
                                wav += [bank_owner.get(ob)]
                            l_av = P.op("tensor", (lambda ob, h, pt, js: lambda e: (
                                e.matmul(pb[ob][0:65, :],
                                         a_v[:, js * VTW + h * (DH + 1):
                                             js * VTW + (h + 1) * (DH + 1)],
                                         pt, start=(js == 0), stop=(js == 15)),
                                e.matmul(pb[7][0:1, 0:64], c_ones128b[0:1, 0:1],
                                         c_ones128b[0:1, 0:64], start=True,
                                         stop=True))[0])(ob, h, pt, js),
                                waits=wav, signal=L(tag))
                            av_done[(hp, js)] = l_av
                            if js == 15:
                                bank_owner[ob] = l_av
                    l_e16 = P.op("vector", lambda e: (
                        e.tensor_copy(s_r64[64:65, 0:512], pb[4][64:65, :]),
                        e.tensor_copy(s_r64[64:65, 512:1024], pb[5][64:65, :]))[-1],
                        waits=[bank_owner[4], bank_owner[5], bank_owner.get(6)],
                        signal=L(tag))
                    bank_owner[4] = l_e16
                    bank_owner[5] = l_e16
                    l_tr = P.op("tensor", (lambda pr: lambda e: [
                        e.transpose(pb[6][:, qs * 16 + 2 * pr + hh: qs * 16 + 2 * pr + hh + 1],
                                    s_r64[64:65, hh * 512 + qs * 128: hh * 512 + (qs + 1) * 128],
                                    c_identf[64:65, 64:65])
                        for hh in range(2) for qs in range(4)][-1])(pr),
                        waits=[l_e16, l_idf, bank_owner[6]], signal=L(tag))
                    bank_owner[6] = l_tr
                    for hp, h in ((0, h0), (1, h1)):
                        ob = 4 + hp
                        asl = slice(pr * OWN + qc * 512, pr * OWN + (qc + 1) * 512)
                        dsta = a_aoT[hp * 64:hp * 64 + 64, asl]
                        if pass_id == 0:
                            l1 = P.op("vector", (lambda dsta, ob: lambda e: e.tensor_copy(
                                dsta, pb[ob][0:64, :]))(dsta, ob),
                                waits=[bank_owner[ob]], signal=L(tag))
                        else:
                            l1 = P.op("vector", (lambda dsta, ob, hp, pr: lambda e:
                                e.scalar_tensor_tensor(
                                    out=dsta, in0=dsta,
                                    scalar=s_bias[hp * 64:hp * 64 + 64, 16 + pr:17 + pr],
                                    in1=pb[ob][0:64, :], op0=ALU.mult, op1=ALU.add))(
                                        dsta, ob, hp, pr),
                                waits=[bank_owner[ob]], signal=L(tag))
                        bank_owner[ob] = l1
                    sTdst = s_sumsT.rearrange("p (s h) -> p s h", h=16)[
                        :, qc * 4:(qc + 1) * 4, 2 * pr:2 * pr + 2]
                    sTsrc = pb[6][:, 0:64].rearrange("p (s h) -> p s h", h=16)[
                        :, :, 2 * pr:2 * pr + 2]
                    if pass_id == 0:
                        l_sc = P.op("vector", (lambda sTdst, sTsrc: lambda e: e.tensor_copy(
                            sTdst, sTsrc))(sTdst, sTsrc),
                            waits=[l_tr], signal=L(tag))
                    else:
                        l_sc = P.op("vector", (lambda sTdst, sTsrc, pr: lambda e:
                            e.scalar_tensor_tensor(
                                out=sTdst, in0=sTdst,
                                scalar=s_bias[:, 16 + pr:17 + pr],
                                in1=sTsrc, op0=ALU.mult, op1=ALU.add))(sTdst, sTsrc, pr),
                            waits=[l_tr], signal=L(tag))
                    bank_owner[6] = l_sc
                    sums_last[0] = l_sc

        def attn_normalize():
            tag = "N"
            l_r = P.op("vector", lambda e: e.reciprocal(s_sumsT, s_sumsT),
                       waits=[sums_last[0]])
            for pr in range(NPAIR):
                for qc in range(4):
                    b = (pr * 4 + qc) % 4
                    asl = slice(pr * OWN + qc * 512, pr * OWN + (qc + 1) * 512)
                    src_r = s_sumsT.rearrange("p (s h) -> p s h", h=16)[
                        :, qc * 4:(qc + 1) * 4, 2 * pr:2 * pr + 2]
                    l_t = P.op("tensor", (lambda b, pr, qc: lambda e: [
                        e.transpose(pb[b][0:2, qs * 128:(qs + 1) * 128],
                                    s_sumsT[:, (qc * 4 + qs) * 16 + 2 * pr:
                                            (qc * 4 + qs) * 16 + 2 * pr + 2],
                                    c_identf)
                        for qs in range(4)][-1])(b, pr, qc),
                        waits=[l_r, l_idf, bank_owner.get(b)], signal=L(tag))
                    bank_owner[b] = l_t
                    l_c = P.op("vector", (lambda b: lambda e: e.tensor_copy(
                        s_sumb16[0:2, :], pb[b][0:2, :]))(b),
                        waits=[l_t], signal=L(tag))
                    bank_owner[b] = l_c
                    lm = P.op("tensor", (lambda b: lambda e: e.matmul(
                        pb[b][:, :], c_sel2[0:2, :], s_sumb16[0:2, :],
                        start=True, stop=True))(b),
                        waits=[l_c, l_sel2], signal=L(tag))
                    bank_owner[b] = lm
                    lt = P.op("vector", (lambda b, asl: lambda e: e.tensor_tensor(
                        out=a_aoT[:, asl], in0=a_aoT[:, asl], in1=pb[b][:, :],
                        op=ALU.mult))(b, asl),
                        waits=[lm], signal=L(tag))
                    bank_owner[b] = lt
            return lt

        import os as _os
        if not _os.environ.get("SKIP_A"):
            proj_pass(0)
            l_b = bias_build(0)
            if not _os.environ.get("SKIP_SIMA"):
                sim_pass(0, l_b)
        else:
            l_b = bias_build(0)
        if debug:
            P.op("sync", dma(dbg["kTA"][:], a_kT), waits=[l_b, sums_last[0]], out_store=True)
            ldbgA = P.op("sync", dma(dbg["vA"][:], a_v), waits=[l_b, sums_last[0]],
                         out_store=True)
            P.raw_wait("sync", ldbgA)
        P.emit_segment("attnA")
        import os as _os
        proj_pass(1)
        l_b = bias_build(1)
        if not _os.environ.get("SKIP_SIMB"):
            sim_pass(1, l_b)
            l_fin = attn_normalize()
        else:
            l_fin = l_b
        if debug:
            P.op("sync", dma(dbg["kT"][:], a_kT), waits=[l_fin], out_store=True)
            P.op("sync", dma(dbg["qT"][:], a_qT), waits=[l_fin], out_store=True)
            P.op("sync", dma(dbg["v"][:], a_v), waits=[l_fin], out_store=True)
            P.op("sync", dma(dbg["aoT"][:], a_aoT), waits=[l_fin], out_store=True)
            P.op("sync", dma(dbg["sums"][:], s_sumsT), waits=[l_fin], out_store=True)
            P.op("sync", dma(dbg["xnT"][:], a_xnT), waits=[l_fin], out_store=True)
            ldbg = P.op("sync", dma(dbg["bias"][:], s_bv[0:1, 0:32]), waits=[l_fin],
                        out_store=True)
            P.raw_wait("sync", ldbg)
        P.emit_segment("attnB")

        A.reset()
        A.mark()
        r_res = [A.f32(D) for _ in range(16)]
        f_w1 = A.bf16(8 * HID)
        r_wo = f_w1.bitcast(BF16)[:, 0:8 * D] if False else None
        r_wo = f_w1[:, 0:8 * D]
        f_ln2T = A.bf16(8 * OWN)
        f_w2 = A.bf16(16 * D)
        f_hT = A.bf16(16 * 512)
        f_comb = A.f32(128)
        f_combb = A.bf16(8)
        f_combT = A.bf16(OWN, rows=8)
        f_xn = A.bf16(D)
        f_g = A.f32(40)
        f_st = A.f32(16)
        f_mv = A.f32(4)
        f_rs = A.f32(8)
        assert A.off <= SBUF_BYTES, A.off

        bank_owner.clear()
        tag = "O"
        l_wo = P.dma("w1", "gsdma", dma(r_wo, wo[:]))
        res_labels = [None] * 16
        res_prev = [None] * 4
        warm_burst()
        for m in range(16):
            l_x = P.dma(f"rs{m % 4}", "sync",
                        dma(r_res[m][:], x_roll[m * 128:(m + 1) * 128, :]),
                        waits=[res_prev[m % 4]])
            for oh in range(2):
                b = (m * 2 + oh) % 4
                lm = P.op("tensor", (lambda m, oh, b: lambda e: (
                    [e.matmul(pb[b][:, :],
                              a_aoT[:, c * OWN + m * 128: c * OWN + (m + 1) * 128],
                              r_wo[:, c * D + oh * 512: c * D + (oh + 1) * 512],
                              start=(c == 0), stop=(c == 7))
                     for c in range(8)],
                    e.matmul(pb[7][0:1, 0:64], c_ones128b[0:1, 0:1],
                             c_ones128b[0:1, 0:64], start=True, stop=True))[0][-1])(m, oh, b),
                    waits=[l_wo, bank_owner.get(b)], signal=L(tag))
                bank_owner[b] = lm
                le = P.op("vector", (lambda m, oh, b: lambda e: e.tensor_tensor(
                    out=r_res[m][:, oh * 512:(oh + 1) * 512],
                    in0=r_res[m][:, oh * 512:(oh + 1) * 512],
                    in1=pb[b][:, :], op=ALU.add))(m, oh, b),
                    waits=[lm, l_x], signal=L(tag))
                bank_owner[b] = le
                res_labels[m] = le
            res_prev[m % 4] = res_labels[m]
        if debug:
            ldbg = None
            for m in range(16):
                ldbg = P.op("sync", dma(dbg["res1"][m * 128:(m + 1) * 128, :], r_res[m][:]),
                            waits=[res_labels[m]], out_store=True)
            P.raw_wait("sync", ldbg)
        P.emit_segment("oproj")

        tag = "F"
        ln_done = [None] * 16
        P.op("scalar", lambda e: e.preload_act_table(AF.Sqrt), signal=False)
        fxn_free = None
        for m in range(16):
            src = r_res[m][:]
            rsb = (m % 2) * 4
            l1 = P.chain("vector", [
                (lambda src: lambda e: (e.bn_stats(f_st[:, 0:6], src[:, 0:512]),
                                        e.bn_stats(f_st[:, 6:12], src[:, 512:1024]))[-1])(src),
                (lambda rsb: lambda e: e.bn_aggr(f_mv[:, 0:2], f_st[:, 0:12]))(rsb),
                (lambda rsb: lambda e: e.tensor_scalar(
                    out=f_rs[:, rsb:rsb + 1], in0=f_mv[:, 1:2], scalar1=EPS,
                    scalar2=None, op0=ALU.add))(rsb),
                (lambda rsb: lambda e: e.reciprocal(
                    f_rs[:, rsb:rsb + 1], f_rs[:, rsb:rsb + 1]))(rsb),
            ], waits=[res_labels[m]])
            l2 = P.op("scalar", (lambda rsb: lambda e: e.activation(
                out=f_rs[:, rsb + 1:rsb + 2], in_=f_rs[:, rsb:rsb + 1], func=AF.Sqrt))(rsb),
                waits=[l1], signal=L(tag))
            l3 = P.chain("vector", [
                (lambda rsb: lambda e: e.tensor_scalar(
                    out=f_rs[:, rsb + 2:rsb + 3], in0=f_mv[:, 0:1],
                    scalar1=f_rs[:, rsb + 1:rsb + 2], scalar2=-1.0,
                    op0=ALU.mult, op1=ALU.mult))(rsb),
                (lambda src, rsb: lambda e: e.tensor_scalar(
                    out=f_xn, in0=src, scalar1=f_rs[:, rsb + 1:rsb + 2],
                    scalar2=f_rs[:, rsb + 2:rsb + 3], op0=ALU.mult, op1=ALU.add))(src, rsb),
            ], waits=[l2] + ([fxn_free] if fxn_free else []))
            l_tr = P.op("tensor", lambda e: [
                e.transpose(pbT[:, d * 128:(d + 1) * 128],
                            f_xn[:, d * 128:(d + 1) * 128], c_ident)
                for d in range(8)][-1],
                waits=[l3, l_ident, bank_owner.get(7)], signal=L(tag))
            bank_owner[7] = l_tr
            fxn_free = l_tr
            l_ev = P.op("vector", (lambda m: lambda e: e.tensor_copy(
                f_ln2T.rearrange("p (d c) -> p d c", d=8)[:, :, m * 128:(m + 1) * 128],
                pbT[:].rearrange("p (d c) -> p d c", d=8)))(m),
                waits=[l_tr], signal=L(tag))
            bank_owner[7] = l_ev
            ln_done[m] = l_ev

        P.op("scalar", lambda e: e.preload_act_table(AF.Exp), signal=False)
        comb_ready = [None] * 16
        combT_ready = [None] * 16
        for m in range(16):
            b = m % 2
            lm = P.op("tensor", (lambda m, b: lambda e: (
                [e.matmul(pb[b][:, 0:8],
                          f_ln2T[:, d * OWN + m * 128: d * OWN + (m + 1) * 128],
                          c_wg[:, d * 8:(d + 1) * 8], start=(d == 0), stop=False)
                 for d in range(8)],
                e.matmul(pb[b][:, 0:8], c_onesrow, c_bg, start=False, stop=True))[1])(m, b),
                waits=[l_wg, l_bg, l_orow, ln_done[m], bank_owner.get(b)], signal=L(tag))
            bank_owner[b] = lm
            l_g1 = P.chain("vector", [
                (lambda b: lambda e: e.tensor_copy(f_g[:, 0:8], pb[b][:, 0:8]))(b),
                lambda e: e.tensor_reduce(f_g[:, 8:9], f_g[:, 0:8], axis=AX.X, op=ALU.max),
                lambda e: e.tensor_scalar(out=f_g[:, 16:24], in0=f_g[:, 0:8],
                                          scalar1=f_g[:, 8:9], scalar2=None,
                                          op0=ALU.is_equal),
                lambda e: e.scalar_tensor_tensor(out=f_g[:, 24:32], in0=f_g[:, 16:24],
                                                 scalar=-1e30, in1=f_g[:, 0:8],
                                                 op0=ALU.mult, op1=ALU.add),
                lambda e: e.tensor_reduce(f_g[:, 9:10], f_g[:, 24:32], axis=AX.X,
                                          op=ALU.max),
                lambda e: e.tensor_scalar(out=f_g[:, 10:11], in0=f_g[:, 8:9],
                                          scalar1=-1.0, scalar2=None, op0=ALU.mult),
            ], waits=[lm])
            bank_owner[b] = l_g1
            l_g2 = P.op("scalar", lambda e: e.activation(
                out=f_g[:, 11:12], in_=f_g[:, 9:10], func=AF.Exp, bias=f_g[:, 10:11]),
                waits=[l_g1], signal=L(tag))
            l_g3 = P.chain("vector", [
                lambda e: e.tensor_scalar(out=f_g[:, 12:13], in0=f_g[:, 11:12],
                                          scalar1=1.0, scalar2=None, op0=ALU.add),
                lambda e: e.reciprocal(f_g[:, 12:13], f_g[:, 12:13]),
                lambda e: e.tensor_tensor(out=f_g[:, 13:14], in0=f_g[:, 11:12],
                                          in1=f_g[:, 12:13], op=ALU.mult),
                lambda e: e.tensor_scalar(out=f_g[:, 24:32], in0=f_g[:, 0:8],
                                          scalar1=f_g[:, 9:10], scalar2=None,
                                          op0=ALU.is_equal),
                lambda e: e.tensor_scalar(out=f_g[:, 32:40], in0=f_g[:, 16:24],
                                          scalar1=-1.0, scalar2=1.0,
                                          op0=ALU.mult, op1=ALU.add),
                lambda e: e.tensor_tensor(out=f_g[:, 24:32], in0=f_g[:, 24:32],
                                          in1=f_g[:, 32:40], op=ALU.mult),
                lambda e: e.tensor_scalar(out=f_g[:, 16:24], in0=f_g[:, 16:24],
                                          scalar1=f_g[:, 12:13], scalar2=None,
                                          op0=ALU.mult),
                lambda e: e.tensor_scalar(out=f_g[:, 24:32], in0=f_g[:, 24:32],
                                          scalar1=f_g[:, 13:14], scalar2=None,
                                          op0=ALU.mult),
                (lambda m: lambda e: e.tensor_tensor(
                    out=f_comb[:, m * 8:(m + 1) * 8], in0=f_g[:, 16:24],
                    in1=f_g[:, 24:32], op=ALU.add))(m),
                (lambda m: lambda e: e.tensor_copy(
                    f_combb[:, 0:8], f_comb[:, m * 8:(m + 1) * 8]))(m),
            ], waits=[l_g2])
            comb_ready[m] = l_g3
            l_tc = P.op("tensor", lambda e: e.transpose(pbT[0:8, 0:128], f_combb[:, 0:8], c_ident),
                        waits=[l_g3, l_ident, bank_owner.get(7)], signal=L(tag))
            bank_owner[7] = l_tc
            l_te = P.op("vector", (lambda m: lambda e: e.tensor_copy(
                f_combT[0:8, m * 128:(m + 1) * 128], pbT[0:8, 0:128]))(m),
                waits=[l_tc], signal=L(tag))
            bank_owner[7] = l_te
            combT_ready[m] = l_te
        if debug:
            P.op("sync", dma(dbg["comb"][:], f_comb), waits=comb_ready, out_store=True)
            ldbg = P.op("sync", dma(dbg["ln2T"][:], f_ln2T), waits=ln_done, out_store=True)
            P.raw_wait("sync", ldbg)

        mm1_done = None
        mm2_done = None
        relu_done = {}
        store_done = [None] * 16
        for ex in range(NEXP):
            w1_ready = P.dma("w1", "gsdma", (lambda ex: lambda e: e.dma_start(
                out=f_w1, in_=w1t[ex]))(ex), waits=[mm1_done])
            w2_ready = P.dma("w2", "gsdma", (lambda ex: lambda e: e.dma_start(
                out=f_w2, in_=w2t[ex]))(ex), waits=[mm2_done])
            for tc in range(4):
                for hf in range(16):
                    b = hf % 4
                    wmm = [w1_ready, bank_owner.get(b)]
                    if ex == 0 and tc == 0 and hf < 4:
                        wmm += ln_done
                    lm = P.op("tensor", (lambda ex, tc, hf, b: lambda e: [
                        e.matmul(pb[b][:, :],
                                 f_w1[:, d * HID + hf * 128: d * HID + (hf + 1) * 128],
                                 f_ln2T[:, d * OWN + tc * 512: d * OWN + (tc + 1) * 512],
                                 start=(d == 0), stop=(d == 7))
                        for d in range(8)][-1])(ex, tc, hf, b),
                        waits=wmm, signal=L(tag))
                    bank_owner[b] = lm
                    lr = P.op("scalar", (lambda ex, hf, b: lambda e: e.activation(
                        out=f_hT[:, hf * 512:(hf + 1) * 512], in_=pb[b][:, :], func=AF.Relu,
                        bias=c_b1[:, ex * 16 + hf: ex * 16 + hf + 1]))(ex, hf, b),
                        waits=[lm, l_b1, mm2_done], signal=L(tag))
                    bank_owner[b] = lr
                    relu_done[hf] = lr
                mm1_done = relu_done[15]
                for m4 in range(4):
                    m = tc * 4 + m4
                    for oh in range(2):
                        b = 4 + (m4 * 2 + oh) % 4
                        wmm = [w2_ready, bank_owner.get(b)] + \
                              [relu_done[hf] for hf in range(16)]
                        if ex == 0:
                            wmm += [combT_ready[m], l_b2]
                        def mk_mm2(ex, m4, m, oh, b):
                            def f(e):
                                last = None
                                for hf in range(16):
                                    last = e.matmul(
                                        pb[b][:, :],
                                        f_hT[:, hf * 512 + m4 * 128: hf * 512 + (m4 + 1) * 128],
                                        f_w2[:, hf * D + oh * 512: hf * D + (oh + 1) * 512],
                                        start=(hf == 0), stop=(hf == 15 and ex != 0))
                                if ex == 0:
                                    last = e.matmul(
                                        pb[b][:, :], f_combT[0:8, m * 128:(m + 1) * 128],
                                        c_b2[0:8, oh * 512:(oh + 1) * 512],
                                        start=False, stop=True)
                                return last
                            return f
                        lm = P.op("tensor", mk_mm2(ex, m4, m, oh, b), waits=wmm,
                                  signal=L(tag))
                        bank_owner[b] = lm
                        le = P.op("vector", (lambda ex, m, oh, b: lambda e:
                            e.scalar_tensor_tensor(
                                out=r_res[m][:, oh * 512:(oh + 1) * 512],
                                in0=pb[b][:, :],
                                scalar=f_comb[:, m * 8 + ex: m * 8 + ex + 1],
                                in1=r_res[m][:, oh * 512:(oh + 1) * 512],
                                op0=ALU.mult, op1=ALU.add))(ex, m, oh, b),
                            waits=[lm, comb_ready[m], store_done[m]], signal=L(tag))
                        bank_owner[b] = le
                        res_labels[m] = le
                mm2_done = bank_owner[4 + 7 % 4]
            if ex == NEXP - 1:
                for m in range(16):
                    store_done[m] = P.op("sync", (lambda m: lambda e: e.dma_start(
                        out=out[m * 128:(m + 1) * 128, :], in_=r_res[m][:]))(m),
                        waits=[res_labels[m]], out_store=True)
        P.raw_wait("sync", store_done[15])
        P.emit_segment("ffn")

    return nc


def _host_constants():
    i = np.arange(0, DH, 2, dtype=np.float64) / DH
    theta = 1.0 / (10000.0 ** i)
    pos = np.arange(SEQ, dtype=np.float64)
    ang = pos[None, :] * theta[(np.arange(128) % 32)][:, None]
    cosfm = np.cos(ang).astype(np.float32)
    sinfm = np.sin(ang).astype(np.float32)
    identb = np.eye(128, dtype=np.float32)
    rot = np.zeros((128, 128), dtype=np.float32)
    for k in range(64):
        rot[2 * k + 1, 2 * k] = -1.0
        rot[2 * k, 2 * k + 1] = 1.0
    sel2 = np.zeros((2, 128), dtype=np.float32)
    sel2[0, 0:64] = 1.0
    sel2[1, 64:128] = 1.0
    bf = lambda a: np.ascontiguousarray(a.astype(ml_dtypes.bfloat16))
    return dict(costab=bf(cosfm), sintab=bf(sinfm), identb=bf(identb),
                rotmat=bf(rot), ones128b=bf(np.ones((128, 128), np.float32)),
                onescol_b=bf(np.ones((128, 1), np.float32)),
                onesrow_f=np.ones((1, 128), np.float32),
                sel2=bf(sel2), identf=identb.copy())


def make_in_maps(L0_attn_x, L0_attn_gamma, L0_attn_Wq, L0_attn_Wkv, L0_attn_W_bilinear,
                 L0_attn_Wo, L0_ffn_W1, L0_ffn_b1, L0_ffn_W2, L0_ffn_b2, L0_ffn_Wg,
                 L0_ffn_bg):
    C = np.ascontiguousarray
    x = np.asarray(L0_attn_x, dtype=np.float32)
    consts = _host_constants()
    A = np.asarray(L0_attn_W_bilinear, np.float32).transpose(0, 2, 1)
    wb_half = C(A.transpose(1, 0, 2).reshape(64, H * DH))
    wbt2 = np.concatenate([wb_half, wb_half], axis=0)
    shared = dict(
        gamma=C(np.asarray(L0_attn_gamma, np.float32)[None, :]),
        wq=C(np.asarray(L0_attn_Wq, np.float32)),
        wkv=C(np.asarray(L0_attn_Wkv, np.float32)),
        wbt2=C(wbt2),
        wo=C(np.asarray(L0_attn_Wo, np.float32).reshape(8, 128, D)
             .transpose(1, 0, 2).reshape(128, 8 * D)),
        w1t=C(np.asarray(L0_ffn_W1, np.float32).transpose(0, 2, 1)
              .reshape(NEXP, 8, 128, HID).transpose(0, 2, 1, 3)
              .reshape(NEXP, 128, 8 * HID)),
        w2t=C(np.asarray(L0_ffn_W2, np.float32).transpose(0, 2, 1)
              .reshape(NEXP, 16, 128, D).transpose(0, 2, 1, 3)
              .reshape(NEXP, 128, 16 * D)),
        b1c=C(np.asarray(L0_ffn_b1, np.float32).reshape(NEXP, HID // 128, 128)
              .transpose(2, 0, 1).reshape(128, NEXP * (HID // 128))),
        b2=C(np.asarray(L0_ffn_b2, np.float32)),
        wgt=C(np.asarray(L0_ffn_Wg, np.float32).T.reshape(8, 128, NEXP)
              .transpose(1, 0, 2).reshape(128, 8 * NEXP)),
        bg=C(np.asarray(L0_ffn_bg, np.float32)[None, :]),
        **consts,
    )
    in_maps = []
    for core in range(8):
        s, o = core // 2, core % 2
        xo = x[s, o * OWN:(o + 1) * OWN]
        xr = x[s, (1 - o) * OWN:(2 - o) * OWN]
        m = dict(shared)
        m["x_roll"] = C(np.concatenate([xo, xr], axis=0))
        own_sl = slice(o * OWN, (o + 1) * OWN)
        oth_sl = slice((1 - o) * OWN, (2 - o) * OWN)
        m["costab"] = C(np.concatenate([consts["costab"][:, own_sl],
                                        consts["costab"][:, oth_sl]], axis=1))
        m["sintab"] = C(np.concatenate([consts["sintab"][:, own_sl],
                                        consts["sintab"][:, oth_sl]], axis=1))
        in_maps.append(m)
    return in_maps


_NC_CACHE = {}


def kernel(**inputs):
    if "main" not in _NC_CACHE:
        _NC_CACHE["main"] = build_nc(debug=False)
    nc = _NC_CACHE["main"]
    in_maps = make_in_maps(**inputs)
    res = run_bass_kernel_spmd(nc, in_maps, core_ids=list(range(8)))
    outp = np.empty((4, SEQ, D), dtype=np.float32)
    for core in range(8):
        s, o = core // 2, core % 2
        outp[s, o * OWN:(o + 1) * OWN] = res.results[core]["out"]
    return outp


# version 05
# speedup vs baseline: 1.4400x; 1.4400x over previous
import numpy as np
import ml_dtypes
from contextlib import ExitStack

import concourse.bass as bass
import concourse.mybir as mybir
from concourse.bass_utils import run_bass_kernel_spmd

F32 = mybir.dt.float32
BF16 = mybir.dt.bfloat16
AF = mybir.ActivationFunctionType
ALU = mybir.AluOpType
AX = mybir.AxisListType

SEQ = 4096
OWN = 2048
CH = 512
NCH = OWN // CH
D = 1024
H = 16
DH = 64
NPAIR = 8
HID = 2048
NEXP = 8
EPS = 1e-5
SHIFT = 40.0

SBUF_BYTES = 207 * 1024
VTW = H * (DH + 1)


class Arena:

    def __init__(self, nc, st):
        self.words = SBUF_BYTES // 4
        self.t = st.enter_context(nc.sbuf_tensor("arena", [128, self.words], F32))
        self.off = 0
        self.marks = []

    def _alloc(self, bytes_pp):
        off = (self.off + 511) // 512 * 512
        assert off + bytes_pp <= SBUF_BYTES, f"arena overflow: {off}+{bytes_pp}"
        self.off = off + bytes_pp
        return off

    def f32(self, cols, rows=128):
        off = self._alloc(cols * 4)
        return self.t[0:rows, off // 4: off // 4 + cols]

    def bf16(self, cols, rows=128):
        off = self._alloc(cols * 2 + (cols * 2) % 4)
        nw = (cols * 2 + 3) // 4
        return self.t[0:rows, off // 4: off // 4 + nw].bitcast(BF16)[:, 0:cols]

    def mark(self):
        self.marks.append(self.off)

    def reset(self):
        self.off = self.marks.pop()


class Prog:
    QUEUES = ("sync", "gsdma", "tensor", "vector", "scalar")
    ENG = ("tensor", "vector", "scalar")

    def __init__(self, nc, st):
        self.nc = nc
        self.st = st
        self.sems = {q: st.enter_context(nc.semaphore("sem_" + q)) for q in self.ENG}
        self.sems["out"] = st.enter_context(nc.semaphore("sem_out"))
        self.ops = []
        self.labels = {}
        self.counts = {q: 0 for q in self.ENG}
        self.counts["out"] = 0
        self.watermarks = {}
        self.uid = 0

    def lab(self, tag="l"):
        self.uid += 1
        return f"{tag}{self.uid}"

    def _chan(self, name):
        if name not in self.sems:
            self.sems[name] = self.st.enter_context(self.nc.semaphore("sem_" + name))
            self.counts[name] = 0
        return name

    def op(self, queue, fn, waits=(), signal=None, out_store=False):
        if out_store:
            return self.dma("out", "sync", fn, waits=waits, signal=signal)
        assert queue in self.ENG, queue
        if signal is False:
            self.ops.append((queue, fn, tuple(w for w in waits if w), None, queue))
            return None
        if signal is None:
            signal = self.lab(queue)
        self.counts[queue] += 1
        assert signal not in self.labels, signal
        self.labels[signal] = (queue, self.counts[queue])
        self.ops.append((queue, fn, tuple(w for w in waits if w), signal, queue))
        return signal

    def dma(self, channel, queue, fn, waits=(), signal=None, serialize=False):
        assert queue in ("sync", "gsdma")
        ch = self._chan(channel)
        if signal is None:
            signal = self.lab(channel)
        self.counts[ch] += 16
        assert signal not in self.labels, signal
        self.labels[signal] = (ch, self.counts[ch])
        self.ops.append((queue, fn, tuple(w for w in waits if w), signal, ch))
        if serialize:
            self.raw_wait(queue, signal)
        return signal

    def chain(self, queue, fns, waits=()):
        lb = None
        for i, fn in enumerate(fns):
            w = list(waits) if i == 0 else [lb]
            lb = self.op(queue, fn, waits=w)
        return lb

    def raw_wait(self, queue, label):
        self.ops.append((queue, None, (label,), None, None))

    def emit_segment(self, name):
        per_q = {q: [] for q in self.QUEUES}
        for rec in self.ops:
            per_q[rec[0]].append(rec)
        self.ops = []
        active = [q for q in self.QUEUES if per_q[q]]

        with self.nc.Block(name) as blk:
            for q in active:
                def mk(qq):
                    def body(eng):
                        for (_, fn, waits, signal, semkey) in per_q[qq]:
                            for labw in waits:
                                pq, cnt = self.labels[labw]
                                key = (qq, pq)
                                if self.watermarks.get(key, 0) < cnt:
                                    eng.wait_ge(self.sems[pq], cnt)
                                    self.watermarks[key] = cnt
                            if fn is None:
                                continue
                            instr = fn(eng)
                            if signal is not None:
                                instr.then_inc(self.sems[semkey],
                                               16 if semkey not in self.ENG else 1)
                    return body
                getattr(blk, "gpsimd" if q == "gsdma" else q)(mk(q))
        for q in self.QUEUES:
            for pq in list(self.counts):
                self.watermarks[(q, pq)] = self.counts[pq]


def build_nc(debug=False):
    nc = bass.Bass("TRN2", target_bir_lowering=False)
    dt = nc.dram_tensor

    x_roll = dt("x_roll", [SEQ, D], F32, kind="ExternalInput").ap()
    gamma = dt("gamma", [1, D], F32, kind="ExternalInput").ap()
    wq = dt("wq", [D, D], F32, kind="ExternalInput").ap()
    wkv = dt("wkv", [D, 2 * D], F32, kind="ExternalInput").ap()
    wbt2 = dt("wbt2", [128, H * DH], F32, kind="ExternalInput").ap()
    wo = dt("wo", [128, 8 * D], F32, kind="ExternalInput").ap()
    w1t = dt("w1t", [NEXP, 128, 8 * HID], F32, kind="ExternalInput").ap()
    w2t = dt("w2t", [NEXP, 128, 16 * D], F32, kind="ExternalInput").ap()
    b1c = dt("b1c", [128, NEXP * (HID // 128)], F32, kind="ExternalInput").ap()
    b2 = dt("b2", [NEXP, D], F32, kind="ExternalInput").ap()
    wgt = dt("wgt", [128, 8 * NEXP], F32, kind="ExternalInput").ap()
    bg = dt("bg", [1, NEXP], F32, kind="ExternalInput").ap()
    costab = dt("costab", [128, SEQ], BF16, kind="ExternalInput").ap()
    sintab = dt("sintab", [128, SEQ], BF16, kind="ExternalInput").ap()
    identb = dt("identb", [128, 128], BF16, kind="ExternalInput").ap()
    rotmat = dt("rotmat", [128, 128], BF16, kind="ExternalInput").ap()
    ones128b = dt("ones128b", [128, 128], BF16, kind="ExternalInput").ap()
    onescol_b = dt("onescol_b", [128, 1], BF16, kind="ExternalInput").ap()
    onesrow_f = dt("onesrow_f", [1, 128], F32, kind="ExternalInput").ap()
    sel2 = dt("sel2", [2, 128], BF16, kind="ExternalInput").ap()
    identf = dt("identf", [128, 128], F32, kind="ExternalInput").ap()

    out = dt("out", [OWN, D], F32, kind="ExternalOutput").ap()
    dbg = {}
    if debug:
        for nm, shp, dtp in [
            ("kT", [128, 8 * OWN], BF16), ("qT", [128, 8 * OWN], BF16),
            ("v", [128, 16 * VTW], BF16), ("aoT", [128, 8 * OWN], BF16),
            ("res1", [OWN, D], F32), ("comb", [128, 128], F32),
            ("kTA", [128, 8 * OWN], BF16), ("vA", [128, 16 * VTW], BF16),
            ("xnT", [128, 8 * CH], BF16),
            ("sums", [128, 256], F32), ("ln2T", [128, 8 * OWN], BF16),
            ("bias", [1, 32], F32),
        ]:
            dbg[nm] = dt("dbg_" + nm, shp, dtp, kind="ExternalOutput").ap()

    with ExitStack() as st:
        P = Prog(nc, st)
        A = Arena(nc, st)
        pb = [st.enter_context(nc.psum_tensor(f"pb{i}", [128, 512], F32)) for i in range(8)]
        pbT = pb[7][:].bitcast(BF16)
        L = P.lab
        bank_owner = {}

        def dma(dst, src):
            return lambda e: e.dma_start(out=dst, in_=src)

        c_ident = A.bf16(128)
        c_rot = A.bf16(128)
        c_ones128b = A.bf16(128)
        c_onescol = A.bf16(1)
        c_onesrow = A.f32(128, rows=1)
        c_wbt = A.bf16(H * DH)
        c_b1 = A.f32(NEXP * (HID // 128))
        c_b2 = A.bf16(D, rows=NEXP)
        c_wg = A.bf16(8 * NEXP)
        c_bg = A.f32(NEXP, rows=1)
        c_sel2 = A.bf16(128, rows=2)
        c_identf = A.f32(128)
        s_bias = A.f32(32)
        s_bv = A.f32(32, rows=1)
        s_mx = A.f32(32, rows=1)

        A.mark()
        a_x = [A.f32(D) for _ in range(2)]
        a_ln1 = A.f32(D)
        a_xn = [A.bf16(D) for _ in range(2)]
        a_st = A.f32(16)
        a_mv = A.f32(4)
        a_rs = A.f32(8)
        a_xnT = A.bf16(8 * CH)
        a_wstg = [A.bf16(4 * D) for _ in range(2)]
        a_scr = A.f32(2048)
        a_kraw = [a_scr[:, 0:256].bitcast(BF16), a_scr[:, 256:512].bitcast(BF16)]
        a_rt1 = [a_scr[:, 512:768].bitcast(BF16), a_scr[:, 768:1024].bitcast(BF16)]
        a_rt2 = [a_scr[:, 1024:1280].bitcast(BF16), a_scr[:, 1280:1536].bitcast(BF16)]
        a_qhat = a_scr[:, 1536:1792].bitcast(BF16)
        a_sq = a_scr[:, 1792:2048].bitcast(BF16)
        a_gbc = A.bf16(D)
        a_kT = A.bf16(8 * OWN)
        a_v = A.bf16(16 * VTW)
        s_r64 = a_scr[:, 0:1024]
        a_qT = A.bf16(8 * OWN)
        a_aoT = A.bf16(8 * OWN)
        a_pt = [A.bf16(512) for _ in range(4)]
        a_cos = A.bf16(OWN)
        a_sin = A.bf16(OWN)
        s_sumsT = A.f32(256)
        s_sumb16 = A.bf16(512, rows=2)
        assert A.off <= SBUF_BYTES, A.off

        l_ident = P.dma("cs", "sync", dma(c_ident, identb[:]), serialize=True)
        l_rot = P.dma("cs", "sync", dma(c_rot, rotmat[:]), serialize=True)
        l_o128 = P.dma("cs", "sync", dma(c_ones128b, ones128b[:]), serialize=True)
        l_ocol = P.dma("cs", "sync", dma(c_onescol, onescol_b[:]), serialize=True)
        l_orow = P.dma("cs", "sync", dma(c_onesrow, onesrow_f[:]), serialize=True)
        l_bg = P.dma("cs", "sync", dma(c_bg, bg[:]), serialize=True)
        l_b1 = P.dma("cs", "sync", dma(c_b1, b1c[:]), serialize=True)
        l_gam = P.dma("cs", "sync", dma(a_x[0][0:1, 0:D], gamma[:]), serialize=True)
        l_wbt = P.dma("cg", "gsdma", dma(c_wbt, wbt2[:]), serialize=True)
        l_b2 = P.dma("cg", "gsdma", dma(c_b2, b2[:]), serialize=True)
        l_wg = P.dma("cg", "gsdma", dma(c_wg, wgt[:]), serialize=True)
        l_sel2 = P.dma("cg", "gsdma", dma(c_sel2, sel2[:]), serialize=True)
        l_idf = P.dma("cg", "gsdma", dma(c_identf, identf[:]), serialize=True)
        P.op("scalar", lambda e: e.activation(out=c_wbt, in_=c_wbt, func=AF.Copy, scale=0.125),
             waits=[l_wbt], signal="wbt_ok")
        for i in range(2):
            P.op("tensor", (lambda i: lambda e: e.matmul(
                pb[i][:, :], c_onesrow, a_x[0][0:1, i * 512:(i + 1) * 512],
                start=True, stop=True))(i),
                waits=[l_orow, l_gam], signal=f"gbcmm{i}")
            P.op("vector", (lambda i: lambda e: e.tensor_copy(
                a_gbc[:, i * 512:(i + 1) * 512], pb[i][:, :]))(i),
                waits=[f"gbcmm{i}"], signal=f"gbc{i}")
        P.op("vector", lambda e: e.memset(s_mx[0:1, :], -1e30), signal="mxinit")
        for tt in range(16):
            P.op("vector", (lambda tt: lambda e: e.memset(
                a_v[:, tt * VTW:(tt + 1) * VTW].rearrange(
                    "p (h u) -> p h u", u=DH + 1)[:, :, DH:DH + 1], 1.0))(tt),
                signal=f"vone{tt}")
        P.op("scalar", lambda e: e.preload_act_table(AF.Sqrt), signal=False)
        P.emit_segment("s0")

        def ln_block(src, dst, deps, tag, parity, gamma_mul):
            rsb = parity * 4
            l1 = P.chain("vector", [
                lambda e: (e.bn_stats(a_st[:, 0:6], src[:, 0:512]),
                           e.bn_stats(a_st[:, 6:12], src[:, 512:1024]))[-1],
                lambda e: e.bn_aggr(a_mv[:, 0:2], a_st[:, 0:12]),
                lambda e: e.tensor_scalar(out=a_rs[:, rsb:rsb + 1], in0=a_mv[:, 1:2],
                                          scalar1=EPS, scalar2=None, op0=ALU.add),
                lambda e: e.reciprocal(a_rs[:, rsb:rsb + 1], a_rs[:, rsb:rsb + 1]),
            ], waits=deps)
            l2 = P.op("scalar", lambda e: e.activation(
                out=a_rs[:, rsb + 1:rsb + 2], in_=a_rs[:, rsb:rsb + 1], func=AF.Sqrt),
                waits=[l1])
            l3 = P.chain("vector", [
                lambda e: e.tensor_scalar(out=a_rs[:, rsb + 2:rsb + 3], in0=a_mv[:, 0:1],
                                          scalar1=a_rs[:, rsb + 1:rsb + 2], scalar2=-1.0,
                                          op0=ALU.mult, op1=ALU.mult),
                lambda e: e.tensor_scalar(out=a_ln1[:], in0=src,
                                          scalar1=a_rs[:, rsb + 1:rsb + 2],
                                          scalar2=a_rs[:, rsb + 2:rsb + 3],
                                          op0=ALU.mult, op1=ALU.add),
                lambda e: (e.bn_stats(a_st[:, 0:6], a_ln1[:, 0:512]),
                           e.bn_stats(a_st[:, 6:12], a_ln1[:, 512:1024]))[-1],
                lambda e: e.bn_aggr(a_mv[:, 2:4], a_st[:, 0:12]),
                lambda e: e.tensor_scalar(out=a_rs[:, rsb:rsb + 1], in0=a_mv[:, 3:4],
                                          scalar1=EPS, scalar2=None, op0=ALU.add),
                lambda e: e.reciprocal(a_rs[:, rsb:rsb + 1], a_rs[:, rsb:rsb + 1]),
            ], waits=[l2])
            l4 = P.op("scalar", lambda e: e.activation(
                out=a_rs[:, rsb + 1:rsb + 2], in_=a_rs[:, rsb:rsb + 1], func=AF.Sqrt),
                waits=[l3])
            l5 = P.chain("vector", [
                lambda e: e.tensor_scalar(out=a_rs[:, rsb + 2:rsb + 3], in0=a_mv[:, 2:3],
                                          scalar1=a_rs[:, rsb + 1:rsb + 2], scalar2=-1.0,
                                          op0=ALU.mult, op1=ALU.mult),
                lambda e: e.tensor_scalar(out=dst, in0=a_ln1[:],
                                          scalar1=a_rs[:, rsb + 1:rsb + 2],
                                          scalar2=a_rs[:, rsb + 2:rsb + 3],
                                          op0=ALU.mult, op1=ALU.add),
                lambda e: e.tensor_tensor(out=dst, in0=dst, in1=a_gbc, op=ALU.mult),
            ], waits=[l4, "gbc0", "gbc1"] + (gamma_mul or []))
            return l5

        kraw_free = [None, None]

        def rope(bank, toksl, dst_ap, buf, tag, l_cos, l_sin, dst_free=None):
            kraw, rt1, rt2 = a_kraw[buf], a_rt1[buf], a_rt2[buf]
            wev = [bank_owner.get(bank)] + (kraw_free[buf] or [])
            l_ev = P.op("scalar", lambda e: e.activation(out=kraw, in_=pb[bank][:, :],
                                                         func=AF.Copy),
                        waits=wev, signal=L(tag))
            l_ro = P.op("tensor", lambda e: e.matmul(pb[bank][:, :], c_rot, kraw,
                                                     start=True, stop=True),
                        waits=[l_ev, l_rot], signal=L(tag))
            l_t1 = P.op("vector", lambda e: e.tensor_tensor(
                out=rt1, in0=kraw, in1=a_cos[:, toksl], op=ALU.mult),
                waits=[l_ev, l_cos], signal=L(tag))
            l_t2 = P.op("vector", lambda e: e.tensor_tensor(
                out=rt2, in0=pb[bank][:, :], in1=a_sin[:, toksl], op=ALU.mult),
                waits=[l_ro, l_t1, l_sin], signal=L(tag))
            l_dst = P.op("vector", lambda e: e.tensor_tensor(
                out=dst_ap, in0=rt1, in1=rt2, op=ALU.add),
                waits=[l_t2] + ([dst_free] if dst_free else []), signal=L(tag))
            kraw_free[buf] = [l_ro, l_t1]
            bank_owner[bank] = l_t2
            return l_dst

        def warm_burst(n=24):
            def f(e):
                r = None
                for _ in range(n):
                    r = e.matmul(pb[7][0:1, :], c_ones128b[0:1, 0:1],
                                 c_b2[0:1, 0:512], start=True, stop=True)
                return r
            P.op("tensor", f, signal=False)

        prev_sq = [None]
        mx_last = [None]

        def normsq(src_ap, bank, mslot, deps, tag):
            l_sq = P.op("vector", lambda e: e.tensor_tensor(
                out=a_sq, in0=src_ap, in1=src_ap, op=ALU.mult),
                waits=list(deps) + ([prev_sq[0]] if prev_sq[0] else []), signal=L(tag))
            l_mm = P.op("tensor", lambda e: e.matmul(pb[bank][0:1, :], c_onescol, a_sq,
                                                     start=True, stop=True),
                        waits=[l_sq, l_ocol, bank_owner.get(bank)], signal=L(tag))
            prev_sq[0] = l_mm
            l_mx = P.chain("vector", [
                lambda e: e.tensor_reduce(s_mx[0:1, 31:32], pb[bank][0:1, :],
                                          axis=AX.X, op=ALU.max),
                lambda e: e.tensor_tensor(out=s_mx[0:1, mslot:mslot + 1],
                                          in0=s_mx[0:1, mslot:mslot + 1],
                                          in1=s_mx[0:1, 31:32], op=ALU.max),
            ], waits=[l_mm, "mxinit"])
            bank_owner[bank] = l_mx
            mx_last[0] = l_mx
            return l_mx

        def proj_pass(pass_id):
            tag = f"P{pass_id}"
            base = pass_id * OWN
            if pass_id == 1:
                P.op("scalar", lambda e: e.preload_act_table(AF.Sqrt), signal=False)
            l_cos = P.dma("tcos", "sync", dma(a_cos, costab[:, base:base + OWN]))
            l_sin = P.dma("tsin", "sync", dma(a_sin, sintab[:, base:base + OWN]))
            warm_burst()
            wstg_free = [None, None]
            x_free = [None, None]
            xn_free = [None, None]
            xnT_free = None
            qhat_free = None
            for tc in range(NCH):
                xn_labels = []
                tr_labels = []
                for t4 in range(4):
                    tile = tc * 4 + t4
                    xb = a_x[t4 % 2]
                    l_x = P.dma(f"x{t4 % 2}", "sync",
                                dma(xb[:], x_roll[base + tile * 128: base + tile * 128 + 128, :]),
                                waits=[x_free[t4 % 2]])
                    xn = a_xn[t4 % 2]
                    l_xn = ln_block(xb[:], xn[:], [l_x], tag, t4 % 2,
                                    gamma_mul=[xn_free[t4 % 2]] if xn_free[t4 % 2] else [])
                    x_free[t4 % 2] = l_xn
                    wtr = [l_xn, l_ident, bank_owner.get(7)]
                    l_tr = P.op("tensor", (lambda xn: lambda e: [
                        e.transpose(pbT[:, d * 128:(d + 1) * 128],
                                    xn[:, d * 128:(d + 1) * 128], c_ident)
                        for d in range(8)][-1])(xn),
                        waits=wtr, signal=L(tag))
                    bank_owner[7] = l_tr
                    xn_free[t4 % 2] = l_tr
                    tr_labels.append(l_tr)
                    l_ev = P.op("vector", (lambda t4: lambda e: e.tensor_copy(
                        a_xnT.rearrange("p (d c) -> p d c", d=8)[:, :, t4 * 128:(t4 + 1) * 128],
                        pbT[:].rearrange("p (d c) -> p d c", d=8)))(t4),
                        waits=[l_tr] + ([xnT_free] if (xnT_free and t4 == 0) else []),
                        signal=L(tag))
                    bank_owner[7] = l_ev
                    xn_labels.append(l_ev)

                def wload(which, half):
                    sbuf = a_wstg[half]
                    r0 = half * 512
                    if which == "k":
                        src = wkv[r0:r0 + 512, 0:D]
                    elif which == "v":
                        src = wkv[r0:r0 + 512, D:2 * D]
                    else:
                        src = wq[r0:r0 + 512, 0:D]
                    src = src.rearrange("(c p) f -> p c f", p=128)
                    dst = sbuf.rearrange("p (c f) -> p c f", f=D)
                    l_w = P.dma(f"ws{half}", "gsdma", dma(dst, src),
                                waits=[wstg_free[half]])
                    return sbuf, l_w

                def proj_mm(which, d, dd, sbuf, l_w, first_waits):
                    if which == "k" or which == "q":
                        def f(e, sbuf=sbuf, d=d, dd=dd):
                            r = None
                            for kf in range(8):
                                r = e.matmul(pb[kf][:, :],
                                             sbuf[:, dd * D + kf * 128: dd * D + (kf + 1) * 128],
                                             a_xnT[:, d * CH:(d + 1) * CH],
                                             start=(d == 0), stop=(d == 7))
                            return r
                    else:
                        def f(e, sbuf=sbuf, d=d, dd=dd):
                            r = None
                            for tt in range(4):
                                for vh in range(2):
                                    r = e.matmul(pb[tt * 2 + vh][:, :],
                                                 a_xnT[:, d * CH + tt * 128: d * CH + (tt + 1) * 128],
                                                 sbuf[:, dd * D + vh * 512: dd * D + (vh + 1) * 512],
                                                 start=(d == 0), stop=(d == 7))
                            return r
                    return P.op("tensor", f, waits=[l_w] + first_waits, signal=L(tag))

                def proj_sweep(which):
                    last_mm = None
                    for half in range(2):
                        sbuf, l_w = wload(which, half)
                        for dd in range(4):
                            d = half * 4 + dd
                            fw = []
                            if d == 0:
                                fw = xn_labels + [bank_owner.get(b) for b in range(8)]
                            if last_mm:
                                fw = fw + [last_mm]
                            last_mm = proj_mm(which, d, dd, sbuf, l_w, fw)
                        wstg_free[half] = last_mm
                    for b in range(8):
                        bank_owner[b] = last_mm
                    return last_mm

                toksl = slice(tc * CH, (tc + 1) * CH)

                last_mm = proj_sweep("k")
                for kf in range(8):
                    dst = a_kT[:, kf * OWN + tc * CH: kf * OWN + (tc + 1) * CH]
                    l_dst = rope(kf, toksl, dst, kf % 2, tag, l_cos, l_sin)
                    normsq(dst, kf, 8 + 8 * pass_id + kf, [l_dst], tag)

                last_mm = proj_sweep("v")
                for tt in range(4):
                    for vh in range(2):
                        b = tt * 2 + vh
                        vtile = tc * 4 + tt
                        l_ev = P.op("vector", (lambda b, vtile, vh: lambda e: e.tensor_copy(
                            a_v[:, vtile * VTW + vh * 8 * (DH + 1):].rearrange(
                                "p (h u) -> p h u", u=DH + 1)[:, 0:8, 0:DH],
                            pb[b][:].rearrange("p (h d) -> p h d", d=DH)))(b, vtile, vh),
                            waits=[bank_owner[b]], signal=L(tag))
                        bank_owner[b] = l_ev

                if pass_id == 0:
                    last_mm = proj_sweep("q")
                    for qf in range(8):
                        l_dst = rope(qf, toksl, a_qhat, qf % 2, tag, l_cos, l_sin,
                                     dst_free=qhat_free)
                        h0, h1 = 2 * qf, 2 * qf + 1
                        l_bl = P.op("tensor", (lambda qf, h0, h1: lambda e: (
                            e.matmul(pb[qf][0:64, :], c_wbt[0:64, h0 * 64:(h0 + 1) * 64],
                                     a_qhat[0:64, :], start=True, stop=True),
                            e.matmul(pb[qf][64:128, :], c_wbt[64:128, h1 * 64:(h1 + 1) * 64],
                                     a_qhat[64:128, :], start=True, stop=True))[-1])(qf, h0, h1),
                            waits=[l_dst, "wbt_ok", bank_owner.get(qf)], signal=L(tag))
                        qhat_free = l_bl
                        bank_owner[qf] = l_bl
                        dstq = a_qT[:, qf * OWN + tc * CH: qf * OWN + (tc + 1) * CH]
                        l_ev = P.op("vector", (lambda dstq, qf: lambda e: e.tensor_copy(
                            dstq, pb[qf][:, :]))(dstq, qf),
                            waits=[l_bl], signal=L(tag))
                        bank_owner[qf] = l_ev
                        normsq(dstq, qf, qf, [l_ev], tag)
                xnT_free = last_mm

        def bias_build(pass_id):
            tag = f"B{pass_id}"
            if pass_id == 0:
                l1 = P.op("vector", lambda e: e.tensor_tensor(
                    out=s_bv[0:1, 0:8], in0=s_mx[0:1, 0:8], in1=s_mx[0:1, 8:16], op=ALU.mult),
                    waits=[mx_last[0]], signal=L(tag))
                l2 = P.op("scalar", lambda e: e.activation(
                    out=s_bv[0:1, 0:8], in_=s_bv[0:1, 0:8], func=AF.Sqrt),
                    waits=[l1], signal=L(tag))
                l3 = P.op("vector", lambda e: e.tensor_scalar(
                    out=s_bv[0:1, 8:16], in0=s_bv[0:1, 0:8], scalar1=-1.0, scalar2=SHIFT,
                    op0=ALU.mult, op1=ALU.add), waits=[l2], signal=L(tag))
                lm = P.op("tensor", lambda e: e.matmul(
                    pb[0][:, 0:8], c_onesrow, s_bv[0:1, 8:16], start=True, stop=True),
                    waits=[l3, bank_owner.get(0)], signal=L(tag))
                bank_owner[0] = lm
                le = P.op("vector", lambda e: e.tensor_copy(s_bias[:, 0:8], pb[0][:, 0:8]),
                          waits=[lm], signal=L(tag))
                bank_owner[0] = le
                return le
            l1 = P.op("vector", lambda e: e.tensor_tensor(
                out=s_bv[0:1, 16:24], in0=s_mx[0:1, 0:8], in1=s_mx[0:1, 16:24], op=ALU.mult),
                waits=[mx_last[0]], signal=L(tag))
            l2 = P.op("scalar", lambda e: e.activation(
                out=s_bv[0:1, 16:24], in_=s_bv[0:1, 16:24], func=AF.Sqrt),
                waits=[l1], signal=L(tag))
            l3 = P.chain("vector", [
                lambda e: e.tensor_tensor(out=s_bv[0:1, 16:24], in0=s_bv[0:1, 16:24],
                                          in1=s_bv[0:1, 0:8], op=ALU.max),
                lambda e: e.tensor_scalar(out=s_bv[0:1, 24:32], in0=s_bv[0:1, 16:24],
                                          scalar1=-1.0, scalar2=SHIFT,
                                          op0=ALU.mult, op1=ALU.add),
                lambda e: e.tensor_tensor(out=s_bv[0:1, 16:24], in0=s_bv[0:1, 24:32],
                                          in1=s_bv[0:1, 8:16], op=ALU.subtract),
            ], waits=[l2])
            P.op("scalar", lambda e: e.preload_act_table(AF.Exp), waits=[l3],
                 signal=False)
            l4 = P.op("scalar", lambda e: e.activation(
                out=s_bv[0:1, 16:24], in_=s_bv[0:1, 16:24], func=AF.Exp),
                waits=[l3], signal=L(tag))
            lm = P.op("tensor", lambda e: (
                e.matmul(pb[0][:, 0:8], c_onesrow, s_bv[0:1, 24:32], start=True, stop=True),
                e.matmul(pb[1][:, 0:8], c_onesrow, s_bv[0:1, 16:24], start=True, stop=True))[-1],
                waits=[l4, bank_owner.get(0), bank_owner.get(1)], signal=L(tag))
            le = P.op("vector", lambda e: (
                e.tensor_copy(s_bias[:, 8:16], pb[0][:, 0:8]),
                e.tensor_copy(s_bias[:, 16:24], pb[1][:, 0:8]))[-1],
                waits=[lm], signal=L(tag))
            bank_owner[0] = le
            bank_owner[1] = le
            return le

        sums_last = [None]

        def sim_pass(pass_id, l_bias):
            tag = f"S{pass_id}"
            P.op("scalar", lambda e: e.preload_act_table(AF.Exp), waits=[l_bias],
                 signal=False)
            biascol = 0 if pass_id == 0 else 8
            warm_burst()
            for pr in range(NPAIR):
                h0, h1 = 2 * pr, 2 * pr + 1
                for qc in range(4):
                    qsl = slice(pr * OWN + qc * 512, pr * OWN + (qc + 1) * 512)
                    av_done = {}
                    exp_done = {}
                    def emit_sims(js):
                        for hp in (0, 1):
                            sb = hp * 2 + js % 2
                            ksl = slice(pr * OWN + js * 128, pr * OWN + (js + 1) * 128)
                            l_sim = P.op("tensor", (lambda sb, hp, ksl, qsl: lambda e: e.matmul(
                                pb[sb][:, :],
                                a_kT[hp * 64:(hp + 1) * 64, ksl],
                                a_qT[hp * 64:(hp + 1) * 64, qsl],
                                start=True, stop=True))(sb, hp, ksl, qsl),
                                waits=[bank_owner.get(sb)], signal=L(tag))
                            bank_owner[sb] = l_sim
                    def emit_exps(js):
                        for hp in (0, 1):
                            sb = hp * 2 + js % 2
                            pt = a_pt[hp * 2 + js % 2]
                            l_exp = P.op("scalar", (lambda sb, pt, pr: lambda e: e.activation(
                                out=pt, in_=pb[sb][:, :], func=AF.Exp,
                                bias=s_bias[:, biascol + pr: biascol + pr + 1]))(sb, pt, pr),
                                waits=[bank_owner[sb], av_done.get((hp, js - 2))],
                                signal=L(tag))
                            bank_owner[sb] = l_exp
                            exp_done[(hp, js)] = l_exp
                    def emit_avs(js):
                        for hp, h in ((0, h0), (1, h1)):
                            ob = 4 + hp
                            pt = a_pt[hp * 2 + js % 2]
                            wav = [exp_done[(hp, js)]]
                            if js == 0:
                                wav += [bank_owner.get(ob)]
                            l_av = P.op("tensor", (lambda ob, h, pt, js: lambda e: (
                                e.matmul(pb[ob][0:65, :],
                                         a_v[:, js * VTW + h * (DH + 1):
                                             js * VTW + (h + 1) * (DH + 1)],
                                         pt, start=(js == 0), stop=(js == 15)),
                                e.matmul(pb[7][0:1, 0:64], c_ones128b[0:1, 0:1],
                                         c_ones128b[0:1, 0:64], start=True,
                                         stop=True))[0])(ob, h, pt, js),
                                waits=wav, signal=L(tag))
                            av_done[(hp, js)] = l_av
                            if js == 15:
                                bank_owner[ob] = l_av
                    for js in range(16):
                        emit_sims(js)
                        emit_exps(js)
                        if js >= 1:
                            emit_avs(js - 1)
                    emit_avs(15)
                    l_e16 = P.op("vector", lambda e: (
                        e.tensor_copy(s_r64[64:65, 0:512], pb[4][64:65, :]),
                        e.tensor_copy(s_r64[64:65, 512:1024], pb[5][64:65, :]))[-1],
                        waits=[bank_owner[4], bank_owner[5], bank_owner.get(6)],
                        signal=L(tag))
                    bank_owner[4] = l_e16
                    bank_owner[5] = l_e16
                    l_tr = P.op("tensor", (lambda pr: lambda e: [
                        e.transpose(pb[6][:, qs * 16 + 2 * pr + hh: qs * 16 + 2 * pr + hh + 1],
                                    s_r64[64:65, hh * 512 + qs * 128: hh * 512 + (qs + 1) * 128],
                                    c_identf[64:65, 64:65])
                        for hh in range(2) for qs in range(4)][-1])(pr),
                        waits=[l_e16, l_idf, bank_owner[6]], signal=L(tag))
                    bank_owner[6] = l_tr
                    for hp, h in ((0, h0), (1, h1)):
                        ob = 4 + hp
                        asl = slice(pr * OWN + qc * 512, pr * OWN + (qc + 1) * 512)
                        dsta = a_aoT[hp * 64:hp * 64 + 64, asl]
                        if pass_id == 0:
                            l1 = P.op("vector", (lambda dsta, ob: lambda e: e.tensor_copy(
                                dsta, pb[ob][0:64, :]))(dsta, ob),
                                waits=[bank_owner[ob]], signal=L(tag))
                        else:
                            l1 = P.op("vector", (lambda dsta, ob, hp, pr: lambda e:
                                e.scalar_tensor_tensor(
                                    out=dsta, in0=dsta,
                                    scalar=s_bias[hp * 64:hp * 64 + 64, 16 + pr:17 + pr],
                                    in1=pb[ob][0:64, :], op0=ALU.mult, op1=ALU.add))(
                                        dsta, ob, hp, pr),
                                waits=[bank_owner[ob]], signal=L(tag))
                        bank_owner[ob] = l1
                    sTdst = s_sumsT.rearrange("p (s h) -> p s h", h=16)[
                        :, qc * 4:(qc + 1) * 4, 2 * pr:2 * pr + 2]
                    sTsrc = pb[6][:, 0:64].rearrange("p (s h) -> p s h", h=16)[
                        :, :, 2 * pr:2 * pr + 2]
                    if pass_id == 0:
                        l_sc = P.op("vector", (lambda sTdst, sTsrc: lambda e: e.tensor_copy(
                            sTdst, sTsrc))(sTdst, sTsrc),
                            waits=[l_tr], signal=L(tag))
                    else:
                        l_sc = P.op("vector", (lambda sTdst, sTsrc, pr: lambda e:
                            e.scalar_tensor_tensor(
                                out=sTdst, in0=sTdst,
                                scalar=s_bias[:, 16 + pr:17 + pr],
                                in1=sTsrc, op0=ALU.mult, op1=ALU.add))(sTdst, sTsrc, pr),
                            waits=[l_tr], signal=L(tag))
                    bank_owner[6] = l_sc
                    sums_last[0] = l_sc

        def attn_normalize():
            tag = "N"
            l_r = P.op("vector", lambda e: e.reciprocal(s_sumsT, s_sumsT),
                       waits=[sums_last[0]])
            for pr in range(NPAIR):
                for qc in range(4):
                    b = (pr * 4 + qc) % 4
                    asl = slice(pr * OWN + qc * 512, pr * OWN + (qc + 1) * 512)
                    src_r = s_sumsT.rearrange("p (s h) -> p s h", h=16)[
                        :, qc * 4:(qc + 1) * 4, 2 * pr:2 * pr + 2]
                    l_t = P.op("tensor", (lambda b, pr, qc: lambda e: [
                        e.transpose(pb[b][0:2, qs * 128:(qs + 1) * 128],
                                    s_sumsT[:, (qc * 4 + qs) * 16 + 2 * pr:
                                            (qc * 4 + qs) * 16 + 2 * pr + 2],
                                    c_identf)
                        for qs in range(4)][-1])(b, pr, qc),
                        waits=[l_r, l_idf, bank_owner.get(b)], signal=L(tag))
                    bank_owner[b] = l_t
                    l_c = P.op("vector", (lambda b: lambda e: e.tensor_copy(
                        s_sumb16[0:2, :], pb[b][0:2, :]))(b),
                        waits=[l_t], signal=L(tag))
                    bank_owner[b] = l_c
                    lm = P.op("tensor", (lambda b: lambda e: e.matmul(
                        pb[b][:, :], c_sel2[0:2, :], s_sumb16[0:2, :],
                        start=True, stop=True))(b),
                        waits=[l_c, l_sel2], signal=L(tag))
                    bank_owner[b] = lm
                    lt = P.op("vector", (lambda b, asl: lambda e: e.tensor_tensor(
                        out=a_aoT[:, asl], in0=a_aoT[:, asl], in1=pb[b][:, :],
                        op=ALU.mult))(b, asl),
                        waits=[lm], signal=L(tag))
                    bank_owner[b] = lt
            return lt

        import os as _os
        if not _os.environ.get("SKIP_A"):
            proj_pass(0)
            l_b = bias_build(0)
            if not _os.environ.get("SKIP_SIMA"):
                sim_pass(0, l_b)
        else:
            l_b = bias_build(0)
        if debug:
            P.op("sync", dma(dbg["kTA"][:], a_kT), waits=[l_b, sums_last[0]], out_store=True)
            ldbgA = P.op("sync", dma(dbg["vA"][:], a_v), waits=[l_b, sums_last[0]],
                         out_store=True)
            P.raw_wait("sync", ldbgA)
        P.emit_segment("attnA")
        import os as _os
        proj_pass(1)
        l_b = bias_build(1)
        if not _os.environ.get("SKIP_SIMB"):
            sim_pass(1, l_b)
            l_fin = attn_normalize()
        else:
            l_fin = l_b
        if debug:
            P.op("sync", dma(dbg["kT"][:], a_kT), waits=[l_fin], out_store=True)
            P.op("sync", dma(dbg["qT"][:], a_qT), waits=[l_fin], out_store=True)
            P.op("sync", dma(dbg["v"][:], a_v), waits=[l_fin], out_store=True)
            P.op("sync", dma(dbg["aoT"][:], a_aoT), waits=[l_fin], out_store=True)
            P.op("sync", dma(dbg["sums"][:], s_sumsT), waits=[l_fin], out_store=True)
            P.op("sync", dma(dbg["xnT"][:], a_xnT), waits=[l_fin], out_store=True)
            ldbg = P.op("sync", dma(dbg["bias"][:], s_bv[0:1, 0:32]), waits=[l_fin],
                        out_store=True)
            P.raw_wait("sync", ldbg)
        P.emit_segment("attnB")

        A.reset()
        A.mark()
        r_res = [A.f32(D) for _ in range(16)]
        f_w1 = A.bf16(8 * HID)
        r_wo = f_w1.bitcast(BF16)[:, 0:8 * D] if False else None
        r_wo = f_w1[:, 0:8 * D]
        f_ln2T = A.bf16(8 * OWN)
        f_w2 = A.bf16(16 * D)
        f_hT = A.bf16(16 * 512)
        f_comb = A.f32(128)
        f_combb = A.bf16(8)
        f_combT = A.bf16(OWN, rows=8)
        f_xn = A.bf16(D)
        f_g = A.f32(40)
        f_st = A.f32(16)
        f_mv = A.f32(4)
        f_rs = A.f32(8)
        assert A.off <= SBUF_BYTES, A.off

        bank_owner.clear()
        tag = "O"
        l_wo = P.dma("w1", "gsdma", dma(r_wo, wo[:]))
        res_labels = [None] * 16
        res_prev = [None] * 4
        warm_burst()
        for m in range(16):
            l_x = P.dma(f"rs{m % 4}", "sync",
                        dma(r_res[m][:], x_roll[m * 128:(m + 1) * 128, :]),
                        waits=[res_prev[m % 4]])
            for oh in range(2):
                b = (m * 2 + oh) % 4
                lm = P.op("tensor", (lambda m, oh, b: lambda e: (
                    [e.matmul(pb[b][:, :],
                              a_aoT[:, c * OWN + m * 128: c * OWN + (m + 1) * 128],
                              r_wo[:, c * D + oh * 512: c * D + (oh + 1) * 512],
                              start=(c == 0), stop=(c == 7))
                     for c in range(8)],
                    e.matmul(pb[7][0:1, 0:64], c_ones128b[0:1, 0:1],
                             c_ones128b[0:1, 0:64], start=True, stop=True))[0][-1])(m, oh, b),
                    waits=[l_wo, bank_owner.get(b)], signal=L(tag))
                bank_owner[b] = lm
                le = P.op("vector", (lambda m, oh, b: lambda e: e.tensor_tensor(
                    out=r_res[m][:, oh * 512:(oh + 1) * 512],
                    in0=r_res[m][:, oh * 512:(oh + 1) * 512],
                    in1=pb[b][:, :], op=ALU.add))(m, oh, b),
                    waits=[lm, l_x], signal=L(tag))
                bank_owner[b] = le
                res_labels[m] = le
            res_prev[m % 4] = res_labels[m]
        if debug:
            ldbg = None
            for m in range(16):
                ldbg = P.op("sync", dma(dbg["res1"][m * 128:(m + 1) * 128, :], r_res[m][:]),
                            waits=[res_labels[m]], out_store=True)
            P.raw_wait("sync", ldbg)
        P.emit_segment("oproj")

        tag = "F"
        ln_done = [None] * 16
        P.op("scalar", lambda e: e.preload_act_table(AF.Sqrt), signal=False)
        fxn_free = None
        for m in range(16):
            src = r_res[m][:]
            rsb = (m % 2) * 4
            l1 = P.chain("vector", [
                (lambda src: lambda e: (e.bn_stats(f_st[:, 0:6], src[:, 0:512]),
                                        e.bn_stats(f_st[:, 6:12], src[:, 512:1024]))[-1])(src),
                (lambda rsb: lambda e: e.bn_aggr(f_mv[:, 0:2], f_st[:, 0:12]))(rsb),
                (lambda rsb: lambda e: e.tensor_scalar(
                    out=f_rs[:, rsb:rsb + 1], in0=f_mv[:, 1:2], scalar1=EPS,
                    scalar2=None, op0=ALU.add))(rsb),
                (lambda rsb: lambda e: e.reciprocal(
                    f_rs[:, rsb:rsb + 1], f_rs[:, rsb:rsb + 1]))(rsb),
            ], waits=[res_labels[m]])
            l2 = P.op("scalar", (lambda rsb: lambda e: e.activation(
                out=f_rs[:, rsb + 1:rsb + 2], in_=f_rs[:, rsb:rsb + 1], func=AF.Sqrt))(rsb),
                waits=[l1], signal=L(tag))
            l3 = P.chain("vector", [
                (lambda rsb: lambda e: e.tensor_scalar(
                    out=f_rs[:, rsb + 2:rsb + 3], in0=f_mv[:, 0:1],
                    scalar1=f_rs[:, rsb + 1:rsb + 2], scalar2=-1.0,
                    op0=ALU.mult, op1=ALU.mult))(rsb),
                (lambda src, rsb: lambda e: e.tensor_scalar(
                    out=f_xn, in0=src, scalar1=f_rs[:, rsb + 1:rsb + 2],
                    scalar2=f_rs[:, rsb + 2:rsb + 3], op0=ALU.mult, op1=ALU.add))(src, rsb),
            ], waits=[l2] + ([fxn_free] if fxn_free else []))
            l_tr = P.op("tensor", lambda e: [
                e.transpose(pbT[:, d * 128:(d + 1) * 128],
                            f_xn[:, d * 128:(d + 1) * 128], c_ident)
                for d in range(8)][-1],
                waits=[l3, l_ident, bank_owner.get(7)], signal=L(tag))
            bank_owner[7] = l_tr
            fxn_free = l_tr
            l_ev = P.op("vector", (lambda m: lambda e: e.tensor_copy(
                f_ln2T.rearrange("p (d c) -> p d c", d=8)[:, :, m * 128:(m + 1) * 128],
                pbT[:].rearrange("p (d c) -> p d c", d=8)))(m),
                waits=[l_tr], signal=L(tag))
            bank_owner[7] = l_ev
            ln_done[m] = l_ev

        P.op("scalar", lambda e: e.preload_act_table(AF.Exp), signal=False)
        comb_ready = [None] * 16
        combT_ready = [None] * 16
        for m in range(16):
            b = m % 2
            lm = P.op("tensor", (lambda m, b: lambda e: (
                [e.matmul(pb[b][:, 0:8],
                          f_ln2T[:, d * OWN + m * 128: d * OWN + (m + 1) * 128],
                          c_wg[:, d * 8:(d + 1) * 8], start=(d == 0), stop=False)
                 for d in range(8)],
                e.matmul(pb[b][:, 0:8], c_onesrow, c_bg, start=False, stop=True))[1])(m, b),
                waits=[l_wg, l_bg, l_orow, ln_done[m], bank_owner.get(b)], signal=L(tag))
            bank_owner[b] = lm
            l_g1 = P.chain("vector", [
                (lambda b: lambda e: e.tensor_copy(f_g[:, 0:8], pb[b][:, 0:8]))(b),
                lambda e: e.tensor_reduce(f_g[:, 8:9], f_g[:, 0:8], axis=AX.X, op=ALU.max),
                lambda e: e.tensor_scalar(out=f_g[:, 16:24], in0=f_g[:, 0:8],
                                          scalar1=f_g[:, 8:9], scalar2=None,
                                          op0=ALU.is_equal),
                lambda e: e.scalar_tensor_tensor(out=f_g[:, 24:32], in0=f_g[:, 16:24],
                                                 scalar=-1e30, in1=f_g[:, 0:8],
                                                 op0=ALU.mult, op1=ALU.add),
                lambda e: e.tensor_reduce(f_g[:, 9:10], f_g[:, 24:32], axis=AX.X,
                                          op=ALU.max),
                lambda e: e.tensor_scalar(out=f_g[:, 10:11], in0=f_g[:, 8:9],
                                          scalar1=-1.0, scalar2=None, op0=ALU.mult),
            ], waits=[lm])
            bank_owner[b] = l_g1
            l_g2 = P.op("scalar", lambda e: e.activation(
                out=f_g[:, 11:12], in_=f_g[:, 9:10], func=AF.Exp, bias=f_g[:, 10:11]),
                waits=[l_g1], signal=L(tag))
            l_g3 = P.chain("vector", [
                lambda e: e.tensor_scalar(out=f_g[:, 12:13], in0=f_g[:, 11:12],
                                          scalar1=1.0, scalar2=None, op0=ALU.add),
                lambda e: e.reciprocal(f_g[:, 12:13], f_g[:, 12:13]),
                lambda e: e.tensor_tensor(out=f_g[:, 13:14], in0=f_g[:, 11:12],
                                          in1=f_g[:, 12:13], op=ALU.mult),
                lambda e: e.tensor_scalar(out=f_g[:, 24:32], in0=f_g[:, 0:8],
                                          scalar1=f_g[:, 9:10], scalar2=None,
                                          op0=ALU.is_equal),
                lambda e: e.tensor_scalar(out=f_g[:, 32:40], in0=f_g[:, 16:24],
                                          scalar1=-1.0, scalar2=1.0,
                                          op0=ALU.mult, op1=ALU.add),
                lambda e: e.tensor_tensor(out=f_g[:, 24:32], in0=f_g[:, 24:32],
                                          in1=f_g[:, 32:40], op=ALU.mult),
                lambda e: e.tensor_scalar(out=f_g[:, 16:24], in0=f_g[:, 16:24],
                                          scalar1=f_g[:, 12:13], scalar2=None,
                                          op0=ALU.mult),
                lambda e: e.tensor_scalar(out=f_g[:, 24:32], in0=f_g[:, 24:32],
                                          scalar1=f_g[:, 13:14], scalar2=None,
                                          op0=ALU.mult),
                (lambda m: lambda e: e.tensor_tensor(
                    out=f_comb[:, m * 8:(m + 1) * 8], in0=f_g[:, 16:24],
                    in1=f_g[:, 24:32], op=ALU.add))(m),
                (lambda m: lambda e: e.tensor_copy(
                    f_combb[:, 0:8], f_comb[:, m * 8:(m + 1) * 8]))(m),
            ], waits=[l_g2])
            comb_ready[m] = l_g3
            l_tc = P.op("tensor", lambda e: e.transpose(pbT[0:8, 0:128], f_combb[:, 0:8], c_ident),
                        waits=[l_g3, l_ident, bank_owner.get(7)], signal=L(tag))
            bank_owner[7] = l_tc
            l_te = P.op("vector", (lambda m: lambda e: e.tensor_copy(
                f_combT[0:8, m * 128:(m + 1) * 128], pbT[0:8, 0:128]))(m),
                waits=[l_tc], signal=L(tag))
            bank_owner[7] = l_te
            combT_ready[m] = l_te
        if debug:
            P.op("sync", dma(dbg["comb"][:], f_comb), waits=comb_ready, out_store=True)
            ldbg = P.op("sync", dma(dbg["ln2T"][:], f_ln2T), waits=ln_done, out_store=True)
            P.raw_wait("sync", ldbg)

        mm1_done = None
        mm2_done = None
        relu_done = {}
        store_done = [None] * 16
        for ex in range(NEXP):
            w1_ready = P.dma("w1", "gsdma", (lambda ex: lambda e: e.dma_start(
                out=f_w1, in_=w1t[ex]))(ex), waits=[mm1_done])
            w2_ready = P.dma("w2", "gsdma", (lambda ex: lambda e: e.dma_start(
                out=f_w2, in_=w2t[ex]))(ex), waits=[mm2_done])
            for tc in range(4):
                for hf in range(16):
                    b = hf % 4
                    wmm = [w1_ready, bank_owner.get(b)]
                    if ex == 0 and tc == 0 and hf < 4:
                        wmm += ln_done
                    lm = P.op("tensor", (lambda ex, tc, hf, b: lambda e: [
                        e.matmul(pb[b][:, :],
                                 f_w1[:, d * HID + hf * 128: d * HID + (hf + 1) * 128],
                                 f_ln2T[:, d * OWN + tc * 512: d * OWN + (tc + 1) * 512],
                                 start=(d == 0), stop=(d == 7))
                        for d in range(8)][-1])(ex, tc, hf, b),
                        waits=wmm, signal=L(tag))
                    bank_owner[b] = lm
                    lr = P.op("scalar", (lambda ex, hf, b: lambda e: e.activation(
                        out=f_hT[:, hf * 512:(hf + 1) * 512], in_=pb[b][:, :], func=AF.Relu,
                        bias=c_b1[:, ex * 16 + hf: ex * 16 + hf + 1]))(ex, hf, b),
                        waits=[lm, l_b1, mm2_done], signal=L(tag))
                    bank_owner[b] = lr
                    relu_done[hf] = lr
                mm1_done = relu_done[15]
                for m4 in range(4):
                    m = tc * 4 + m4
                    for oh in range(2):
                        b = 4 + (m4 * 2 + oh) % 4
                        wmm = [w2_ready, bank_owner.get(b)] + \
                              [relu_done[hf] for hf in range(16)]
                        if ex == 0:
                            wmm += [combT_ready[m], l_b2]
                        def mk_mm2(ex, m4, m, oh, b):
                            def f(e):
                                last = None
                                for hf in range(16):
                                    last = e.matmul(
                                        pb[b][:, :],
                                        f_hT[:, hf * 512 + m4 * 128: hf * 512 + (m4 + 1) * 128],
                                        f_w2[:, hf * D + oh * 512: hf * D + (oh + 1) * 512],
                                        start=(hf == 0), stop=(hf == 15 and ex != 0))
                                if ex == 0:
                                    last = e.matmul(
                                        pb[b][:, :], f_combT[0:8, m * 128:(m + 1) * 128],
                                        c_b2[0:8, oh * 512:(oh + 1) * 512],
                                        start=False, stop=True)
                                return last
                            return f
                        lm = P.op("tensor", mk_mm2(ex, m4, m, oh, b), waits=wmm,
                                  signal=L(tag))
                        bank_owner[b] = lm
                        le = P.op("vector", (lambda ex, m, oh, b: lambda e:
                            e.scalar_tensor_tensor(
                                out=r_res[m][:, oh * 512:(oh + 1) * 512],
                                in0=pb[b][:, :],
                                scalar=f_comb[:, m * 8 + ex: m * 8 + ex + 1],
                                in1=r_res[m][:, oh * 512:(oh + 1) * 512],
                                op0=ALU.mult, op1=ALU.add))(ex, m, oh, b),
                            waits=[lm, comb_ready[m], store_done[m]], signal=L(tag))
                        bank_owner[b] = le
                        res_labels[m] = le
                mm2_done = bank_owner[4 + 7 % 4]
            if ex == NEXP - 1:
                for m in range(16):
                    store_done[m] = P.op("sync", (lambda m: lambda e: e.dma_start(
                        out=out[m * 128:(m + 1) * 128, :], in_=r_res[m][:]))(m),
                        waits=[res_labels[m]], out_store=True)
        P.raw_wait("sync", store_done[15])
        P.emit_segment("ffn")

    return nc


def _host_constants():
    i = np.arange(0, DH, 2, dtype=np.float64) / DH
    theta = 1.0 / (10000.0 ** i)
    pos = np.arange(SEQ, dtype=np.float64)
    ang = pos[None, :] * theta[(np.arange(128) % 32)][:, None]
    cosfm = np.cos(ang).astype(np.float32)
    sinfm = np.sin(ang).astype(np.float32)
    identb = np.eye(128, dtype=np.float32)
    rot = np.zeros((128, 128), dtype=np.float32)
    for k in range(64):
        rot[2 * k + 1, 2 * k] = -1.0
        rot[2 * k, 2 * k + 1] = 1.0
    sel2 = np.zeros((2, 128), dtype=np.float32)
    sel2[0, 0:64] = 1.0
    sel2[1, 64:128] = 1.0
    bf = lambda a: np.ascontiguousarray(a.astype(ml_dtypes.bfloat16))
    return dict(costab=bf(cosfm), sintab=bf(sinfm), identb=bf(identb),
                rotmat=bf(rot), ones128b=bf(np.ones((128, 128), np.float32)),
                onescol_b=bf(np.ones((128, 1), np.float32)),
                onesrow_f=np.ones((1, 128), np.float32),
                sel2=bf(sel2), identf=identb.copy())


def make_in_maps(L0_attn_x, L0_attn_gamma, L0_attn_Wq, L0_attn_Wkv, L0_attn_W_bilinear,
                 L0_attn_Wo, L0_ffn_W1, L0_ffn_b1, L0_ffn_W2, L0_ffn_b2, L0_ffn_Wg,
                 L0_ffn_bg):
    C = np.ascontiguousarray
    x = np.asarray(L0_attn_x, dtype=np.float32)
    consts = _host_constants()
    A = np.asarray(L0_attn_W_bilinear, np.float32).transpose(0, 2, 1)
    wb_half = C(A.transpose(1, 0, 2).reshape(64, H * DH))
    wbt2 = np.concatenate([wb_half, wb_half], axis=0)
    shared = dict(
        gamma=C(np.asarray(L0_attn_gamma, np.float32)[None, :]),
        wq=C(np.asarray(L0_attn_Wq, np.float32)),
        wkv=C(np.asarray(L0_attn_Wkv, np.float32)),
        wbt2=C(wbt2),
        wo=C(np.asarray(L0_attn_Wo, np.float32).reshape(8, 128, D)
             .transpose(1, 0, 2).reshape(128, 8 * D)),
        w1t=C(np.asarray(L0_ffn_W1, np.float32).transpose(0, 2, 1)
              .reshape(NEXP, 8, 128, HID).transpose(0, 2, 1, 3)
              .reshape(NEXP, 128, 8 * HID)),
        w2t=C(np.asarray(L0_ffn_W2, np.float32).transpose(0, 2, 1)
              .reshape(NEXP, 16, 128, D).transpose(0, 2, 1, 3)
              .reshape(NEXP, 128, 16 * D)),
        b1c=C(np.asarray(L0_ffn_b1, np.float32).reshape(NEXP, HID // 128, 128)
              .transpose(2, 0, 1).reshape(128, NEXP * (HID // 128))),
        b2=C(np.asarray(L0_ffn_b2, np.float32)),
        wgt=C(np.asarray(L0_ffn_Wg, np.float32).T.reshape(8, 128, NEXP)
              .transpose(1, 0, 2).reshape(128, 8 * NEXP)),
        bg=C(np.asarray(L0_ffn_bg, np.float32)[None, :]),
        **consts,
    )
    in_maps = []
    for core in range(8):
        s, o = core // 2, core % 2
        xo = x[s, o * OWN:(o + 1) * OWN]
        xr = x[s, (1 - o) * OWN:(2 - o) * OWN]
        m = dict(shared)
        m["x_roll"] = C(np.concatenate([xo, xr], axis=0))
        own_sl = slice(o * OWN, (o + 1) * OWN)
        oth_sl = slice((1 - o) * OWN, (2 - o) * OWN)
        m["costab"] = C(np.concatenate([consts["costab"][:, own_sl],
                                        consts["costab"][:, oth_sl]], axis=1))
        m["sintab"] = C(np.concatenate([consts["sintab"][:, own_sl],
                                        consts["sintab"][:, oth_sl]], axis=1))
        in_maps.append(m)
    return in_maps


_NC_CACHE = {}


def kernel(**inputs):
    if "main" not in _NC_CACHE:
        _NC_CACHE["main"] = build_nc(debug=False)
    nc = _NC_CACHE["main"]
    in_maps = make_in_maps(**inputs)
    res = run_bass_kernel_spmd(nc, in_maps, core_ids=list(range(8)))
    outp = np.empty((4, SEQ, D), dtype=np.float32)
    for core in range(8):
        s, o = core // 2, core % 2
        outp[s, o * OWN:(o + 1) * OWN] = res.results[core]["out"]
    return outp


# version 06
# speedup vs baseline: 1.4418x; 1.0012x over previous
import numpy as np
import ml_dtypes
from contextlib import ExitStack

import concourse.bass as bass
import concourse.mybir as mybir
from concourse.bass_utils import run_bass_kernel_spmd

F32 = mybir.dt.float32
BF16 = mybir.dt.bfloat16
AF = mybir.ActivationFunctionType
ALU = mybir.AluOpType
AX = mybir.AxisListType

SEQ = 4096
OWN = 2048
CH = 512
NCH = OWN // CH
D = 1024
H = 16
DH = 64
NPAIR = 8
HID = 2048
NEXP = 8
EPS = 1e-5
SHIFT = 40.0

SBUF_BYTES = 207 * 1024
VTW = H * (DH + 1)


class Arena:

    def __init__(self, nc, st):
        self.words = SBUF_BYTES // 4
        self.t = st.enter_context(nc.sbuf_tensor("arena", [128, self.words], F32))
        self.off = 0
        self.marks = []

    def _alloc(self, bytes_pp):
        off = (self.off + 511) // 512 * 512
        assert off + bytes_pp <= SBUF_BYTES, f"arena overflow: {off}+{bytes_pp}"
        self.off = off + bytes_pp
        return off

    def f32(self, cols, rows=128):
        off = self._alloc(cols * 4)
        return self.t[0:rows, off // 4: off // 4 + cols]

    def bf16(self, cols, rows=128):
        off = self._alloc(cols * 2 + (cols * 2) % 4)
        nw = (cols * 2 + 3) // 4
        return self.t[0:rows, off // 4: off // 4 + nw].bitcast(BF16)[:, 0:cols]

    def mark(self):
        self.marks.append(self.off)

    def reset(self):
        self.off = self.marks.pop()


class Prog:
    QUEUES = ("sync", "gsdma", "tensor", "vector", "scalar")
    ENG = ("tensor", "vector", "scalar")

    def __init__(self, nc, st):
        self.nc = nc
        self.st = st
        self.sems = {q: st.enter_context(nc.semaphore("sem_" + q)) for q in self.ENG}
        self.sems["out"] = st.enter_context(nc.semaphore("sem_out"))
        self.ops = []
        self.labels = {}
        self.counts = {q: 0 for q in self.ENG}
        self.counts["out"] = 0
        self.watermarks = {}
        self.uid = 0

    def lab(self, tag="l"):
        self.uid += 1
        return f"{tag}{self.uid}"

    def _chan(self, name):
        if name not in self.sems:
            self.sems[name] = self.st.enter_context(self.nc.semaphore("sem_" + name))
            self.counts[name] = 0
        return name

    def op(self, queue, fn, waits=(), signal=None, out_store=False):
        if out_store:
            return self.dma("out", "sync", fn, waits=waits, signal=signal)
        assert queue in self.ENG, queue
        if signal is False:
            self.ops.append((queue, fn, tuple(w for w in waits if w), None, queue))
            return None
        if signal is None:
            signal = self.lab(queue)
        self.counts[queue] += 1
        assert signal not in self.labels, signal
        self.labels[signal] = (queue, self.counts[queue])
        self.ops.append((queue, fn, tuple(w for w in waits if w), signal, queue))
        return signal

    def dma(self, channel, queue, fn, waits=(), signal=None, serialize=False):
        assert queue in ("sync", "gsdma")
        ch = self._chan(channel)
        if signal is None:
            signal = self.lab(channel)
        self.counts[ch] += 16
        assert signal not in self.labels, signal
        self.labels[signal] = (ch, self.counts[ch])
        self.ops.append((queue, fn, tuple(w for w in waits if w), signal, ch))
        if serialize:
            self.raw_wait(queue, signal)
        return signal

    def chain(self, queue, fns, waits=()):
        lb = None
        for i, fn in enumerate(fns):
            w = list(waits) if i == 0 else [lb]
            lb = self.op(queue, fn, waits=w)
        return lb

    def raw_wait(self, queue, label):
        self.ops.append((queue, None, (label,), None, None))

    def emit_segment(self, name):
        per_q = {q: [] for q in self.QUEUES}
        for rec in self.ops:
            per_q[rec[0]].append(rec)
        self.ops = []
        active = [q for q in self.QUEUES if per_q[q]]

        with self.nc.Block(name) as blk:
            for q in active:
                def mk(qq):
                    def body(eng):
                        for (_, fn, waits, signal, semkey) in per_q[qq]:
                            for labw in waits:
                                pq, cnt = self.labels[labw]
                                key = (qq, pq)
                                if self.watermarks.get(key, 0) < cnt:
                                    eng.wait_ge(self.sems[pq], cnt)
                                    self.watermarks[key] = cnt
                            if fn is None:
                                continue
                            instr = fn(eng)
                            if signal is not None:
                                instr.then_inc(self.sems[semkey],
                                               16 if semkey not in self.ENG else 1)
                    return body
                getattr(blk, "gpsimd" if q == "gsdma" else q)(mk(q))
        for q in self.QUEUES:
            for pq in list(self.counts):
                self.watermarks[(q, pq)] = self.counts[pq]


def build_nc(debug=False):
    nc = bass.Bass("TRN2", target_bir_lowering=False)
    dt = nc.dram_tensor

    x_roll = dt("x_roll", [SEQ, D], F32, kind="ExternalInput").ap()
    gamma = dt("gamma", [1, D], F32, kind="ExternalInput").ap()
    wq = dt("wq", [D, D], F32, kind="ExternalInput").ap()
    wkv = dt("wkv", [D, 2 * D], F32, kind="ExternalInput").ap()
    wbt2 = dt("wbt2", [128, H * DH], F32, kind="ExternalInput").ap()
    wo = dt("wo", [128, 8 * D], F32, kind="ExternalInput").ap()
    w1t = dt("w1t", [NEXP, 128, 8 * HID], F32, kind="ExternalInput").ap()
    w2t = dt("w2t", [NEXP, 128, 16 * D], F32, kind="ExternalInput").ap()
    b1c = dt("b1c", [128, NEXP * (HID // 128)], F32, kind="ExternalInput").ap()
    b2 = dt("b2", [NEXP, D], F32, kind="ExternalInput").ap()
    wgt = dt("wgt", [128, 8 * NEXP], F32, kind="ExternalInput").ap()
    bg = dt("bg", [1, NEXP], F32, kind="ExternalInput").ap()
    costab = dt("costab", [128, SEQ], BF16, kind="ExternalInput").ap()
    sintab = dt("sintab", [128, SEQ], BF16, kind="ExternalInput").ap()
    identb = dt("identb", [128, 128], BF16, kind="ExternalInput").ap()
    rotmat = dt("rotmat", [128, 128], BF16, kind="ExternalInput").ap()
    ones128b = dt("ones128b", [128, 128], BF16, kind="ExternalInput").ap()
    onescol_b = dt("onescol_b", [128, 1], BF16, kind="ExternalInput").ap()
    onesrow_f = dt("onesrow_f", [1, 128], F32, kind="ExternalInput").ap()
    sel2 = dt("sel2", [2, 128], BF16, kind="ExternalInput").ap()
    identf = dt("identf", [128, 128], F32, kind="ExternalInput").ap()

    out = dt("out", [OWN, D], F32, kind="ExternalOutput").ap()
    dbg = {}
    if debug:
        for nm, shp, dtp in [
            ("kT", [128, 8 * OWN], BF16), ("qT", [128, 8 * OWN], BF16),
            ("v", [128, 16 * VTW], BF16), ("aoT", [128, 8 * OWN], BF16),
            ("res1", [OWN, D], F32), ("comb", [128, 128], F32),
            ("kTA", [128, 8 * OWN], BF16), ("vA", [128, 16 * VTW], BF16),
            ("xnT", [128, 8 * CH], BF16),
            ("sums", [128, 256], F32), ("ln2T", [128, 8 * OWN], BF16),
            ("bias", [1, 32], F32),
        ]:
            dbg[nm] = dt("dbg_" + nm, shp, dtp, kind="ExternalOutput").ap()

    with ExitStack() as st:
        P = Prog(nc, st)
        A = Arena(nc, st)
        pb = [st.enter_context(nc.psum_tensor(f"pb{i}", [128, 512], F32)) for i in range(8)]
        pbT = pb[7][:].bitcast(BF16)
        L = P.lab
        bank_owner = {}

        def dma(dst, src):
            return lambda e: e.dma_start(out=dst, in_=src)

        c_ident = A.bf16(128)
        c_rot = A.bf16(128)
        c_ones128b = A.bf16(128)
        c_onescol = A.bf16(1)
        c_onesrow = A.f32(128, rows=1)
        c_wbt = A.bf16(H * DH)
        c_b1 = A.f32(NEXP * (HID // 128))
        c_b2 = A.bf16(D, rows=NEXP)
        c_wg = A.bf16(8 * NEXP)
        c_bg = A.f32(NEXP, rows=1)
        c_sel2 = A.bf16(128, rows=2)
        c_identf = A.f32(128)
        s_bias = A.f32(32)
        s_bv = A.f32(32, rows=1)
        s_mx = A.f32(32, rows=1)

        A.mark()
        a_x = [A.f32(D) for _ in range(2)]
        a_ln1 = A.f32(D)
        a_xn = [A.bf16(D) for _ in range(2)]
        a_st = A.f32(16)
        a_mv = A.f32(4)
        a_rs = A.f32(8)
        a_xnT = A.bf16(8 * CH)
        a_wstg = [A.bf16(4 * D) for _ in range(2)]
        a_scr = A.f32(2048)
        a_kraw = [a_scr[:, 0:256].bitcast(BF16), a_scr[:, 256:512].bitcast(BF16)]
        a_rt1 = [a_scr[:, 512:768].bitcast(BF16), a_scr[:, 768:1024].bitcast(BF16)]
        a_rt2 = [a_scr[:, 1024:1280].bitcast(BF16), a_scr[:, 1280:1536].bitcast(BF16)]
        a_qhat = a_scr[:, 1536:1792].bitcast(BF16)
        a_sq = a_scr[:, 1792:2048].bitcast(BF16)
        a_gbc = A.bf16(D)
        a_kT = A.bf16(8 * OWN)
        a_v = A.bf16(16 * VTW + 64)
        s_r64 = a_scr[:, 0:1024]
        a_qT = A.bf16(8 * OWN)
        a_aoT = A.bf16(8 * OWN)
        a_pt = [A.bf16(512) for _ in range(4)]
        a_cos = A.bf16(OWN)
        a_sin = A.bf16(OWN)
        s_sumsT = A.f32(256)
        s_sumb16 = A.bf16(512, rows=2)
        assert A.off <= SBUF_BYTES, A.off

        l_ident = P.dma("cs", "sync", dma(c_ident, identb[:]), serialize=True)
        l_rot = P.dma("cs", "sync", dma(c_rot, rotmat[:]), serialize=True)
        l_o128 = P.dma("cs", "sync", dma(c_ones128b, ones128b[:]), serialize=True)
        l_ocol = P.dma("cs", "sync", dma(c_onescol, onescol_b[:]), serialize=True)
        l_orow = P.dma("cs", "sync", dma(c_onesrow, onesrow_f[:]), serialize=True)
        l_bg = P.dma("cs", "sync", dma(c_bg, bg[:]), serialize=True)
        l_b1 = P.dma("cs", "sync", dma(c_b1, b1c[:]), serialize=True)
        l_gam = P.dma("cs", "sync", dma(a_x[0][0:1, 0:D], gamma[:]), serialize=True)
        l_wbt = P.dma("cg", "gsdma", dma(c_wbt, wbt2[:]), serialize=True)
        l_b2 = P.dma("cg", "gsdma", dma(c_b2, b2[:]), serialize=True)
        l_wg = P.dma("cg", "gsdma", dma(c_wg, wgt[:]), serialize=True)
        l_sel2 = P.dma("cg", "gsdma", dma(c_sel2, sel2[:]), serialize=True)
        l_idf = P.dma("cg", "gsdma", dma(c_identf, identf[:]), serialize=True)
        P.op("scalar", lambda e: e.activation(out=c_wbt, in_=c_wbt, func=AF.Copy, scale=0.125),
             waits=[l_wbt], signal="wbt_ok")
        for i in range(2):
            P.op("tensor", (lambda i: lambda e: e.matmul(
                pb[i][:, :], c_onesrow, a_x[0][0:1, i * 512:(i + 1) * 512],
                start=True, stop=True))(i),
                waits=[l_orow, l_gam], signal=f"gbcmm{i}")
            P.op("vector", (lambda i: lambda e: e.tensor_copy(
                a_gbc[:, i * 512:(i + 1) * 512], pb[i][:, :]))(i),
                waits=[f"gbcmm{i}"], signal=f"gbc{i}")
        P.op("vector", lambda e: e.memset(s_mx[0:1, :], -1e30), signal="mxinit")
        for tt in range(16):
            P.op("vector", (lambda tt: lambda e: e.memset(
                a_v[:, tt * VTW:(tt + 1) * VTW].rearrange(
                    "p (h u) -> p h u", u=DH + 1)[:, :, DH:DH + 1], 1.0))(tt),
                signal=f"vone{tt}")
        P.op("vector", lambda e: e.memset(a_v[:, 16 * VTW:16 * VTW + 64], 1.0),
             signal="vpad")
        P.op("scalar", lambda e: e.preload_act_table(AF.Sqrt), signal=False)
        P.emit_segment("s0")

        def ln_block(src, dst, deps, tag, parity, gamma_mul):
            rsb = parity * 4
            l1 = P.chain("vector", [
                lambda e: (e.bn_stats(a_st[:, 0:6], src[:, 0:512]),
                           e.bn_stats(a_st[:, 6:12], src[:, 512:1024]))[-1],
                lambda e: e.bn_aggr(a_mv[:, 0:2], a_st[:, 0:12]),
                lambda e: e.tensor_scalar(out=a_rs[:, rsb:rsb + 1], in0=a_mv[:, 1:2],
                                          scalar1=EPS, scalar2=None, op0=ALU.add),
                lambda e: e.reciprocal(a_rs[:, rsb:rsb + 1], a_rs[:, rsb:rsb + 1]),
            ], waits=deps)
            l2 = P.op("scalar", lambda e: e.activation(
                out=a_rs[:, rsb + 1:rsb + 2], in_=a_rs[:, rsb:rsb + 1], func=AF.Sqrt),
                waits=[l1])
            l2b = P.op("vector", lambda e: e.tensor_scalar(
                out=a_rs[:, rsb + 2:rsb + 3], in0=a_mv[:, 0:1],
                scalar1=a_rs[:, rsb + 1:rsb + 2], scalar2=-1.0,
                op0=ALU.mult, op1=ALU.mult), waits=[l2])
            l2c = P.op("scalar", lambda e: e.activation(
                out=a_ln1[:], in_=src, func=AF.Identity,
                scale=a_rs[:, rsb + 1:rsb + 2], bias=a_rs[:, rsb + 2:rsb + 3]),
                waits=[l2b])
            l3 = P.chain("vector", [
                lambda e: (e.bn_stats(a_st[:, 0:6], a_ln1[:, 0:512]),
                           e.bn_stats(a_st[:, 6:12], a_ln1[:, 512:1024]))[-1],
                lambda e: e.bn_aggr(a_mv[:, 2:4], a_st[:, 0:12]),
                lambda e: e.tensor_scalar(out=a_rs[:, rsb:rsb + 1], in0=a_mv[:, 3:4],
                                          scalar1=EPS, scalar2=None, op0=ALU.add),
                lambda e: e.reciprocal(a_rs[:, rsb:rsb + 1], a_rs[:, rsb:rsb + 1]),
            ], waits=[l2c])
            l4 = P.op("scalar", lambda e: e.activation(
                out=a_rs[:, rsb + 1:rsb + 2], in_=a_rs[:, rsb:rsb + 1], func=AF.Sqrt),
                waits=[l3])
            l4b = P.op("vector", lambda e: e.tensor_scalar(
                out=a_rs[:, rsb + 2:rsb + 3], in0=a_mv[:, 2:3],
                scalar1=a_rs[:, rsb + 1:rsb + 2], scalar2=-1.0,
                op0=ALU.mult, op1=ALU.mult), waits=[l4])
            l4c = P.op("scalar", lambda e: e.activation(
                out=dst, in_=a_ln1[:], func=AF.Identity,
                scale=a_rs[:, rsb + 1:rsb + 2], bias=a_rs[:, rsb + 2:rsb + 3]),
                waits=[l4b] + (gamma_mul or []))
            l5 = P.op("vector", lambda e: e.tensor_tensor(
                out=dst, in0=dst, in1=a_gbc, op=ALU.mult),
                waits=[l4c, "gbc0", "gbc1"])
            return l5

        kraw_free = [None, None]

        def rope(bank, toksl, dst_ap, buf, tag, l_cos, l_sin, dst_free=None):
            kraw, rt1, rt2 = a_kraw[buf], a_rt1[buf], a_rt2[buf]
            wev = [bank_owner.get(bank)] + (kraw_free[buf] or [])
            l_ev = P.op("scalar", lambda e: e.activation(out=kraw, in_=pb[bank][:, :],
                                                         func=AF.Copy),
                        waits=wev, signal=L(tag))
            l_ro = P.op("tensor", lambda e: e.matmul(pb[bank][:, :], c_rot, kraw,
                                                     start=True, stop=True),
                        waits=[l_ev, l_rot], signal=L(tag))
            l_t1 = P.op("vector", lambda e: e.tensor_tensor(
                out=rt1, in0=kraw, in1=a_cos[:, toksl], op=ALU.mult),
                waits=[l_ev, l_cos], signal=L(tag))
            l_t2 = P.op("vector", lambda e: e.tensor_tensor(
                out=rt2, in0=pb[bank][:, :], in1=a_sin[:, toksl], op=ALU.mult),
                waits=[l_ro, l_t1, l_sin], signal=L(tag))
            l_dst = P.op("vector", lambda e: e.tensor_tensor(
                out=dst_ap, in0=rt1, in1=rt2, op=ALU.add),
                waits=[l_t2] + ([dst_free] if dst_free else []), signal=L(tag))
            kraw_free[buf] = [l_ro, l_t1]
            bank_owner[bank] = l_t2
            return l_dst

        def warm_burst(n=24):
            def f(e):
                r = None
                for _ in range(n):
                    r = e.matmul(pb[7][0:1, :], c_ones128b[0:1, 0:1],
                                 c_b2[0:1, 0:512], start=True, stop=True)
                return r
            P.op("tensor", f, signal=False)

        prev_sq = [None]
        mx_last = [None]

        def normsq(src_ap, bank, mslot, deps, tag):
            l_sq = P.op("vector", lambda e: e.tensor_tensor(
                out=a_sq, in0=src_ap, in1=src_ap, op=ALU.mult),
                waits=list(deps) + ([prev_sq[0]] if prev_sq[0] else []), signal=L(tag))
            l_mm = P.op("tensor", lambda e: e.matmul(pb[bank][0:1, :], c_onescol, a_sq,
                                                     start=True, stop=True),
                        waits=[l_sq, l_ocol, bank_owner.get(bank)], signal=L(tag))
            prev_sq[0] = l_mm
            l_mx = P.chain("vector", [
                lambda e: e.tensor_reduce(s_mx[0:1, 31:32], pb[bank][0:1, :],
                                          axis=AX.X, op=ALU.max),
                lambda e: e.tensor_tensor(out=s_mx[0:1, mslot:mslot + 1],
                                          in0=s_mx[0:1, mslot:mslot + 1],
                                          in1=s_mx[0:1, 31:32], op=ALU.max),
            ], waits=[l_mm, "mxinit"])
            bank_owner[bank] = l_mx
            mx_last[0] = l_mx
            return l_mx

        def proj_pass(pass_id):
            tag = f"P{pass_id}"
            base = pass_id * OWN
            if pass_id == 1:
                P.op("scalar", lambda e: e.preload_act_table(AF.Sqrt), signal=False)
            l_cos = P.dma("tcos", "sync", dma(a_cos, costab[:, base:base + OWN]))
            l_sin = P.dma("tsin", "sync", dma(a_sin, sintab[:, base:base + OWN]))
            warm_burst()
            wstg_free = [None, None]
            x_free = [None, None]
            xn_free = [None, None]
            xnT_free = None
            qhat_free = None
            for tc in range(NCH):
                xn_labels = []
                tr_labels = []
                for t4 in range(4):
                    tile = tc * 4 + t4
                    xb = a_x[t4 % 2]
                    l_x = P.dma(f"x{t4 % 2}", "sync",
                                dma(xb[:], x_roll[base + tile * 128: base + tile * 128 + 128, :]),
                                waits=[x_free[t4 % 2]])
                    xn = a_xn[t4 % 2]
                    l_xn = ln_block(xb[:], xn[:], [l_x], tag, t4 % 2,
                                    gamma_mul=[xn_free[t4 % 2]] if xn_free[t4 % 2] else [])
                    x_free[t4 % 2] = l_xn
                    wtr = [l_xn, l_ident, bank_owner.get(7)]
                    l_tr = P.op("tensor", (lambda xn: lambda e: [
                        e.transpose(pbT[:, d * 128:(d + 1) * 128],
                                    xn[:, d * 128:(d + 1) * 128], c_ident)
                        for d in range(8)][-1])(xn),
                        waits=wtr, signal=L(tag))
                    bank_owner[7] = l_tr
                    xn_free[t4 % 2] = l_tr
                    tr_labels.append(l_tr)
                    l_ev = P.op("vector", (lambda t4: lambda e: e.tensor_copy(
                        a_xnT.rearrange("p (d c) -> p d c", d=8)[:, :, t4 * 128:(t4 + 1) * 128],
                        pbT[:].rearrange("p (d c) -> p d c", d=8)))(t4),
                        waits=[l_tr] + ([xnT_free] if (xnT_free and t4 == 0) else []),
                        signal=L(tag))
                    bank_owner[7] = l_ev
                    xn_labels.append(l_ev)

                def wload(which, half):
                    sbuf = a_wstg[half]
                    r0 = half * 512
                    if which == "k":
                        src = wkv[r0:r0 + 512, 0:D]
                    elif which == "v":
                        src = wkv[r0:r0 + 512, D:2 * D]
                    else:
                        src = wq[r0:r0 + 512, 0:D]
                    src = src.rearrange("(c p) f -> p c f", p=128)
                    dst = sbuf.rearrange("p (c f) -> p c f", f=D)
                    l_w = P.dma(f"ws{half}", "gsdma", dma(dst, src),
                                waits=[wstg_free[half]])
                    return sbuf, l_w

                def proj_mm(which, d, dd, sbuf, l_w, first_waits):
                    if which == "k" or which == "q":
                        def f(e, sbuf=sbuf, d=d, dd=dd):
                            r = None
                            for kf in range(8):
                                r = e.matmul(pb[kf][:, :],
                                             sbuf[:, dd * D + kf * 128: dd * D + (kf + 1) * 128],
                                             a_xnT[:, d * CH:(d + 1) * CH],
                                             start=(d == 0), stop=(d == 7))
                            return r
                    else:
                        def f(e, sbuf=sbuf, d=d, dd=dd):
                            r = None
                            for tt in range(4):
                                for vh in range(2):
                                    r = e.matmul(pb[tt * 2 + vh][:, :],
                                                 a_xnT[:, d * CH + tt * 128: d * CH + (tt + 1) * 128],
                                                 sbuf[:, dd * D + vh * 512: dd * D + (vh + 1) * 512],
                                                 start=(d == 0), stop=(d == 7))
                            return r
                    return P.op("tensor", f, waits=[l_w] + first_waits, signal=L(tag))

                def proj_sweep(which):
                    last_mm = None
                    for half in range(2):
                        sbuf, l_w = wload(which, half)
                        for dd in range(4):
                            d = half * 4 + dd
                            fw = []
                            if d == 0:
                                fw = xn_labels + [bank_owner.get(b) for b in range(8)]
                            if last_mm:
                                fw = fw + [last_mm]
                            last_mm = proj_mm(which, d, dd, sbuf, l_w, fw)
                        wstg_free[half] = last_mm
                    for b in range(8):
                        bank_owner[b] = last_mm
                    return last_mm

                toksl = slice(tc * CH, (tc + 1) * CH)

                last_mm = proj_sweep("k")
                for kf in range(8):
                    dst = a_kT[:, kf * OWN + tc * CH: kf * OWN + (tc + 1) * CH]
                    l_dst = rope(kf, toksl, dst, kf % 2, tag, l_cos, l_sin)
                    normsq(dst, kf, 8 + 8 * pass_id + kf, [l_dst], tag)

                last_mm = proj_sweep("v")
                for tt in range(4):
                    for vh in range(2):
                        b = tt * 2 + vh
                        vtile = tc * 4 + tt
                        l_ev = P.op("vector", (lambda b, vtile, vh: lambda e: e.tensor_copy(
                            a_v[:, vtile * VTW + vh * 8 * (DH + 1):
                                vtile * VTW + (vh + 1) * 8 * (DH + 1)].rearrange(
                                "p (h u) -> p h u", u=DH + 1)[:, :, 0:DH],
                            pb[b][:].rearrange("p (h d) -> p h d", d=DH)))(b, vtile, vh),
                            waits=[bank_owner[b]], signal=L(tag))
                        bank_owner[b] = l_ev

                if pass_id == 0:
                    last_mm = proj_sweep("q")
                    for qf in range(8):
                        l_dst = rope(qf, toksl, a_qhat, qf % 2, tag, l_cos, l_sin,
                                     dst_free=qhat_free)
                        h0, h1 = 2 * qf, 2 * qf + 1
                        l_bl = P.op("tensor", (lambda qf, h0, h1: lambda e: (
                            e.matmul(pb[qf][0:64, :], c_wbt[0:64, h0 * 64:(h0 + 1) * 64],
                                     a_qhat[0:64, :], start=True, stop=True),
                            e.matmul(pb[qf][64:128, :], c_wbt[64:128, h1 * 64:(h1 + 1) * 64],
                                     a_qhat[64:128, :], start=True, stop=True))[-1])(qf, h0, h1),
                            waits=[l_dst, "wbt_ok", bank_owner.get(qf)], signal=L(tag))
                        qhat_free = l_bl
                        bank_owner[qf] = l_bl
                        dstq = a_qT[:, qf * OWN + tc * CH: qf * OWN + (tc + 1) * CH]
                        l_ev = P.op("vector", (lambda dstq, qf: lambda e: e.tensor_copy(
                            dstq, pb[qf][:, :]))(dstq, qf),
                            waits=[l_bl], signal=L(tag))
                        bank_owner[qf] = l_ev
                        normsq(dstq, qf, qf, [l_ev], tag)
                xnT_free = last_mm

        def bias_build(pass_id):
            tag = f"B{pass_id}"
            if pass_id == 0:
                l1 = P.op("vector", lambda e: e.tensor_tensor(
                    out=s_bv[0:1, 0:8], in0=s_mx[0:1, 0:8], in1=s_mx[0:1, 8:16], op=ALU.mult),
                    waits=[mx_last[0]], signal=L(tag))
                l2 = P.op("scalar", lambda e: e.activation(
                    out=s_bv[0:1, 0:8], in_=s_bv[0:1, 0:8], func=AF.Sqrt),
                    waits=[l1], signal=L(tag))
                l3 = P.op("vector", lambda e: e.tensor_scalar(
                    out=s_bv[0:1, 8:16], in0=s_bv[0:1, 0:8], scalar1=-1.0, scalar2=SHIFT,
                    op0=ALU.mult, op1=ALU.add), waits=[l2], signal=L(tag))
                lm = P.op("tensor", lambda e: e.matmul(
                    pb[0][:, 0:8], c_onesrow, s_bv[0:1, 8:16], start=True, stop=True),
                    waits=[l3, bank_owner.get(0)], signal=L(tag))
                bank_owner[0] = lm
                le = P.op("vector", lambda e: e.tensor_copy(s_bias[:, 0:8], pb[0][:, 0:8]),
                          waits=[lm], signal=L(tag))
                bank_owner[0] = le
                return le
            l1 = P.op("vector", lambda e: e.tensor_tensor(
                out=s_bv[0:1, 16:24], in0=s_mx[0:1, 0:8], in1=s_mx[0:1, 16:24], op=ALU.mult),
                waits=[mx_last[0]], signal=L(tag))
            l2 = P.op("scalar", lambda e: e.activation(
                out=s_bv[0:1, 16:24], in_=s_bv[0:1, 16:24], func=AF.Sqrt),
                waits=[l1], signal=L(tag))
            l3 = P.chain("vector", [
                lambda e: e.tensor_tensor(out=s_bv[0:1, 16:24], in0=s_bv[0:1, 16:24],
                                          in1=s_bv[0:1, 0:8], op=ALU.max),
                lambda e: e.tensor_scalar(out=s_bv[0:1, 24:32], in0=s_bv[0:1, 16:24],
                                          scalar1=-1.0, scalar2=SHIFT,
                                          op0=ALU.mult, op1=ALU.add),
                lambda e: e.tensor_tensor(out=s_bv[0:1, 16:24], in0=s_bv[0:1, 24:32],
                                          in1=s_bv[0:1, 8:16], op=ALU.subtract),
            ], waits=[l2])
            P.op("scalar", lambda e: e.preload_act_table(AF.Exp), waits=[l3],
                 signal=False)
            l4 = P.op("scalar", lambda e: e.activation(
                out=s_bv[0:1, 16:24], in_=s_bv[0:1, 16:24], func=AF.Exp),
                waits=[l3], signal=L(tag))
            lm = P.op("tensor", lambda e: (
                e.matmul(pb[0][:, 0:8], c_onesrow, s_bv[0:1, 24:32], start=True, stop=True),
                e.matmul(pb[1][:, 0:8], c_onesrow, s_bv[0:1, 16:24], start=True, stop=True))[-1],
                waits=[l4, bank_owner.get(0), bank_owner.get(1)], signal=L(tag))
            le = P.op("vector", lambda e: (
                e.tensor_copy(s_bias[:, 8:16], pb[0][:, 0:8]),
                e.tensor_copy(s_bias[:, 16:24], pb[1][:, 0:8]))[-1],
                waits=[lm], signal=L(tag))
            bank_owner[0] = le
            bank_owner[1] = le
            return le

        sums_last = [None]

        def sim_pass(pass_id, l_bias):
            tag = f"S{pass_id}"
            P.op("scalar", lambda e: e.preload_act_table(AF.Exp), waits=[l_bias],
                 signal=False)
            biascol = 0 if pass_id == 0 else 8
            warm_burst()
            for pr in range(NPAIR):
                h0, h1 = 2 * pr, 2 * pr + 1
                for qc in range(4):
                    qsl = slice(pr * OWN + qc * 512, pr * OWN + (qc + 1) * 512)
                    av_done = {}
                    exp_done = {}
                    def emit_sims(js):
                        for hp in (0, 1):
                            sb = hp * 2 + js % 2
                            ksl = slice(pr * OWN + js * 128, pr * OWN + (js + 1) * 128)
                            l_sim = P.op("tensor", (lambda sb, hp, ksl, qsl: lambda e: e.matmul(
                                pb[sb][:, :],
                                a_kT[hp * 64:(hp + 1) * 64, ksl],
                                a_qT[hp * 64:(hp + 1) * 64, qsl],
                                start=True, stop=True))(sb, hp, ksl, qsl),
                                waits=[bank_owner.get(sb)], signal=L(tag))
                            bank_owner[sb] = l_sim
                    def emit_exps(js):
                        for hp in (0, 1):
                            sb = hp * 2 + js % 2
                            pt = a_pt[hp * 2 + js % 2]
                            l_exp = P.op("scalar", (lambda sb, pt, pr: lambda e: e.activation(
                                out=pt, in_=pb[sb][:, :], func=AF.Exp,
                                bias=s_bias[:, biascol + pr: biascol + pr + 1]))(sb, pt, pr),
                                waits=[bank_owner[sb], av_done.get((hp, js - 2))],
                                signal=L(tag))
                            bank_owner[sb] = l_exp
                            exp_done[(hp, js)] = l_exp
                    def emit_avs(js):
                        for hp, h in ((0, h0), (1, h1)):
                            ob = 4 + hp
                            pt = a_pt[hp * 2 + js % 2]
                            wav = [exp_done[(hp, js)]]
                            if js == 0:
                                wav += [bank_owner.get(ob)]
                            l_av = P.op("tensor", (lambda ob, h, pt, js: lambda e: (
                                e.matmul(pb[ob][0:128, :],
                                         a_v[:, js * VTW + h * (DH + 1):
                                             js * VTW + h * (DH + 1) + 128],
                                         pt, start=(js == 0), stop=(js == 15)),
                                e.matmul(pb[7][0:1, 0:64], c_ones128b[0:1, 0:1],
                                         c_ones128b[0:1, 0:64], start=True,
                                         stop=True))[0])(ob, h, pt, js),
                                waits=wav, signal=L(tag))
                            av_done[(hp, js)] = l_av
                            if js == 15:
                                bank_owner[ob] = l_av
                    for js in range(16):
                        emit_sims(js)
                        emit_exps(js)
                        if js >= 1:
                            emit_avs(js - 1)
                    emit_avs(15)
                    l_e16 = P.op("vector", lambda e: (
                        e.tensor_copy(s_r64[64:65, 0:512], pb[4][64:65, :]),
                        e.tensor_copy(s_r64[64:65, 512:1024], pb[5][64:65, :]))[-1],
                        waits=[bank_owner[4], bank_owner[5], bank_owner.get(6)],
                        signal=L(tag))
                    bank_owner[4] = l_e16
                    bank_owner[5] = l_e16
                    l_tr = P.op("tensor", (lambda pr: lambda e: [
                        e.transpose(pb[6][:, qs * 16 + 2 * pr + hh: qs * 16 + 2 * pr + hh + 1],
                                    s_r64[64:65, hh * 512 + qs * 128: hh * 512 + (qs + 1) * 128],
                                    c_identf[64:65, 64:65])
                        for hh in range(2) for qs in range(4)][-1])(pr),
                        waits=[l_e16, l_idf, bank_owner[6]], signal=L(tag))
                    bank_owner[6] = l_tr
                    for hp, h in ((0, h0), (1, h1)):
                        ob = 4 + hp
                        asl = slice(pr * OWN + qc * 512, pr * OWN + (qc + 1) * 512)
                        dsta = a_aoT[hp * 64:hp * 64 + 64, asl]
                        if pass_id == 0:
                            l1 = P.op("vector", (lambda dsta, ob: lambda e: e.tensor_copy(
                                dsta, pb[ob][0:64, :]))(dsta, ob),
                                waits=[bank_owner[ob]], signal=L(tag))
                        else:
                            l1 = P.op("vector", (lambda dsta, ob, hp, pr: lambda e:
                                e.scalar_tensor_tensor(
                                    out=dsta, in0=dsta,
                                    scalar=s_bias[hp * 64:hp * 64 + 64, 16 + pr:17 + pr],
                                    in1=pb[ob][0:64, :], op0=ALU.mult, op1=ALU.add))(
                                        dsta, ob, hp, pr),
                                waits=[bank_owner[ob]], signal=L(tag))
                        bank_owner[ob] = l1
                    sTdst = s_sumsT.rearrange("p (s h) -> p s h", h=16)[
                        :, qc * 4:(qc + 1) * 4, 2 * pr:2 * pr + 2]
                    sTsrc = pb[6][:, 0:64].rearrange("p (s h) -> p s h", h=16)[
                        :, :, 2 * pr:2 * pr + 2]
                    if pass_id == 0:
                        l_sc = P.op("vector", (lambda sTdst, sTsrc: lambda e: e.tensor_copy(
                            sTdst, sTsrc))(sTdst, sTsrc),
                            waits=[l_tr], signal=L(tag))
                    else:
                        l_sc = P.op("vector", (lambda sTdst, sTsrc, pr: lambda e:
                            e.scalar_tensor_tensor(
                                out=sTdst, in0=sTdst,
                                scalar=s_bias[:, 16 + pr:17 + pr],
                                in1=sTsrc, op0=ALU.mult, op1=ALU.add))(sTdst, sTsrc, pr),
                            waits=[l_tr], signal=L(tag))
                    bank_owner[6] = l_sc
                    sums_last[0] = l_sc

        def attn_normalize():
            tag = "N"
            l_r = P.op("vector", lambda e: e.reciprocal(s_sumsT, s_sumsT),
                       waits=[sums_last[0]])
            for pr in range(NPAIR):
                for qc in range(4):
                    b = (pr * 4 + qc) % 4
                    asl = slice(pr * OWN + qc * 512, pr * OWN + (qc + 1) * 512)
                    src_r = s_sumsT.rearrange("p (s h) -> p s h", h=16)[
                        :, qc * 4:(qc + 1) * 4, 2 * pr:2 * pr + 2]
                    l_t = P.op("tensor", (lambda b, pr, qc: lambda e: [
                        e.transpose(pb[b][0:2, qs * 128:(qs + 1) * 128],
                                    s_sumsT[:, (qc * 4 + qs) * 16 + 2 * pr:
                                            (qc * 4 + qs) * 16 + 2 * pr + 2],
                                    c_identf)
                        for qs in range(4)][-1])(b, pr, qc),
                        waits=[l_r, l_idf, bank_owner.get(b)], signal=L(tag))
                    bank_owner[b] = l_t
                    l_c = P.op("vector", (lambda b: lambda e: e.tensor_copy(
                        s_sumb16[0:2, :], pb[b][0:2, :]))(b),
                        waits=[l_t], signal=L(tag))
                    bank_owner[b] = l_c
                    lm = P.op("tensor", (lambda b: lambda e: e.matmul(
                        pb[b][:, :], c_sel2[0:2, :], s_sumb16[0:2, :],
                        start=True, stop=True))(b),
                        waits=[l_c, l_sel2], signal=L(tag))
                    bank_owner[b] = lm
                    lt = P.op("vector", (lambda b, asl: lambda e: e.tensor_tensor(
                        out=a_aoT[:, asl], in0=a_aoT[:, asl], in1=pb[b][:, :],
                        op=ALU.mult))(b, asl),
                        waits=[lm], signal=L(tag))
                    bank_owner[b] = lt
            return lt

        import os as _os
        if not _os.environ.get("SKIP_A"):
            proj_pass(0)
            l_b = bias_build(0)
            if not _os.environ.get("SKIP_SIMA"):
                sim_pass(0, l_b)
        else:
            l_b = bias_build(0)
        if debug:
            P.op("sync", dma(dbg["kTA"][:], a_kT), waits=[l_b, sums_last[0]], out_store=True)
            ldbgA = P.op("sync", dma(dbg["vA"][:], a_v[:, 0:16 * VTW]),
                         waits=[l_b, sums_last[0]], out_store=True)
            P.raw_wait("sync", ldbgA)
        P.emit_segment("attnA")
        import os as _os
        proj_pass(1)
        l_b = bias_build(1)
        if not _os.environ.get("SKIP_SIMB"):
            sim_pass(1, l_b)
            l_fin = attn_normalize()
        else:
            l_fin = l_b
        if debug:
            P.op("sync", dma(dbg["kT"][:], a_kT), waits=[l_fin], out_store=True)
            P.op("sync", dma(dbg["qT"][:], a_qT), waits=[l_fin], out_store=True)
            P.op("sync", dma(dbg["v"][:], a_v[:, 0:16 * VTW]), waits=[l_fin], out_store=True)
            P.op("sync", dma(dbg["aoT"][:], a_aoT), waits=[l_fin], out_store=True)
            P.op("sync", dma(dbg["sums"][:], s_sumsT), waits=[l_fin], out_store=True)
            P.op("sync", dma(dbg["xnT"][:], a_xnT), waits=[l_fin], out_store=True)
            ldbg = P.op("sync", dma(dbg["bias"][:], s_bv[0:1, 0:32]), waits=[l_fin],
                        out_store=True)
            P.raw_wait("sync", ldbg)
        P.emit_segment("attnB")

        A.reset()
        A.mark()
        r_res = [A.f32(D) for _ in range(16)]
        f_w1 = A.bf16(8 * HID)
        r_wo = f_w1.bitcast(BF16)[:, 0:8 * D] if False else None
        r_wo = f_w1[:, 0:8 * D]
        f_ln2T = A.bf16(8 * OWN)
        f_w2 = A.bf16(16 * D)
        f_hT = A.bf16(16 * 512)
        f_comb = A.f32(128)
        f_combb = A.bf16(8)
        f_combT = A.bf16(OWN, rows=8)
        f_xn = A.bf16(D)
        f_g = A.f32(40)
        f_st = A.f32(16)
        f_mv = A.f32(4)
        f_rs = A.f32(8)
        assert A.off <= SBUF_BYTES, A.off

        bank_owner.clear()
        tag = "O"
        l_wo = P.dma("w1", "gsdma", dma(r_wo, wo[:]))
        res_labels = [None] * 16
        res_prev = [None] * 4
        warm_burst()
        for m in range(16):
            l_x = P.dma(f"rs{m % 4}", "sync",
                        dma(r_res[m][:], x_roll[m * 128:(m + 1) * 128, :]),
                        waits=[res_prev[m % 4]])
            for oh in range(2):
                b = (m * 2 + oh) % 4
                lm = P.op("tensor", (lambda m, oh, b: lambda e: (
                    [e.matmul(pb[b][:, :],
                              a_aoT[:, c * OWN + m * 128: c * OWN + (m + 1) * 128],
                              r_wo[:, c * D + oh * 512: c * D + (oh + 1) * 512],
                              start=(c == 0), stop=(c == 7))
                     for c in range(8)],
                    e.matmul(pb[7][0:1, 0:64], c_ones128b[0:1, 0:1],
                             c_ones128b[0:1, 0:64], start=True, stop=True))[0][-1])(m, oh, b),
                    waits=[l_wo, bank_owner.get(b)], signal=L(tag))
                bank_owner[b] = lm
                le = P.op("vector", (lambda m, oh, b: lambda e: e.tensor_tensor(
                    out=r_res[m][:, oh * 512:(oh + 1) * 512],
                    in0=r_res[m][:, oh * 512:(oh + 1) * 512],
                    in1=pb[b][:, :], op=ALU.add))(m, oh, b),
                    waits=[lm, l_x], signal=L(tag))
                bank_owner[b] = le
                res_labels[m] = le
            res_prev[m % 4] = res_labels[m]
            oproj_last_mm = lm
        res_dump = [None] * 16
        if debug:
            for m in range(16):
                res_dump[m] = P.dma("dres", "sync",
                                    dma(dbg["res1"][m * 128:(m + 1) * 128, :], r_res[m][:]),
                                    waits=[res_labels[m]], serialize=True)

        tag = "F"
        ln_done = [None] * 16
        P.op("scalar", lambda e: e.preload_act_table(AF.Sqrt), signal=False)
        fxn_free = None
        for m in range(16):
            src = r_res[m][:]
            rsb = (m % 2) * 4
            l1 = P.chain("vector", [
                (lambda src: lambda e: (e.bn_stats(f_st[:, 0:6], src[:, 0:512]),
                                        e.bn_stats(f_st[:, 6:12], src[:, 512:1024]))[-1])(src),
                (lambda rsb: lambda e: e.bn_aggr(f_mv[:, 0:2], f_st[:, 0:12]))(rsb),
                (lambda rsb: lambda e: e.tensor_scalar(
                    out=f_rs[:, rsb:rsb + 1], in0=f_mv[:, 1:2], scalar1=EPS,
                    scalar2=None, op0=ALU.add))(rsb),
                (lambda rsb: lambda e: e.reciprocal(
                    f_rs[:, rsb:rsb + 1], f_rs[:, rsb:rsb + 1]))(rsb),
            ], waits=[res_labels[m]])
            l2 = P.op("scalar", (lambda rsb: lambda e: e.activation(
                out=f_rs[:, rsb + 1:rsb + 2], in_=f_rs[:, rsb:rsb + 1], func=AF.Sqrt))(rsb),
                waits=[l1], signal=L(tag))
            l3 = P.chain("vector", [
                (lambda rsb: lambda e: e.tensor_scalar(
                    out=f_rs[:, rsb + 2:rsb + 3], in0=f_mv[:, 0:1],
                    scalar1=f_rs[:, rsb + 1:rsb + 2], scalar2=-1.0,
                    op0=ALU.mult, op1=ALU.mult))(rsb),
                (lambda src, rsb: lambda e: e.tensor_scalar(
                    out=f_xn, in0=src, scalar1=f_rs[:, rsb + 1:rsb + 2],
                    scalar2=f_rs[:, rsb + 2:rsb + 3], op0=ALU.mult, op1=ALU.add))(src, rsb),
            ], waits=[l2] + ([fxn_free] if fxn_free else []))
            l_tr = P.op("tensor", lambda e: [
                e.transpose(pbT[:, d * 128:(d + 1) * 128],
                            f_xn[:, d * 128:(d + 1) * 128], c_ident)
                for d in range(8)][-1],
                waits=[l3, l_ident, bank_owner.get(7)], signal=L(tag))
            bank_owner[7] = l_tr
            fxn_free = l_tr
            l_ev = P.op("vector", (lambda m: lambda e: e.tensor_copy(
                f_ln2T.rearrange("p (d c) -> p d c", d=8)[:, :, m * 128:(m + 1) * 128],
                pbT[:].rearrange("p (d c) -> p d c", d=8)))(m),
                waits=[l_tr], signal=L(tag))
            bank_owner[7] = l_ev
            ln_done[m] = l_ev

        P.op("scalar", lambda e: e.preload_act_table(AF.Exp), signal=False)
        comb_ready = [None] * 16
        combT_ready = [None] * 16
        for m in range(16):
            b = m % 2
            lm = P.op("tensor", (lambda m, b: lambda e: (
                [e.matmul(pb[b][:, 0:8],
                          f_ln2T[:, d * OWN + m * 128: d * OWN + (m + 1) * 128],
                          c_wg[:, d * 8:(d + 1) * 8], start=(d == 0), stop=False)
                 for d in range(8)],
                e.matmul(pb[b][:, 0:8], c_onesrow, c_bg, start=False, stop=True))[1])(m, b),
                waits=[l_wg, l_bg, l_orow, ln_done[m], bank_owner.get(b)], signal=L(tag))
            bank_owner[b] = lm
            l_g1 = P.chain("vector", [
                (lambda b: lambda e: e.tensor_copy(f_g[:, 0:8], pb[b][:, 0:8]))(b),
                lambda e: e.tensor_reduce(f_g[:, 8:9], f_g[:, 0:8], axis=AX.X, op=ALU.max),
                lambda e: e.tensor_scalar(out=f_g[:, 16:24], in0=f_g[:, 0:8],
                                          scalar1=f_g[:, 8:9], scalar2=None,
                                          op0=ALU.is_equal),
                lambda e: e.scalar_tensor_tensor(out=f_g[:, 24:32], in0=f_g[:, 16:24],
                                                 scalar=-1e30, in1=f_g[:, 0:8],
                                                 op0=ALU.mult, op1=ALU.add),
                lambda e: e.tensor_reduce(f_g[:, 9:10], f_g[:, 24:32], axis=AX.X,
                                          op=ALU.max),
                lambda e: e.tensor_scalar(out=f_g[:, 10:11], in0=f_g[:, 8:9],
                                          scalar1=-1.0, scalar2=None, op0=ALU.mult),
            ], waits=[lm])
            bank_owner[b] = l_g1
            l_g2 = P.op("scalar", lambda e: e.activation(
                out=f_g[:, 11:12], in_=f_g[:, 9:10], func=AF.Exp, bias=f_g[:, 10:11]),
                waits=[l_g1], signal=L(tag))
            l_g3 = P.chain("vector", [
                lambda e: e.tensor_scalar(out=f_g[:, 12:13], in0=f_g[:, 11:12],
                                          scalar1=1.0, scalar2=None, op0=ALU.add),
                lambda e: e.reciprocal(f_g[:, 12:13], f_g[:, 12:13]),
                lambda e: e.tensor_tensor(out=f_g[:, 13:14], in0=f_g[:, 11:12],
                                          in1=f_g[:, 12:13], op=ALU.mult),
                lambda e: e.tensor_scalar(out=f_g[:, 24:32], in0=f_g[:, 0:8],
                                          scalar1=f_g[:, 9:10], scalar2=None,
                                          op0=ALU.is_equal),
                lambda e: e.tensor_scalar(out=f_g[:, 32:40], in0=f_g[:, 16:24],
                                          scalar1=-1.0, scalar2=1.0,
                                          op0=ALU.mult, op1=ALU.add),
                lambda e: e.tensor_tensor(out=f_g[:, 24:32], in0=f_g[:, 24:32],
                                          in1=f_g[:, 32:40], op=ALU.mult),
                lambda e: e.tensor_scalar(out=f_g[:, 16:24], in0=f_g[:, 16:24],
                                          scalar1=f_g[:, 12:13], scalar2=None,
                                          op0=ALU.mult),
                lambda e: e.tensor_scalar(out=f_g[:, 24:32], in0=f_g[:, 24:32],
                                          scalar1=f_g[:, 13:14], scalar2=None,
                                          op0=ALU.mult),
                (lambda m: lambda e: e.tensor_tensor(
                    out=f_comb[:, m * 8:(m + 1) * 8], in0=f_g[:, 16:24],
                    in1=f_g[:, 24:32], op=ALU.add))(m),
                (lambda m: lambda e: e.tensor_copy(
                    f_combb[:, 0:8], f_comb[:, m * 8:(m + 1) * 8]))(m),
            ], waits=[l_g2])
            comb_ready[m] = l_g3
            l_tc = P.op("tensor", lambda e: e.transpose(pbT[0:8, 0:128], f_combb[:, 0:8], c_ident),
                        waits=[l_g3, l_ident, bank_owner.get(7)], signal=L(tag))
            bank_owner[7] = l_tc
            l_te = P.op("vector", (lambda m: lambda e: e.tensor_copy(
                f_combT[0:8, m * 128:(m + 1) * 128], pbT[0:8, 0:128]))(m),
                waits=[l_tc], signal=L(tag))
            bank_owner[7] = l_te
            combT_ready[m] = l_te
        if debug:
            P.op("sync", dma(dbg["comb"][:], f_comb), waits=comb_ready, out_store=True)
            ldbg = P.op("sync", dma(dbg["ln2T"][:], f_ln2T), waits=ln_done, out_store=True)
            P.raw_wait("sync", ldbg)

        mm1_done = None
        mm2_done = None
        relu_done = {}
        store_done = [None] * 16
        for ex in range(NEXP):
            w1_ready = P.dma("w1", "gsdma", (lambda ex: lambda e: e.dma_start(
                out=f_w1, in_=w1t[ex]))(ex),
                waits=[mm1_done if ex else oproj_last_mm])
            w2_ready = P.dma("w2", "gsdma", (lambda ex: lambda e: e.dma_start(
                out=f_w2, in_=w2t[ex]))(ex), waits=[mm2_done])
            for tc in range(4):
                for hf in range(16):
                    b = hf % 4
                    wmm = [w1_ready, bank_owner.get(b)]
                    if ex == 0 and tc == 0 and hf < 4:
                        wmm += ln_done
                    lm = P.op("tensor", (lambda ex, tc, hf, b: lambda e: [
                        e.matmul(pb[b][:, :],
                                 f_w1[:, d * HID + hf * 128: d * HID + (hf + 1) * 128],
                                 f_ln2T[:, d * OWN + tc * 512: d * OWN + (tc + 1) * 512],
                                 start=(d == 0), stop=(d == 7))
                        for d in range(8)][-1])(ex, tc, hf, b),
                        waits=wmm, signal=L(tag))
                    bank_owner[b] = lm
                    lr = P.op("scalar", (lambda ex, hf, b: lambda e: e.activation(
                        out=f_hT[:, hf * 512:(hf + 1) * 512], in_=pb[b][:, :], func=AF.Relu,
                        bias=c_b1[:, ex * 16 + hf: ex * 16 + hf + 1]))(ex, hf, b),
                        waits=[lm, l_b1, mm2_done], signal=L(tag))
                    bank_owner[b] = lr
                    relu_done[hf] = lr
                mm1_done = relu_done[15]
                for m4 in range(4):
                    m = tc * 4 + m4
                    for oh in range(2):
                        b = 4 + (m4 * 2 + oh) % 4
                        wmm = [w2_ready, bank_owner.get(b)] + \
                              [relu_done[hf] for hf in range(16)]
                        if ex == 0:
                            wmm += [combT_ready[m], l_b2]
                        def mk_mm2(ex, m4, m, oh, b):
                            def f(e):
                                last = None
                                for hf in range(16):
                                    last = e.matmul(
                                        pb[b][:, :],
                                        f_hT[:, hf * 512 + m4 * 128: hf * 512 + (m4 + 1) * 128],
                                        f_w2[:, hf * D + oh * 512: hf * D + (oh + 1) * 512],
                                        start=(hf == 0), stop=(hf == 15 and ex != 0))
                                if ex == 0:
                                    last = e.matmul(
                                        pb[b][:, :], f_combT[0:8, m * 128:(m + 1) * 128],
                                        c_b2[0:8, oh * 512:(oh + 1) * 512],
                                        start=False, stop=True)
                                return last
                            return f
                        lm = P.op("tensor", mk_mm2(ex, m4, m, oh, b), waits=wmm,
                                  signal=L(tag))
                        bank_owner[b] = lm
                        le = P.op("vector", (lambda ex, m, oh, b: lambda e:
                            e.scalar_tensor_tensor(
                                out=r_res[m][:, oh * 512:(oh + 1) * 512],
                                in0=pb[b][:, :],
                                scalar=f_comb[:, m * 8 + ex: m * 8 + ex + 1],
                                in1=r_res[m][:, oh * 512:(oh + 1) * 512],
                                op0=ALU.mult, op1=ALU.add))(ex, m, oh, b),
                            waits=[lm, comb_ready[m], store_done[m]] +
                                  ([res_dump[m]] if ex == 0 else []),
                            signal=L(tag))
                        bank_owner[b] = le
                        res_labels[m] = le
                mm2_done = bank_owner[4 + 7 % 4]
            if ex == NEXP - 1:
                for m in range(16):
                    store_done[m] = P.op("sync", (lambda m: lambda e: e.dma_start(
                        out=out[m * 128:(m + 1) * 128, :], in_=r_res[m][:]))(m),
                        waits=[res_labels[m]], out_store=True)
        P.raw_wait("sync", store_done[15])
        P.emit_segment("ffn")

    return nc


def _host_constants():
    i = np.arange(0, DH, 2, dtype=np.float64) / DH
    theta = 1.0 / (10000.0 ** i)
    pos = np.arange(SEQ, dtype=np.float64)
    ang = pos[None, :] * theta[(np.arange(128) % 32)][:, None]
    cosfm = np.cos(ang).astype(np.float32)
    sinfm = np.sin(ang).astype(np.float32)
    identb = np.eye(128, dtype=np.float32)
    rot = np.zeros((128, 128), dtype=np.float32)
    for k in range(64):
        rot[2 * k + 1, 2 * k] = -1.0
        rot[2 * k, 2 * k + 1] = 1.0
    sel2 = np.zeros((2, 128), dtype=np.float32)
    sel2[0, 0:64] = 1.0
    sel2[1, 64:128] = 1.0
    bf = lambda a: np.ascontiguousarray(a.astype(ml_dtypes.bfloat16))
    return dict(costab=bf(cosfm), sintab=bf(sinfm), identb=bf(identb),
                rotmat=bf(rot), ones128b=bf(np.ones((128, 128), np.float32)),
                onescol_b=bf(np.ones((128, 1), np.float32)),
                onesrow_f=np.ones((1, 128), np.float32),
                sel2=bf(sel2), identf=identb.copy())


def make_in_maps(L0_attn_x, L0_attn_gamma, L0_attn_Wq, L0_attn_Wkv, L0_attn_W_bilinear,
                 L0_attn_Wo, L0_ffn_W1, L0_ffn_b1, L0_ffn_W2, L0_ffn_b2, L0_ffn_Wg,
                 L0_ffn_bg):
    C = np.ascontiguousarray
    x = np.asarray(L0_attn_x, dtype=np.float32)
    consts = _host_constants()
    A = np.asarray(L0_attn_W_bilinear, np.float32).transpose(0, 2, 1)
    wb_half = C(A.transpose(1, 0, 2).reshape(64, H * DH))
    wbt2 = np.concatenate([wb_half, wb_half], axis=0)
    shared = dict(
        gamma=C(np.asarray(L0_attn_gamma, np.float32)[None, :]),
        wq=C(np.asarray(L0_attn_Wq, np.float32)),
        wkv=C(np.asarray(L0_attn_Wkv, np.float32)),
        wbt2=C(wbt2),
        wo=C(np.asarray(L0_attn_Wo, np.float32).reshape(8, 128, D)
             .transpose(1, 0, 2).reshape(128, 8 * D)),
        w1t=C(np.asarray(L0_ffn_W1, np.float32).transpose(0, 2, 1)
              .reshape(NEXP, 8, 128, HID).transpose(0, 2, 1, 3)
              .reshape(NEXP, 128, 8 * HID)),
        w2t=C(np.asarray(L0_ffn_W2, np.float32).transpose(0, 2, 1)
              .reshape(NEXP, 16, 128, D).transpose(0, 2, 1, 3)
              .reshape(NEXP, 128, 16 * D)),
        b1c=C(np.asarray(L0_ffn_b1, np.float32).reshape(NEXP, HID // 128, 128)
              .transpose(2, 0, 1).reshape(128, NEXP * (HID // 128))),
        b2=C(np.asarray(L0_ffn_b2, np.float32)),
        wgt=C(np.asarray(L0_ffn_Wg, np.float32).T.reshape(8, 128, NEXP)
              .transpose(1, 0, 2).reshape(128, 8 * NEXP)),
        bg=C(np.asarray(L0_ffn_bg, np.float32)[None, :]),
        **consts,
    )
    in_maps = []
    for core in range(8):
        s, o = core // 2, core % 2
        xo = x[s, o * OWN:(o + 1) * OWN]
        xr = x[s, (1 - o) * OWN:(2 - o) * OWN]
        m = dict(shared)
        m["x_roll"] = C(np.concatenate([xo, xr], axis=0))
        own_sl = slice(o * OWN, (o + 1) * OWN)
        oth_sl = slice((1 - o) * OWN, (2 - o) * OWN)
        m["costab"] = C(np.concatenate([consts["costab"][:, own_sl],
                                        consts["costab"][:, oth_sl]], axis=1))
        m["sintab"] = C(np.concatenate([consts["sintab"][:, own_sl],
                                        consts["sintab"][:, oth_sl]], axis=1))
        in_maps.append(m)
    return in_maps


_NC_CACHE = {}


def kernel(**inputs):
    if "main" not in _NC_CACHE:
        _NC_CACHE["main"] = build_nc(debug=False)
    nc = _NC_CACHE["main"]
    in_maps = make_in_maps(**inputs)
    res = run_bass_kernel_spmd(nc, in_maps, core_ids=list(range(8)))
    outp = np.empty((4, SEQ, D), dtype=np.float32)
    for core in range(8):
        s, o = core // 2, core % 2
        outp[s, o * OWN:(o + 1) * OWN] = res.results[core]["out"]
    return outp


# version 07
# speedup vs baseline: 1.6372x; 1.1355x over previous
import numpy as np
import ml_dtypes
from contextlib import ExitStack

import concourse.bass as bass
import concourse.mybir as mybir
from concourse.bass_utils import run_bass_kernel_spmd

F32 = mybir.dt.float32
BF16 = mybir.dt.bfloat16
AF = mybir.ActivationFunctionType
ALU = mybir.AluOpType
AX = mybir.AxisListType

SEQ = 4096
OWN = 2048
CH = 512
NCH = OWN // CH
D = 1024
H = 16
DH = 64
NPAIR = 8
HID = 2048
NEXP = 8
EPS = 1e-5
SHIFT = 40.0

SBUF_BYTES = 207 * 1024
VTW = H * (DH + 1)


class Arena:

    def __init__(self, nc, st):
        self.words = SBUF_BYTES // 4
        self.t = st.enter_context(nc.sbuf_tensor("arena", [128, self.words], F32))
        self.off = 0
        self.marks = []

    def _alloc(self, bytes_pp):
        off = (self.off + 511) // 512 * 512
        assert off + bytes_pp <= SBUF_BYTES, f"arena overflow: {off}+{bytes_pp}"
        self.off = off + bytes_pp
        return off

    def f32(self, cols, rows=128):
        off = self._alloc(cols * 4)
        return self.t[0:rows, off // 4: off // 4 + cols]

    def bf16(self, cols, rows=128):
        off = self._alloc(cols * 2 + (cols * 2) % 4)
        nw = (cols * 2 + 3) // 4
        return self.t[0:rows, off // 4: off // 4 + nw].bitcast(BF16)[:, 0:cols]

    def mark(self):
        self.marks.append(self.off)

    def reset(self):
        self.off = self.marks.pop()


class Prog:
    QUEUES = ("sync", "gsdma", "tensor", "vector", "scalar")
    ENG = ("tensor", "vector", "scalar")

    def __init__(self, nc, st):
        self.nc = nc
        self.st = st
        self.sems = {q: st.enter_context(nc.semaphore("sem_" + q)) for q in self.ENG}
        self.sems["out"] = st.enter_context(nc.semaphore("sem_out"))
        self.ops = []
        self.labels = {}
        self.counts = {q: 0 for q in self.ENG}
        self.counts["out"] = 0
        self.watermarks = {}
        self.uid = 0

    def lab(self, tag="l"):
        self.uid += 1
        return f"{tag}{self.uid}"

    def _chan(self, name):
        if name not in self.sems:
            self.sems[name] = self.st.enter_context(self.nc.semaphore("sem_" + name))
            self.counts[name] = 0
        return name

    def op(self, queue, fn, waits=(), signal=None, out_store=False):
        if out_store:
            return self.dma("out", "sync", fn, waits=waits, signal=signal)
        assert queue in self.ENG, queue
        if signal is False:
            self.ops.append((queue, fn, tuple(w for w in waits if w), None, queue))
            return None
        if signal is None:
            signal = self.lab(queue)
        self.counts[queue] += 1
        assert signal not in self.labels, signal
        self.labels[signal] = (queue, self.counts[queue])
        self.ops.append((queue, fn, tuple(w for w in waits if w), signal, queue))
        return signal

    def dma(self, channel, queue, fn, waits=(), signal=None, serialize=False):
        assert queue in ("sync", "gsdma")
        ch = self._chan(channel)
        if signal is None:
            signal = self.lab(channel)
        self.counts[ch] += 16
        assert signal not in self.labels, signal
        self.labels[signal] = (ch, self.counts[ch])
        self.ops.append((queue, fn, tuple(w for w in waits if w), signal, ch))
        if serialize:
            self.raw_wait(queue, signal)
        return signal

    def chain(self, queue, fns, waits=()):
        lb = None
        for i, fn in enumerate(fns):
            w = list(waits) if i == 0 else [lb]
            lb = self.op(queue, fn, waits=w)
        return lb

    def raw_wait(self, queue, label):
        self.ops.append((queue, None, (label,), None, None))

    def emit_segment(self, name):
        per_q = {q: [] for q in self.QUEUES}
        for rec in self.ops:
            per_q[rec[0]].append(rec)
        self.ops = []
        active = [q for q in self.QUEUES if per_q[q]]

        with self.nc.Block(name) as blk:
            for q in active:
                def mk(qq):
                    def body(eng):
                        for (_, fn, waits, signal, semkey) in per_q[qq]:
                            for labw in waits:
                                pq, cnt = self.labels[labw]
                                key = (qq, pq)
                                if self.watermarks.get(key, 0) < cnt:
                                    eng.wait_ge(self.sems[pq], cnt)
                                    self.watermarks[key] = cnt
                            if fn is None:
                                continue
                            instr = fn(eng)
                            if signal is not None:
                                instr.then_inc(self.sems[semkey],
                                               16 if semkey not in self.ENG else 1)
                    return body
                getattr(blk, "gpsimd" if q == "gsdma" else q)(mk(q))
        for q in self.QUEUES:
            for pq in list(self.counts):
                self.watermarks[(q, pq)] = self.counts[pq]


def build_nc(debug=False):
    nc = bass.Bass("TRN2", target_bir_lowering=False)
    dt = nc.dram_tensor

    x_roll = dt("x_roll", [SEQ, D], F32, kind="ExternalInput").ap()
    gamma = dt("gamma", [1, D], F32, kind="ExternalInput").ap()
    wq = dt("wq", [D, D], F32, kind="ExternalInput").ap()
    wkv = dt("wkv", [D, 2 * D], F32, kind="ExternalInput").ap()
    wbt2 = dt("wbt2", [128, H * DH], F32, kind="ExternalInput").ap()
    wo = dt("wo", [128, 8 * D], F32, kind="ExternalInput").ap()
    w1t = dt("w1t", [NEXP, 128, 8 * HID], F32, kind="ExternalInput").ap()
    w2t = dt("w2t", [NEXP, 128, 16 * D], F32, kind="ExternalInput").ap()
    b1c = dt("b1c", [128, NEXP * (HID // 128)], F32, kind="ExternalInput").ap()
    b2 = dt("b2", [NEXP, D], F32, kind="ExternalInput").ap()
    wgt = dt("wgt", [128, 8 * NEXP], F32, kind="ExternalInput").ap()
    bg = dt("bg", [1, NEXP], F32, kind="ExternalInput").ap()
    costab = dt("costab", [128, SEQ], BF16, kind="ExternalInput").ap()
    sintab = dt("sintab", [128, SEQ], BF16, kind="ExternalInput").ap()
    identb = dt("identb", [128, 128], BF16, kind="ExternalInput").ap()
    rotmat = dt("rotmat", [128, 128], BF16, kind="ExternalInput").ap()
    ones128b = dt("ones128b", [128, 128], BF16, kind="ExternalInput").ap()
    onescol_b = dt("onescol_b", [128, 1], BF16, kind="ExternalInput").ap()
    onesrow_f = dt("onesrow_f", [1, 128], F32, kind="ExternalInput").ap()
    sel2 = dt("sel2", [2, 128], BF16, kind="ExternalInput").ap()
    identf = dt("identf", [128, 128], F32, kind="ExternalInput").ap()

    out = dt("out", [OWN, D], F32, kind="ExternalOutput").ap()
    dbg = {}
    if debug:
        for nm, shp, dtp in [
            ("kT", [128, 8 * OWN], BF16), ("qT", [128, 8 * OWN], BF16),
            ("v", [128, 16 * VTW], BF16), ("aoT", [128, 8 * OWN], BF16),
            ("res1", [OWN, D], F32), ("comb", [128, 128], F32),
            ("kTA", [128, 8 * OWN], BF16), ("vA", [128, 16 * VTW], BF16),
            ("xnT", [128, 8 * CH], BF16),
            ("sums", [128, 256], F32), ("ln2T", [128, 8 * OWN], BF16),
            ("bias", [1, 32], F32),
        ]:
            dbg[nm] = dt("dbg_" + nm, shp, dtp, kind="ExternalOutput").ap()

    with ExitStack() as st:
        P = Prog(nc, st)
        A = Arena(nc, st)
        pb = [st.enter_context(nc.psum_tensor(f"pb{i}", [128, 512], F32)) for i in range(8)]
        pbT = pb[7][:].bitcast(BF16)
        L = P.lab
        bank_owner = {}

        def dma(dst, src):
            return lambda e: e.dma_start(out=dst, in_=src)

        c_ident = A.bf16(128)
        c_rot = A.bf16(128)
        c_ones128b = A.bf16(128)
        c_onescol = A.bf16(1)
        c_onesrow = A.f32(128, rows=1)
        c_wbt = A.bf16(H * DH)
        c_b1 = A.f32(NEXP * (HID // 128))
        c_b2 = A.bf16(D, rows=NEXP)
        c_wg = A.bf16(8 * NEXP)
        c_bg = A.f32(NEXP, rows=1)
        c_sel2 = A.bf16(128, rows=2)
        c_identf = A.f32(128)
        s_bias = A.f32(32)
        s_bv = A.f32(32, rows=1)
        s_mx = A.f32(32, rows=1)

        A.mark()
        a_x = [A.f32(D) for _ in range(2)]
        a_ln1 = A.f32(D)
        a_xn = [A.bf16(D) for _ in range(2)]
        a_st = A.f32(16)
        a_mv = A.f32(4)
        a_rs = A.f32(8)
        a_xnT = A.bf16(8 * CH)
        a_wstg = [A.bf16(4 * D) for _ in range(2)]
        a_scr = A.f32(2048)
        a_kraw = [a_scr[:, 0:256].bitcast(BF16), a_scr[:, 256:512].bitcast(BF16)]
        a_rt1 = [a_scr[:, 512:768].bitcast(BF16), a_scr[:, 768:1024].bitcast(BF16)]
        a_rt2 = [a_scr[:, 1024:1280].bitcast(BF16), a_scr[:, 1280:1536].bitcast(BF16)]
        a_qhat = a_scr[:, 1536:1792].bitcast(BF16)
        a_sq = a_scr[:, 1792:2048].bitcast(BF16)
        a_gbc = A.bf16(D)
        a_kT = A.bf16(8 * OWN)
        a_v = A.bf16(16 * VTW + 64)
        s_r64 = a_scr[:, 0:1024]
        a_qT = A.bf16(8 * OWN)
        a_aoT = A.bf16(8 * OWN)
        a_pt = [A.bf16(512) for _ in range(4)]
        a_cos = A.bf16(OWN)
        a_sin = A.bf16(OWN)
        s_sumsT = A.f32(256)
        s_sumb16 = A.bf16(512, rows=2)
        assert A.off <= SBUF_BYTES, A.off

        l_ident = P.dma("cs", "sync", dma(c_ident, identb[:]), serialize=True)
        l_rot = P.dma("cs", "sync", dma(c_rot, rotmat[:]), serialize=True)
        l_o128 = P.dma("cs", "sync", dma(c_ones128b, ones128b[:]), serialize=True)
        l_ocol = P.dma("cs", "sync", dma(c_onescol, onescol_b[:]), serialize=True)
        l_orow = P.dma("cs", "sync", dma(c_onesrow, onesrow_f[:]), serialize=True)
        l_bg = P.dma("cs", "sync", dma(c_bg, bg[:]), serialize=True)
        l_b1 = P.dma("cs", "sync", dma(c_b1, b1c[:]), serialize=True)
        l_gam = P.dma("cs", "sync", dma(a_x[0][0:1, 0:D], gamma[:]), serialize=True)
        l_wbt = P.dma("cg", "gsdma", dma(c_wbt, wbt2[:]), serialize=True)
        l_b2 = P.dma("cg", "gsdma", dma(c_b2, b2[:]), serialize=True)
        l_wg = P.dma("cg", "gsdma", dma(c_wg, wgt[:]), serialize=True)
        l_sel2 = P.dma("cg", "gsdma", dma(c_sel2, sel2[:]), serialize=True)
        l_idf = P.dma("cg", "gsdma", dma(c_identf, identf[:]), serialize=True)
        P.op("scalar", lambda e: e.activation(out=c_wbt, in_=c_wbt, func=AF.Copy, scale=0.125),
             waits=[l_wbt], signal="wbt_ok")
        for i in range(2):
            P.op("tensor", (lambda i: lambda e: e.matmul(
                pb[i][:, :], c_onesrow, a_x[0][0:1, i * 512:(i + 1) * 512],
                start=True, stop=True))(i),
                waits=[l_orow, l_gam], signal=f"gbcmm{i}")
            P.op("vector", (lambda i: lambda e: e.tensor_copy(
                a_gbc[:, i * 512:(i + 1) * 512], pb[i][:, :]))(i),
                waits=[f"gbcmm{i}"], signal=f"gbc{i}")
        P.op("vector", lambda e: e.memset(s_mx[0:1, :], -1e30), signal="mxinit")
        for tt in range(16):
            P.op("vector", (lambda tt: lambda e: e.memset(
                a_v[:, tt * VTW:(tt + 1) * VTW].rearrange(
                    "p (h u) -> p h u", u=DH + 1)[:, :, DH:DH + 1], 1.0))(tt),
                signal=f"vone{tt}")
        P.op("vector", lambda e: e.memset(a_v[:, 16 * VTW:16 * VTW + 64], 1.0),
             signal="vpad")
        P.op("scalar", lambda e: e.preload_act_table(AF.Sqrt), signal=False)
        P.emit_segment("s0")

        def ln_block(src, dst, deps, tag, parity, gamma_mul):
            rsb = parity * 4
            l1 = P.chain("vector", [
                lambda e: (e.bn_stats(a_st[:, 0:6], src[:, 0:512]),
                           e.bn_stats(a_st[:, 6:12], src[:, 512:1024]))[-1],
                lambda e: e.bn_aggr(a_mv[:, 0:2], a_st[:, 0:12]),
                lambda e: e.tensor_scalar(out=a_rs[:, rsb:rsb + 1], in0=a_mv[:, 1:2],
                                          scalar1=EPS, scalar2=None, op0=ALU.add),
                lambda e: e.reciprocal(a_rs[:, rsb:rsb + 1], a_rs[:, rsb:rsb + 1]),
            ], waits=deps)
            l2 = P.op("scalar", lambda e: e.activation(
                out=a_rs[:, rsb + 1:rsb + 2], in_=a_rs[:, rsb:rsb + 1], func=AF.Sqrt),
                waits=[l1])
            l2b = P.op("vector", lambda e: e.tensor_scalar(
                out=a_rs[:, rsb + 2:rsb + 3], in0=a_mv[:, 0:1],
                scalar1=a_rs[:, rsb + 1:rsb + 2], scalar2=-1.0,
                op0=ALU.mult, op1=ALU.mult), waits=[l2])
            l2c = P.op("scalar", lambda e: e.activation(
                out=a_ln1[:], in_=src, func=AF.Identity,
                scale=a_rs[:, rsb + 1:rsb + 2], bias=a_rs[:, rsb + 2:rsb + 3]),
                waits=[l2b])
            l3 = P.chain("vector", [
                lambda e: (e.bn_stats(a_st[:, 0:6], a_ln1[:, 0:512]),
                           e.bn_stats(a_st[:, 6:12], a_ln1[:, 512:1024]))[-1],
                lambda e: e.bn_aggr(a_mv[:, 2:4], a_st[:, 0:12]),
                lambda e: e.tensor_scalar(out=a_rs[:, rsb:rsb + 1], in0=a_mv[:, 3:4],
                                          scalar1=EPS, scalar2=None, op0=ALU.add),
                lambda e: e.reciprocal(a_rs[:, rsb:rsb + 1], a_rs[:, rsb:rsb + 1]),
            ], waits=[l2c])
            l4 = P.op("scalar", lambda e: e.activation(
                out=a_rs[:, rsb + 1:rsb + 2], in_=a_rs[:, rsb:rsb + 1], func=AF.Sqrt),
                waits=[l3])
            l4b = P.op("vector", lambda e: e.tensor_scalar(
                out=a_rs[:, rsb + 2:rsb + 3], in0=a_mv[:, 2:3],
                scalar1=a_rs[:, rsb + 1:rsb + 2], scalar2=-1.0,
                op0=ALU.mult, op1=ALU.mult), waits=[l4])
            l4c = P.op("scalar", lambda e: e.activation(
                out=dst, in_=a_ln1[:], func=AF.Identity,
                scale=a_rs[:, rsb + 1:rsb + 2], bias=a_rs[:, rsb + 2:rsb + 3]),
                waits=[l4b] + (gamma_mul or []))
            l5 = P.op("vector", lambda e: e.tensor_tensor(
                out=dst, in0=dst, in1=a_gbc, op=ALU.mult),
                waits=[l4c, "gbc0", "gbc1"])
            return l5

        kraw_free = [None, None]

        def rope(bank, toksl, dst_ap, buf, tag, l_cos, l_sin, dst_free=None):
            kraw, rt1, rt2 = a_kraw[buf], a_rt1[buf], a_rt2[buf]
            wev = [bank_owner.get(bank)] + (kraw_free[buf] or [])
            l_ev = P.op("scalar", lambda e: e.activation(out=kraw, in_=pb[bank][:, :],
                                                         func=AF.Copy),
                        waits=wev, signal=L(tag))
            l_ro = P.op("tensor", lambda e: e.matmul(pb[bank][:, :], c_rot, kraw,
                                                     start=True, stop=True),
                        waits=[l_ev, l_rot], signal=L(tag))
            l_t1 = P.op("vector", lambda e: e.tensor_tensor(
                out=rt1, in0=kraw, in1=a_cos[:, toksl], op=ALU.mult),
                waits=[l_ev, l_cos], signal=L(tag))
            l_t2 = P.op("vector", lambda e: e.tensor_tensor(
                out=rt2, in0=pb[bank][:, :], in1=a_sin[:, toksl], op=ALU.mult),
                waits=[l_ro, l_t1, l_sin], signal=L(tag))
            l_dst = P.op("vector", lambda e: e.tensor_tensor(
                out=dst_ap, in0=rt1, in1=rt2, op=ALU.add),
                waits=[l_t2] + ([dst_free] if dst_free else []), signal=L(tag))
            kraw_free[buf] = [l_ro, l_t1]
            bank_owner[bank] = l_t2
            return l_dst

        def warm_burst(n=24):
            def f(e):
                r = None
                for _ in range(n):
                    r = e.matmul(pb[7][0:1, :], c_ones128b[0:1, 0:1],
                                 c_b2[0:1, 0:512], start=True, stop=True)
                return r
            P.op("tensor", f, signal=False)

        prev_sq = [None]
        mx_last = [None]

        def normsq(src_ap, bank, mslot, deps, tag):
            l_sq = P.op("vector", lambda e: e.tensor_tensor(
                out=a_sq, in0=src_ap, in1=src_ap, op=ALU.mult),
                waits=list(deps) + ([prev_sq[0]] if prev_sq[0] else []), signal=L(tag))
            l_mm = P.op("tensor", lambda e: e.matmul(pb[bank][0:1, :], c_onescol, a_sq,
                                                     start=True, stop=True),
                        waits=[l_sq, l_ocol, bank_owner.get(bank)], signal=L(tag))
            prev_sq[0] = l_mm
            l_mx = P.chain("vector", [
                lambda e: e.tensor_reduce(s_mx[0:1, 31:32], pb[bank][0:1, :],
                                          axis=AX.X, op=ALU.max),
                lambda e: e.tensor_tensor(out=s_mx[0:1, mslot:mslot + 1],
                                          in0=s_mx[0:1, mslot:mslot + 1],
                                          in1=s_mx[0:1, 31:32], op=ALU.max),
            ], waits=[l_mm, "mxinit"])
            bank_owner[bank] = l_mx
            mx_last[0] = l_mx
            return l_mx

        def proj_pass(pass_id):
            tag = f"P{pass_id}"
            base = pass_id * OWN
            if pass_id == 1:
                P.op("scalar", lambda e: e.preload_act_table(AF.Sqrt), signal=False)
            l_cos = P.dma("tcos", "sync", dma(a_cos, costab[:, base:base + OWN]))
            l_sin = P.dma("tsin", "sync", dma(a_sin, sintab[:, base:base + OWN]))
            warm_burst()
            wstg_free = [None, None]
            x_free = [None, None]
            xn_free = [None, None]
            xnT_free = None
            qhat_free = None
            for tc in range(NCH):
                xn_labels = []
                tr_labels = []
                for t4 in range(4):
                    tile = tc * 4 + t4
                    xb = a_x[t4 % 2]
                    l_x = P.dma(f"x{t4 % 2}", "sync",
                                dma(xb[:], x_roll[base + tile * 128: base + tile * 128 + 128, :]),
                                waits=[x_free[t4 % 2]])
                    xn = a_xn[t4 % 2]
                    l_xn = ln_block(xb[:], xn[:], [l_x], tag, t4 % 2,
                                    gamma_mul=[xn_free[t4 % 2]] if xn_free[t4 % 2] else [])
                    x_free[t4 % 2] = l_xn
                    wtr = [l_xn, l_ident, bank_owner.get(7)]
                    l_tr = P.op("tensor", (lambda xn: lambda e: [
                        e.transpose(pbT[:, d * 128:(d + 1) * 128],
                                    xn[:, d * 128:(d + 1) * 128], c_ident)
                        for d in range(8)][-1])(xn),
                        waits=wtr, signal=L(tag))
                    bank_owner[7] = l_tr
                    xn_free[t4 % 2] = l_tr
                    tr_labels.append(l_tr)
                    l_ev = P.op("vector", (lambda t4: lambda e: e.tensor_copy(
                        a_xnT.rearrange("p (d c) -> p d c", d=8)[:, :, t4 * 128:(t4 + 1) * 128],
                        pbT[:].rearrange("p (d c) -> p d c", d=8)))(t4),
                        waits=[l_tr] + ([xnT_free] if (xnT_free and t4 == 0) else []),
                        signal=L(tag))
                    bank_owner[7] = l_ev
                    xn_labels.append(l_ev)

                def wload(which, half):
                    sbuf = a_wstg[half]
                    r0 = half * 512
                    if which == "k":
                        src = wkv[r0:r0 + 512, 0:D]
                    elif which == "v":
                        src = wkv[r0:r0 + 512, D:2 * D]
                    else:
                        src = wq[r0:r0 + 512, 0:D]
                    src = src.rearrange("(c p) f -> p c f", p=128)
                    dst = sbuf.rearrange("p (c f) -> p c f", f=D)
                    l_w = P.dma(f"ws{half}", "gsdma", dma(dst, src),
                                waits=[wstg_free[half]])
                    return sbuf, l_w

                def proj_mm(which, d, dd, sbuf, l_w, first_waits):
                    if which == "k" or which == "q":
                        def f(e, sbuf=sbuf, d=d, dd=dd):
                            r = None
                            for kf in range(8):
                                r = e.matmul(pb[kf][:, :],
                                             sbuf[:, dd * D + kf * 128: dd * D + (kf + 1) * 128],
                                             a_xnT[:, d * CH:(d + 1) * CH],
                                             start=(d == 0), stop=(d == 7))
                            return r
                    else:
                        def f(e, sbuf=sbuf, d=d, dd=dd):
                            r = None
                            for tt in range(4):
                                for vh in range(2):
                                    r = e.matmul(pb[tt * 2 + vh][:, :],
                                                 a_xnT[:, d * CH + tt * 128: d * CH + (tt + 1) * 128],
                                                 sbuf[:, dd * D + vh * 512: dd * D + (vh + 1) * 512],
                                                 start=(d == 0), stop=(d == 7))
                            return r
                    return P.op("tensor", f, waits=[l_w] + first_waits, signal=L(tag))

                def proj_sweep(which):
                    last_mm = None
                    for half in range(2):
                        sbuf, l_w = wload(which, half)
                        for dd in range(4):
                            d = half * 4 + dd
                            fw = []
                            if d == 0:
                                fw = xn_labels + [bank_owner.get(b) for b in range(8)]
                            if last_mm:
                                fw = fw + [last_mm]
                            last_mm = proj_mm(which, d, dd, sbuf, l_w, fw)
                        wstg_free[half] = last_mm
                    for b in range(8):
                        bank_owner[b] = last_mm
                    return last_mm

                toksl = slice(tc * CH, (tc + 1) * CH)

                last_mm = proj_sweep("k")
                for kf in range(8):
                    dst = a_kT[:, kf * OWN + tc * CH: kf * OWN + (tc + 1) * CH]
                    l_dst = rope(kf, toksl, dst, kf % 2, tag, l_cos, l_sin)
                    normsq(dst, kf, 8 + 8 * pass_id + kf, [l_dst], tag)

                last_mm = proj_sweep("v")
                for tt in range(4):
                    for vh in range(2):
                        b = tt * 2 + vh
                        vtile = tc * 4 + tt
                        l_ev = P.op("vector", (lambda b, vtile, vh: lambda e: e.tensor_copy(
                            a_v[:, vtile * VTW + vh * 8 * (DH + 1):
                                vtile * VTW + (vh + 1) * 8 * (DH + 1)].rearrange(
                                "p (h u) -> p h u", u=DH + 1)[:, :, 0:DH],
                            pb[b][:].rearrange("p (h d) -> p h d", d=DH)))(b, vtile, vh),
                            waits=[bank_owner[b]], signal=L(tag))
                        bank_owner[b] = l_ev

                if pass_id == 0:
                    last_mm = proj_sweep("q")
                    for qf in range(8):
                        l_dst = rope(qf, toksl, a_qhat, qf % 2, tag, l_cos, l_sin,
                                     dst_free=qhat_free)
                        h0, h1 = 2 * qf, 2 * qf + 1
                        l_bl = P.op("tensor", (lambda qf, h0, h1: lambda e: (
                            e.matmul(pb[qf][0:64, :], c_wbt[0:64, h0 * 64:(h0 + 1) * 64],
                                     a_qhat[0:64, :], start=True, stop=True),
                            e.matmul(pb[qf][64:128, :], c_wbt[64:128, h1 * 64:(h1 + 1) * 64],
                                     a_qhat[64:128, :], start=True, stop=True))[-1])(qf, h0, h1),
                            waits=[l_dst, "wbt_ok", bank_owner.get(qf)], signal=L(tag))
                        qhat_free = l_bl
                        bank_owner[qf] = l_bl
                        dstq = a_qT[:, qf * OWN + tc * CH: qf * OWN + (tc + 1) * CH]
                        l_ev = P.op("vector", (lambda dstq, qf: lambda e: e.tensor_copy(
                            dstq, pb[qf][:, :]))(dstq, qf),
                            waits=[l_bl], signal=L(tag))
                        bank_owner[qf] = l_ev
                        normsq(dstq, qf, qf, [l_ev], tag)
                xnT_free = last_mm

        def bias_build(pass_id):
            tag = f"B{pass_id}"
            if pass_id == 0:
                l1 = P.op("vector", lambda e: e.tensor_tensor(
                    out=s_bv[0:1, 0:8], in0=s_mx[0:1, 0:8], in1=s_mx[0:1, 8:16], op=ALU.mult),
                    waits=[mx_last[0]], signal=L(tag))
                l2 = P.op("scalar", lambda e: e.activation(
                    out=s_bv[0:1, 0:8], in_=s_bv[0:1, 0:8], func=AF.Sqrt),
                    waits=[l1], signal=L(tag))
                l3 = P.op("vector", lambda e: e.tensor_scalar(
                    out=s_bv[0:1, 8:16], in0=s_bv[0:1, 0:8], scalar1=-1.0, scalar2=SHIFT,
                    op0=ALU.mult, op1=ALU.add), waits=[l2], signal=L(tag))
                lm = P.op("tensor", lambda e: e.matmul(
                    pb[0][:, 0:8], c_onesrow, s_bv[0:1, 8:16], start=True, stop=True),
                    waits=[l3, bank_owner.get(0)], signal=L(tag))
                bank_owner[0] = lm
                le = P.op("vector", lambda e: e.tensor_copy(s_bias[:, 0:8], pb[0][:, 0:8]),
                          waits=[lm], signal=L(tag))
                bank_owner[0] = le
                return le
            l1 = P.op("vector", lambda e: e.tensor_tensor(
                out=s_bv[0:1, 16:24], in0=s_mx[0:1, 0:8], in1=s_mx[0:1, 16:24], op=ALU.mult),
                waits=[mx_last[0]], signal=L(tag))
            l2 = P.op("scalar", lambda e: e.activation(
                out=s_bv[0:1, 16:24], in_=s_bv[0:1, 16:24], func=AF.Sqrt),
                waits=[l1], signal=L(tag))
            l3 = P.chain("vector", [
                lambda e: e.tensor_tensor(out=s_bv[0:1, 16:24], in0=s_bv[0:1, 16:24],
                                          in1=s_bv[0:1, 0:8], op=ALU.max),
                lambda e: e.tensor_scalar(out=s_bv[0:1, 24:32], in0=s_bv[0:1, 16:24],
                                          scalar1=-1.0, scalar2=SHIFT,
                                          op0=ALU.mult, op1=ALU.add),
                lambda e: e.tensor_tensor(out=s_bv[0:1, 16:24], in0=s_bv[0:1, 24:32],
                                          in1=s_bv[0:1, 8:16], op=ALU.subtract),
            ], waits=[l2])
            P.op("scalar", lambda e: e.preload_act_table(AF.Exp), waits=[l3],
                 signal=False)
            l4 = P.op("scalar", lambda e: e.activation(
                out=s_bv[0:1, 16:24], in_=s_bv[0:1, 16:24], func=AF.Exp),
                waits=[l3], signal=L(tag))
            lm = P.op("tensor", lambda e: (
                e.matmul(pb[0][:, 0:8], c_onesrow, s_bv[0:1, 24:32], start=True, stop=True),
                e.matmul(pb[1][:, 0:8], c_onesrow, s_bv[0:1, 16:24], start=True, stop=True))[-1],
                waits=[l4, bank_owner.get(0), bank_owner.get(1)], signal=L(tag))
            le = P.op("vector", lambda e: (
                e.tensor_copy(s_bias[:, 8:16], pb[0][:, 0:8]),
                e.tensor_copy(s_bias[:, 16:24], pb[1][:, 0:8]))[-1],
                waits=[lm], signal=L(tag))
            bank_owner[0] = le
            bank_owner[1] = le
            return le

        sums_last = [None]

        def sim_pass(pass_id, l_bias):
            tag = f"S{pass_id}"
            P.op("scalar", lambda e: e.preload_act_table(AF.Exp), waits=[l_bias],
                 signal=False)
            biascol = 0 if pass_id == 0 else 8
            warm_burst()
            for pr in range(NPAIR):
                h0, h1 = 2 * pr, 2 * pr + 1
                for qc in range(4):
                    qsl = slice(pr * OWN + qc * 512, pr * OWN + (qc + 1) * 512)
                    av_done = {}
                    exp_done = {}
                    def emit_sims(js):
                        for hp in (0, 1):
                            sb = hp * 2 + js % 2
                            ksl = slice(pr * OWN + js * 128, pr * OWN + (js + 1) * 128)
                            l_sim = P.op("tensor", (lambda sb, hp, ksl, qsl: lambda e: e.matmul(
                                pb[sb][:, :],
                                a_kT[hp * 64:(hp + 1) * 64, ksl],
                                a_qT[hp * 64:(hp + 1) * 64, qsl],
                                start=True, stop=True))(sb, hp, ksl, qsl),
                                waits=[bank_owner.get(sb)], signal=L(tag))
                            bank_owner[sb] = l_sim
                    def emit_exps(js):
                        for hp in (0, 1):
                            sb = hp * 2 + js % 2
                            pt = a_pt[hp * 2 + js % 2]
                            l_exp = P.op("scalar", (lambda sb, pt, pr: lambda e: e.activation(
                                out=pt, in_=pb[sb][:, :], func=AF.Exp,
                                bias=s_bias[:, biascol + pr: biascol + pr + 1]))(sb, pt, pr),
                                waits=[bank_owner[sb], av_done.get((hp, js - 2))],
                                signal=L(tag))
                            bank_owner[sb] = l_exp
                            exp_done[(hp, js)] = l_exp
                    def emit_avs(js):
                        for hp, h in ((0, h0), (1, h1)):
                            ob = 4 + hp
                            pt = a_pt[hp * 2 + js % 2]
                            wav = [exp_done[(hp, js)]]
                            if js == 0:
                                wav += [bank_owner.get(ob)]
                            l_av = P.op("tensor", (lambda ob, h, pt, js: lambda e:
                                e.matmul(pb[ob][0:128, :],
                                         a_v[:, js * VTW + h * (DH + 1):
                                             js * VTW + h * (DH + 1) + 128],
                                         pt, start=(js == 0), stop=(js == 15)))(
                                             ob, h, pt, js),
                                waits=wav, signal=L(tag))
                            av_done[(hp, js)] = l_av
                            if js == 15:
                                bank_owner[ob] = l_av
                    for js in range(16):
                        emit_sims(js)
                        emit_exps(js)
                        if js >= 1:
                            emit_avs(js - 1)
                    emit_avs(15)
                    l_e16 = P.op("vector", lambda e: (
                        e.tensor_copy(s_r64[64:65, 0:512], pb[4][64:65, :]),
                        e.tensor_copy(s_r64[64:65, 512:1024], pb[5][64:65, :]))[-1],
                        waits=[bank_owner[4], bank_owner[5], bank_owner.get(6)],
                        signal=L(tag))
                    bank_owner[4] = l_e16
                    bank_owner[5] = l_e16
                    l_tr = P.op("tensor", (lambda pr: lambda e: [
                        e.transpose(pb[6][:, qs * 16 + 2 * pr + hh: qs * 16 + 2 * pr + hh + 1],
                                    s_r64[64:65, hh * 512 + qs * 128: hh * 512 + (qs + 1) * 128],
                                    c_identf[64:65, 64:65])
                        for hh in range(2) for qs in range(4)][-1])(pr),
                        waits=[l_e16, l_idf, bank_owner[6]], signal=L(tag))
                    bank_owner[6] = l_tr
                    for hp, h in ((0, h0), (1, h1)):
                        ob = 4 + hp
                        asl = slice(pr * OWN + qc * 512, pr * OWN + (qc + 1) * 512)
                        dsta = a_aoT[hp * 64:hp * 64 + 64, asl]
                        if pass_id == 0:
                            l1 = P.op("vector", (lambda dsta, ob: lambda e: e.tensor_copy(
                                dsta, pb[ob][0:64, :]))(dsta, ob),
                                waits=[bank_owner[ob]], signal=L(tag))
                        else:
                            l1 = P.op("vector", (lambda dsta, ob, hp, pr: lambda e:
                                e.scalar_tensor_tensor(
                                    out=dsta, in0=dsta,
                                    scalar=s_bias[hp * 64:hp * 64 + 64, 16 + pr:17 + pr],
                                    in1=pb[ob][0:64, :], op0=ALU.mult, op1=ALU.add))(
                                        dsta, ob, hp, pr),
                                waits=[bank_owner[ob]], signal=L(tag))
                        bank_owner[ob] = l1
                    sTdst = s_sumsT.rearrange("p (s h) -> p s h", h=16)[
                        :, qc * 4:(qc + 1) * 4, 2 * pr:2 * pr + 2]
                    sTsrc = pb[6][:, 0:64].rearrange("p (s h) -> p s h", h=16)[
                        :, :, 2 * pr:2 * pr + 2]
                    if pass_id == 0:
                        l_sc = P.op("vector", (lambda sTdst, sTsrc: lambda e: e.tensor_copy(
                            sTdst, sTsrc))(sTdst, sTsrc),
                            waits=[l_tr], signal=L(tag))
                    else:
                        l_sc = P.op("vector", (lambda sTdst, sTsrc, pr: lambda e:
                            e.scalar_tensor_tensor(
                                out=sTdst, in0=sTdst,
                                scalar=s_bias[:, 16 + pr:17 + pr],
                                in1=sTsrc, op0=ALU.mult, op1=ALU.add))(sTdst, sTsrc, pr),
                            waits=[l_tr], signal=L(tag))
                    bank_owner[6] = l_sc
                    sums_last[0] = l_sc

        def attn_normalize():
            tag = "N"
            l_r = P.op("vector", lambda e: e.reciprocal(s_sumsT, s_sumsT),
                       waits=[sums_last[0]])
            for pr in range(NPAIR):
                for qc in range(4):
                    b = (pr * 4 + qc) % 4
                    asl = slice(pr * OWN + qc * 512, pr * OWN + (qc + 1) * 512)
                    src_r = s_sumsT.rearrange("p (s h) -> p s h", h=16)[
                        :, qc * 4:(qc + 1) * 4, 2 * pr:2 * pr + 2]
                    l_t = P.op("tensor", (lambda b, pr, qc: lambda e: [
                        e.transpose(pb[b][0:2, qs * 128:(qs + 1) * 128],
                                    s_sumsT[:, (qc * 4 + qs) * 16 + 2 * pr:
                                            (qc * 4 + qs) * 16 + 2 * pr + 2],
                                    c_identf)
                        for qs in range(4)][-1])(b, pr, qc),
                        waits=[l_r, l_idf, bank_owner.get(b)], signal=L(tag))
                    bank_owner[b] = l_t
                    l_c = P.op("vector", (lambda b: lambda e: e.tensor_copy(
                        s_sumb16[0:2, :], pb[b][0:2, :]))(b),
                        waits=[l_t], signal=L(tag))
                    bank_owner[b] = l_c
                    lm = P.op("tensor", (lambda b: lambda e: e.matmul(
                        pb[b][:, :], c_sel2[0:2, :], s_sumb16[0:2, :],
                        start=True, stop=True))(b),
                        waits=[l_c, l_sel2], signal=L(tag))
                    bank_owner[b] = lm
                    lt = P.op("vector", (lambda b, asl: lambda e: e.tensor_tensor(
                        out=a_aoT[:, asl], in0=a_aoT[:, asl], in1=pb[b][:, :],
                        op=ALU.mult))(b, asl),
                        waits=[lm], signal=L(tag))
                    bank_owner[b] = lt
            return lt

        import os as _os
        if not _os.environ.get("SKIP_A"):
            proj_pass(0)
            l_b = bias_build(0)
            if not _os.environ.get("SKIP_SIMA"):
                sim_pass(0, l_b)
        else:
            l_b = bias_build(0)
        if debug:
            P.op("sync", dma(dbg["kTA"][:], a_kT), waits=[l_b, sums_last[0]], out_store=True)
            ldbgA = P.op("sync", dma(dbg["vA"][:], a_v[:, 0:16 * VTW]),
                         waits=[l_b, sums_last[0]], out_store=True)
            P.raw_wait("sync", ldbgA)
        P.emit_segment("attnA")
        import os as _os
        proj_pass(1)
        l_b = bias_build(1)
        if not _os.environ.get("SKIP_SIMB"):
            sim_pass(1, l_b)
            l_fin = attn_normalize()
        else:
            l_fin = l_b
        if debug:
            P.op("sync", dma(dbg["kT"][:], a_kT), waits=[l_fin], out_store=True)
            P.op("sync", dma(dbg["qT"][:], a_qT), waits=[l_fin], out_store=True)
            P.op("sync", dma(dbg["v"][:], a_v[:, 0:16 * VTW]), waits=[l_fin], out_store=True)
            P.op("sync", dma(dbg["aoT"][:], a_aoT), waits=[l_fin], out_store=True)
            P.op("sync", dma(dbg["sums"][:], s_sumsT), waits=[l_fin], out_store=True)
            P.op("sync", dma(dbg["xnT"][:], a_xnT), waits=[l_fin], out_store=True)
            ldbg = P.op("sync", dma(dbg["bias"][:], s_bv[0:1, 0:32]), waits=[l_fin],
                        out_store=True)
            P.raw_wait("sync", ldbg)
        P.emit_segment("attnB")

        A.reset()
        A.mark()
        r_res = [A.f32(D) for _ in range(16)]
        f_w1 = A.bf16(8 * HID)
        r_wo = f_w1.bitcast(BF16)[:, 0:8 * D] if False else None
        r_wo = f_w1[:, 0:8 * D]
        f_ln2T = A.bf16(8 * OWN)
        f_w2 = A.bf16(16 * D)
        f_hT = A.bf16(16 * 512)
        f_comb = A.f32(128)
        f_combb = A.bf16(8)
        f_combT = A.bf16(OWN, rows=8)
        f_xn = A.bf16(D)
        f_g = A.f32(40)
        f_st = A.f32(16)
        f_mv = A.f32(4)
        f_rs = A.f32(8)
        assert A.off <= SBUF_BYTES, A.off

        bank_owner.clear()
        tag = "O"
        l_wo = P.dma("w1", "gsdma", dma(r_wo, wo[:]))
        res_labels = [None] * 16
        res_prev = [None] * 4
        warm_burst()
        for m in range(16):
            l_x = P.dma(f"rs{m % 4}", "sync",
                        dma(r_res[m][:], x_roll[m * 128:(m + 1) * 128, :]),
                        waits=[res_prev[m % 4]])
            for oh in range(2):
                b = (m * 2 + oh) % 4
                lm = P.op("tensor", (lambda m, oh, b: lambda e: [
                    e.matmul(pb[b][:, :],
                             a_aoT[:, c * OWN + m * 128: c * OWN + (m + 1) * 128],
                             r_wo[:, c * D + oh * 512: c * D + (oh + 1) * 512],
                             start=(c == 0), stop=(c == 7))
                    for c in range(8)][-1])(m, oh, b),
                    waits=[l_wo, bank_owner.get(b)], signal=L(tag))
                bank_owner[b] = lm
                le = P.op("vector", (lambda m, oh, b: lambda e: e.tensor_tensor(
                    out=r_res[m][:, oh * 512:(oh + 1) * 512],
                    in0=r_res[m][:, oh * 512:(oh + 1) * 512],
                    in1=pb[b][:, :], op=ALU.add))(m, oh, b),
                    waits=[lm, l_x], signal=L(tag))
                bank_owner[b] = le
                res_labels[m] = le
            res_prev[m % 4] = res_labels[m]
            oproj_last_mm = lm
        res_dump = [None] * 16
        if debug:
            for m in range(16):
                res_dump[m] = P.dma("dres", "sync",
                                    dma(dbg["res1"][m * 128:(m + 1) * 128, :], r_res[m][:]),
                                    waits=[res_labels[m]], serialize=True)

        tag = "F"
        ln_done = [None] * 16
        P.op("scalar", lambda e: e.preload_act_table(AF.Sqrt), signal=False)
        fxn_free = None
        for m in range(16):
            src = r_res[m][:]
            rsb = (m % 2) * 4
            l1 = P.chain("vector", [
                (lambda src: lambda e: (e.bn_stats(f_st[:, 0:6], src[:, 0:512]),
                                        e.bn_stats(f_st[:, 6:12], src[:, 512:1024]))[-1])(src),
                (lambda rsb: lambda e: e.bn_aggr(f_mv[:, 0:2], f_st[:, 0:12]))(rsb),
                (lambda rsb: lambda e: e.tensor_scalar(
                    out=f_rs[:, rsb:rsb + 1], in0=f_mv[:, 1:2], scalar1=EPS,
                    scalar2=None, op0=ALU.add))(rsb),
                (lambda rsb: lambda e: e.reciprocal(
                    f_rs[:, rsb:rsb + 1], f_rs[:, rsb:rsb + 1]))(rsb),
            ], waits=[res_labels[m]])
            l2 = P.op("scalar", (lambda rsb: lambda e: e.activation(
                out=f_rs[:, rsb + 1:rsb + 2], in_=f_rs[:, rsb:rsb + 1], func=AF.Sqrt))(rsb),
                waits=[l1], signal=L(tag))
            l3 = P.chain("vector", [
                (lambda rsb: lambda e: e.tensor_scalar(
                    out=f_rs[:, rsb + 2:rsb + 3], in0=f_mv[:, 0:1],
                    scalar1=f_rs[:, rsb + 1:rsb + 2], scalar2=-1.0,
                    op0=ALU.mult, op1=ALU.mult))(rsb),
                (lambda src, rsb: lambda e: e.tensor_scalar(
                    out=f_xn, in0=src, scalar1=f_rs[:, rsb + 1:rsb + 2],
                    scalar2=f_rs[:, rsb + 2:rsb + 3], op0=ALU.mult, op1=ALU.add))(src, rsb),
            ], waits=[l2] + ([fxn_free] if fxn_free else []))
            l_tr = P.op("tensor", lambda e: [
                e.transpose(pbT[:, d * 128:(d + 1) * 128],
                            f_xn[:, d * 128:(d + 1) * 128], c_ident)
                for d in range(8)][-1],
                waits=[l3, l_ident, bank_owner.get(7)], signal=L(tag))
            bank_owner[7] = l_tr
            fxn_free = l_tr
            l_ev = P.op("vector", (lambda m: lambda e: e.tensor_copy(
                f_ln2T.rearrange("p (d c) -> p d c", d=8)[:, :, m * 128:(m + 1) * 128],
                pbT[:].rearrange("p (d c) -> p d c", d=8)))(m),
                waits=[l_tr], signal=L(tag))
            bank_owner[7] = l_ev
            ln_done[m] = l_ev

        P.op("scalar", lambda e: e.preload_act_table(AF.Exp), signal=False)
        comb_ready = [None] * 16
        combT_ready = [None] * 16
        for m in range(16):
            b = m % 2
            lm = P.op("tensor", (lambda m, b: lambda e: (
                [e.matmul(pb[b][:, 0:8],
                          f_ln2T[:, d * OWN + m * 128: d * OWN + (m + 1) * 128],
                          c_wg[:, d * 8:(d + 1) * 8], start=(d == 0), stop=False)
                 for d in range(8)],
                e.matmul(pb[b][:, 0:8], c_onesrow, c_bg, start=False, stop=True))[1])(m, b),
                waits=[l_wg, l_bg, l_orow, ln_done[m], bank_owner.get(b)], signal=L(tag))
            bank_owner[b] = lm
            l_g1 = P.chain("vector", [
                (lambda b: lambda e: e.tensor_copy(f_g[:, 0:8], pb[b][:, 0:8]))(b),
                lambda e: e.tensor_reduce(f_g[:, 8:9], f_g[:, 0:8], axis=AX.X, op=ALU.max),
                lambda e: e.tensor_scalar(out=f_g[:, 16:24], in0=f_g[:, 0:8],
                                          scalar1=f_g[:, 8:9], scalar2=None,
                                          op0=ALU.is_equal),
                lambda e: e.scalar_tensor_tensor(out=f_g[:, 24:32], in0=f_g[:, 16:24],
                                                 scalar=-1e30, in1=f_g[:, 0:8],
                                                 op0=ALU.mult, op1=ALU.add),
                lambda e: e.tensor_reduce(f_g[:, 9:10], f_g[:, 24:32], axis=AX.X,
                                          op=ALU.max),
                lambda e: e.tensor_scalar(out=f_g[:, 10:11], in0=f_g[:, 8:9],
                                          scalar1=-1.0, scalar2=None, op0=ALU.mult),
            ], waits=[lm])
            bank_owner[b] = l_g1
            l_g2 = P.op("scalar", lambda e: e.activation(
                out=f_g[:, 11:12], in_=f_g[:, 9:10], func=AF.Exp, bias=f_g[:, 10:11]),
                waits=[l_g1], signal=L(tag))
            l_g3 = P.chain("vector", [
                lambda e: e.tensor_scalar(out=f_g[:, 12:13], in0=f_g[:, 11:12],
                                          scalar1=1.0, scalar2=None, op0=ALU.add),
                lambda e: e.reciprocal(f_g[:, 12:13], f_g[:, 12:13]),
                lambda e: e.tensor_tensor(out=f_g[:, 13:14], in0=f_g[:, 11:12],
                                          in1=f_g[:, 12:13], op=ALU.mult),
                lambda e: e.tensor_scalar(out=f_g[:, 24:32], in0=f_g[:, 0:8],
                                          scalar1=f_g[:, 9:10], scalar2=None,
                                          op0=ALU.is_equal),
                lambda e: e.tensor_scalar(out=f_g[:, 32:40], in0=f_g[:, 16:24],
                                          scalar1=-1.0, scalar2=1.0,
                                          op0=ALU.mult, op1=ALU.add),
                lambda e: e.tensor_tensor(out=f_g[:, 24:32], in0=f_g[:, 24:32],
                                          in1=f_g[:, 32:40], op=ALU.mult),
                lambda e: e.tensor_scalar(out=f_g[:, 16:24], in0=f_g[:, 16:24],
                                          scalar1=f_g[:, 12:13], scalar2=None,
                                          op0=ALU.mult),
                lambda e: e.tensor_scalar(out=f_g[:, 24:32], in0=f_g[:, 24:32],
                                          scalar1=f_g[:, 13:14], scalar2=None,
                                          op0=ALU.mult),
                (lambda m: lambda e: e.tensor_tensor(
                    out=f_comb[:, m * 8:(m + 1) * 8], in0=f_g[:, 16:24],
                    in1=f_g[:, 24:32], op=ALU.add))(m),
                (lambda m: lambda e: e.tensor_copy(
                    f_combb[:, 0:8], f_comb[:, m * 8:(m + 1) * 8]))(m),
            ], waits=[l_g2])
            comb_ready[m] = l_g3
            l_tc = P.op("tensor", lambda e: e.transpose(pbT[0:8, 0:128], f_combb[:, 0:8], c_ident),
                        waits=[l_g3, l_ident, bank_owner.get(7)], signal=L(tag))
            bank_owner[7] = l_tc
            l_te = P.op("vector", (lambda m: lambda e: e.tensor_copy(
                f_combT[0:8, m * 128:(m + 1) * 128], pbT[0:8, 0:128]))(m),
                waits=[l_tc], signal=L(tag))
            bank_owner[7] = l_te
            combT_ready[m] = l_te
        if debug:
            P.op("sync", dma(dbg["comb"][:], f_comb), waits=comb_ready, out_store=True)
            ldbg = P.op("sync", dma(dbg["ln2T"][:], f_ln2T), waits=ln_done, out_store=True)
            P.raw_wait("sync", ldbg)

        mm1_done = None
        mm2_done = None
        relu_done = {}
        store_done = [None] * 16
        for ex in range(NEXP):
            w1_ready = P.dma("w1", "gsdma", (lambda ex: lambda e: e.dma_start(
                out=f_w1, in_=w1t[ex]))(ex),
                waits=[mm1_done if ex else oproj_last_mm])
            w2_ready = P.dma("w2", "gsdma", (lambda ex: lambda e: e.dma_start(
                out=f_w2, in_=w2t[ex]))(ex), waits=[mm2_done])
            for tc in range(4):
                for hf in range(16):
                    b = hf % 4
                    wmm = [w1_ready, bank_owner.get(b)]
                    if ex == 0 and tc == 0 and hf < 4:
                        wmm += ln_done
                    lm = P.op("tensor", (lambda ex, tc, hf, b: lambda e: [
                        e.matmul(pb[b][:, :],
                                 f_w1[:, d * HID + hf * 128: d * HID + (hf + 1) * 128],
                                 f_ln2T[:, d * OWN + tc * 512: d * OWN + (tc + 1) * 512],
                                 start=(d == 0), stop=(d == 7))
                        for d in range(8)][-1])(ex, tc, hf, b),
                        waits=wmm, signal=L(tag))
                    bank_owner[b] = lm
                    lr = P.op("scalar", (lambda ex, hf, b: lambda e: e.activation(
                        out=f_hT[:, hf * 512:(hf + 1) * 512], in_=pb[b][:, :], func=AF.Relu,
                        bias=c_b1[:, ex * 16 + hf: ex * 16 + hf + 1]))(ex, hf, b),
                        waits=[lm, l_b1, mm2_done], signal=L(tag))
                    bank_owner[b] = lr
                    relu_done[hf] = lr
                mm1_done = relu_done[15]
                for m4 in range(4):
                    m = tc * 4 + m4
                    for oh in range(2):
                        b = 4 + (m4 * 2 + oh) % 4
                        wmm = [w2_ready, bank_owner.get(b)] + \
                              [relu_done[hf] for hf in range(16)]
                        if ex == 0:
                            wmm += [combT_ready[m], l_b2]
                        def mk_mm2(ex, m4, m, oh, b):
                            def f(e):
                                last = None
                                for hf in range(16):
                                    last = e.matmul(
                                        pb[b][:, :],
                                        f_hT[:, hf * 512 + m4 * 128: hf * 512 + (m4 + 1) * 128],
                                        f_w2[:, hf * D + oh * 512: hf * D + (oh + 1) * 512],
                                        start=(hf == 0), stop=(hf == 15 and ex != 0))
                                if ex == 0:
                                    last = e.matmul(
                                        pb[b][:, :], f_combT[0:8, m * 128:(m + 1) * 128],
                                        c_b2[0:8, oh * 512:(oh + 1) * 512],
                                        start=False, stop=True)
                                return last
                            return f
                        lm = P.op("tensor", mk_mm2(ex, m4, m, oh, b), waits=wmm,
                                  signal=L(tag))
                        bank_owner[b] = lm
                        le = P.op("vector", (lambda ex, m, oh, b: lambda e:
                            e.scalar_tensor_tensor(
                                out=r_res[m][:, oh * 512:(oh + 1) * 512],
                                in0=pb[b][:, :],
                                scalar=f_comb[:, m * 8 + ex: m * 8 + ex + 1],
                                in1=r_res[m][:, oh * 512:(oh + 1) * 512],
                                op0=ALU.mult, op1=ALU.add))(ex, m, oh, b),
                            waits=[lm, comb_ready[m], store_done[m]] +
                                  ([res_dump[m]] if ex == 0 else []),
                            signal=L(tag))
                        bank_owner[b] = le
                        res_labels[m] = le
                mm2_done = bank_owner[4 + 7 % 4]
            if ex == NEXP - 1:
                for m in range(16):
                    store_done[m] = P.op("sync", (lambda m: lambda e: e.dma_start(
                        out=out[m * 128:(m + 1) * 128, :], in_=r_res[m][:]))(m),
                        waits=[res_labels[m]], out_store=True)
        P.raw_wait("sync", store_done[15])
        P.emit_segment("ffn")

    return nc


def _host_constants():
    i = np.arange(0, DH, 2, dtype=np.float64) / DH
    theta = 1.0 / (10000.0 ** i)
    pos = np.arange(SEQ, dtype=np.float64)
    ang = pos[None, :] * theta[(np.arange(128) % 32)][:, None]
    cosfm = np.cos(ang).astype(np.float32)
    sinfm = np.sin(ang).astype(np.float32)
    identb = np.eye(128, dtype=np.float32)
    rot = np.zeros((128, 128), dtype=np.float32)
    for k in range(64):
        rot[2 * k + 1, 2 * k] = -1.0
        rot[2 * k, 2 * k + 1] = 1.0
    sel2 = np.zeros((2, 128), dtype=np.float32)
    sel2[0, 0:64] = 1.0
    sel2[1, 64:128] = 1.0
    bf = lambda a: np.ascontiguousarray(a.astype(ml_dtypes.bfloat16))
    return dict(costab=bf(cosfm), sintab=bf(sinfm), identb=bf(identb),
                rotmat=bf(rot), ones128b=bf(np.ones((128, 128), np.float32)),
                onescol_b=bf(np.ones((128, 1), np.float32)),
                onesrow_f=np.ones((1, 128), np.float32),
                sel2=bf(sel2), identf=identb.copy())


def make_in_maps(L0_attn_x, L0_attn_gamma, L0_attn_Wq, L0_attn_Wkv, L0_attn_W_bilinear,
                 L0_attn_Wo, L0_ffn_W1, L0_ffn_b1, L0_ffn_W2, L0_ffn_b2, L0_ffn_Wg,
                 L0_ffn_bg):
    C = np.ascontiguousarray
    x = np.asarray(L0_attn_x, dtype=np.float32)
    consts = _host_constants()
    A = np.asarray(L0_attn_W_bilinear, np.float32).transpose(0, 2, 1)
    wb_half = C(A.transpose(1, 0, 2).reshape(64, H * DH))
    wbt2 = np.concatenate([wb_half, wb_half], axis=0)
    shared = dict(
        gamma=C(np.asarray(L0_attn_gamma, np.float32)[None, :]),
        wq=C(np.asarray(L0_attn_Wq, np.float32)),
        wkv=C(np.asarray(L0_attn_Wkv, np.float32)),
        wbt2=C(wbt2),
        wo=C(np.asarray(L0_attn_Wo, np.float32).reshape(8, 128, D)
             .transpose(1, 0, 2).reshape(128, 8 * D)),
        w1t=C(np.asarray(L0_ffn_W1, np.float32).transpose(0, 2, 1)
              .reshape(NEXP, 8, 128, HID).transpose(0, 2, 1, 3)
              .reshape(NEXP, 128, 8 * HID)),
        w2t=C(np.asarray(L0_ffn_W2, np.float32).transpose(0, 2, 1)
              .reshape(NEXP, 16, 128, D).transpose(0, 2, 1, 3)
              .reshape(NEXP, 128, 16 * D)),
        b1c=C(np.asarray(L0_ffn_b1, np.float32).reshape(NEXP, HID // 128, 128)
              .transpose(2, 0, 1).reshape(128, NEXP * (HID // 128))),
        b2=C(np.asarray(L0_ffn_b2, np.float32)),
        wgt=C(np.asarray(L0_ffn_Wg, np.float32).T.reshape(8, 128, NEXP)
              .transpose(1, 0, 2).reshape(128, 8 * NEXP)),
        bg=C(np.asarray(L0_ffn_bg, np.float32)[None, :]),
        **consts,
    )
    in_maps = []
    for core in range(8):
        s, o = core // 2, core % 2
        xo = x[s, o * OWN:(o + 1) * OWN]
        xr = x[s, (1 - o) * OWN:(2 - o) * OWN]
        m = dict(shared)
        m["x_roll"] = C(np.concatenate([xo, xr], axis=0))
        own_sl = slice(o * OWN, (o + 1) * OWN)
        oth_sl = slice((1 - o) * OWN, (2 - o) * OWN)
        m["costab"] = C(np.concatenate([consts["costab"][:, own_sl],
                                        consts["costab"][:, oth_sl]], axis=1))
        m["sintab"] = C(np.concatenate([consts["sintab"][:, own_sl],
                                        consts["sintab"][:, oth_sl]], axis=1))
        in_maps.append(m)
    return in_maps


_NC_CACHE = {}


def kernel(**inputs):
    if "main" not in _NC_CACHE:
        _NC_CACHE["main"] = build_nc(debug=False)
    nc = _NC_CACHE["main"]
    in_maps = make_in_maps(**inputs)
    res = run_bass_kernel_spmd(nc, in_maps, core_ids=list(range(8)))
    outp = np.empty((4, SEQ, D), dtype=np.float32)
    for core in range(8):
        s, o = core // 2, core % 2
        outp[s, o * OWN:(o + 1) * OWN] = res.results[core]["out"]
    return outp


# version 08
# speedup vs baseline: 1.6435x; 1.0039x over previous
import numpy as np
import ml_dtypes
from contextlib import ExitStack

import concourse.bass as bass
import concourse.mybir as mybir
from concourse.bass_utils import run_bass_kernel_spmd

F32 = mybir.dt.float32
BF16 = mybir.dt.bfloat16
AF = mybir.ActivationFunctionType
ALU = mybir.AluOpType
AX = mybir.AxisListType

SEQ = 4096
OWN = 2048
CH = 512
NCH = OWN // CH
D = 1024
H = 16
DH = 64
NPAIR = 8
HID = 2048
NEXP = 8
EPS = 1e-5
SHIFT = 40.0

SBUF_BYTES = 207 * 1024
VTW = H * (DH + 1)


class Arena:

    def __init__(self, nc, st):
        self.words = SBUF_BYTES // 4
        self.t = st.enter_context(nc.sbuf_tensor("arena", [128, self.words], F32))
        self.off = 0
        self.marks = []

    def _alloc(self, bytes_pp):
        off = (self.off + 511) // 512 * 512
        assert off + bytes_pp <= SBUF_BYTES, f"arena overflow: {off}+{bytes_pp}"
        self.off = off + bytes_pp
        return off

    def f32(self, cols, rows=128):
        off = self._alloc(cols * 4)
        return self.t[0:rows, off // 4: off // 4 + cols]

    def bf16(self, cols, rows=128):
        off = self._alloc(cols * 2 + (cols * 2) % 4)
        nw = (cols * 2 + 3) // 4
        return self.t[0:rows, off // 4: off // 4 + nw].bitcast(BF16)[:, 0:cols]

    def mark(self):
        self.marks.append(self.off)

    def reset(self):
        self.off = self.marks.pop()


class Prog:
    QUEUES = ("sync", "gsdma", "tensor", "vector", "scalar")
    ENG = ("tensor", "vector", "scalar")

    def __init__(self, nc, st):
        self.nc = nc
        self.st = st
        self.sems = {q: st.enter_context(nc.semaphore("sem_" + q)) for q in self.ENG}
        self.sems["out"] = st.enter_context(nc.semaphore("sem_out"))
        self.ops = []
        self.labels = {}
        self.counts = {q: 0 for q in self.ENG}
        self.counts["out"] = 0
        self.watermarks = {}
        self.uid = 0

    def lab(self, tag="l"):
        self.uid += 1
        return f"{tag}{self.uid}"

    def _chan(self, name):
        if name not in self.sems:
            self.sems[name] = self.st.enter_context(self.nc.semaphore("sem_" + name))
            self.counts[name] = 0
        return name

    def op(self, queue, fn, waits=(), signal=None, out_store=False):
        if out_store:
            return self.dma("out", "sync", fn, waits=waits, signal=signal)
        assert queue in self.ENG, queue
        if signal is False:
            self.ops.append((queue, fn, tuple(w for w in waits if w), None, queue))
            return None
        if signal is None:
            signal = self.lab(queue)
        self.counts[queue] += 1
        assert signal not in self.labels, signal
        self.labels[signal] = (queue, self.counts[queue])
        self.ops.append((queue, fn, tuple(w for w in waits if w), signal, queue))
        return signal

    def dma(self, channel, queue, fn, waits=(), signal=None, serialize=False):
        assert queue in ("sync", "gsdma")
        ch = self._chan(channel)
        if signal is None:
            signal = self.lab(channel)
        self.counts[ch] += 16
        assert signal not in self.labels, signal
        self.labels[signal] = (ch, self.counts[ch])
        self.ops.append((queue, fn, tuple(w for w in waits if w), signal, ch))
        if serialize:
            self.raw_wait(queue, signal)
        return signal

    def chain(self, queue, fns, waits=()):
        lb = None
        for i, fn in enumerate(fns):
            w = list(waits) if i == 0 else [lb]
            lb = self.op(queue, fn, waits=w)
        return lb

    def raw_wait(self, queue, label):
        self.ops.append((queue, None, (label,), None, None))

    def emit_segment(self, name):
        per_q = {q: [] for q in self.QUEUES}
        for rec in self.ops:
            per_q[rec[0]].append(rec)
        self.ops = []
        active = [q for q in self.QUEUES if per_q[q]]

        with self.nc.Block(name) as blk:
            for q in active:
                def mk(qq):
                    def body(eng):
                        for (_, fn, waits, signal, semkey) in per_q[qq]:
                            for labw in waits:
                                pq, cnt = self.labels[labw]
                                key = (qq, pq)
                                if self.watermarks.get(key, 0) < cnt:
                                    eng.wait_ge(self.sems[pq], cnt)
                                    self.watermarks[key] = cnt
                            if fn is None:
                                continue
                            instr = fn(eng)
                            if signal is not None:
                                instr.then_inc(self.sems[semkey],
                                               16 if semkey not in self.ENG else 1)
                    return body
                getattr(blk, "gpsimd" if q == "gsdma" else q)(mk(q))
        for q in self.QUEUES:
            for pq in list(self.counts):
                self.watermarks[(q, pq)] = self.counts[pq]


def build_nc(debug=False):
    nc = bass.Bass("TRN2", target_bir_lowering=False)
    dt = nc.dram_tensor

    x_roll = dt("x_roll", [SEQ, D], F32, kind="ExternalInput").ap()
    gamma = dt("gamma", [1, D], F32, kind="ExternalInput").ap()
    wq = dt("wq", [D, D], F32, kind="ExternalInput").ap()
    wkv = dt("wkv", [D, 2 * D], F32, kind="ExternalInput").ap()
    wbt2 = dt("wbt2", [128, H * DH], F32, kind="ExternalInput").ap()
    wo = dt("wo", [128, 8 * D], F32, kind="ExternalInput").ap()
    w1t = dt("w1t", [NEXP, 128, 8 * HID], F32, kind="ExternalInput").ap()
    w2t = dt("w2t", [NEXP, 128, 16 * D], F32, kind="ExternalInput").ap()
    b1c = dt("b1c", [128, NEXP * (HID // 128)], F32, kind="ExternalInput").ap()
    b2 = dt("b2", [NEXP, D], F32, kind="ExternalInput").ap()
    wgt = dt("wgt", [128, 8 * NEXP], F32, kind="ExternalInput").ap()
    bg = dt("bg", [1, NEXP], F32, kind="ExternalInput").ap()
    costab = dt("costab", [128, SEQ], BF16, kind="ExternalInput").ap()
    sintab = dt("sintab", [128, SEQ], BF16, kind="ExternalInput").ap()
    identb = dt("identb", [128, 128], BF16, kind="ExternalInput").ap()
    rotmat = dt("rotmat", [128, 128], BF16, kind="ExternalInput").ap()
    ones128b = dt("ones128b", [128, 128], BF16, kind="ExternalInput").ap()
    onescol_b = dt("onescol_b", [128, 1], BF16, kind="ExternalInput").ap()
    onesrow_f = dt("onesrow_f", [1, 128], F32, kind="ExternalInput").ap()
    sel2 = dt("sel2", [2, 128], BF16, kind="ExternalInput").ap()
    identf = dt("identf", [128, 128], F32, kind="ExternalInput").ap()

    out = dt("out", [OWN, D], F32, kind="ExternalOutput").ap()
    dbg = {}
    if debug:
        for nm, shp, dtp in [
            ("kT", [128, 8 * OWN], BF16), ("qT", [128, 8 * OWN], BF16),
            ("v", [128, 16 * VTW], BF16), ("aoT", [128, 8 * OWN], BF16),
            ("res1", [OWN, D], F32), ("comb", [128, 128], F32),
            ("kTA", [128, 8 * OWN], BF16), ("vA", [128, 16 * VTW], BF16),
            ("xnT", [128, 8 * CH], BF16),
            ("sums", [128, 256], F32), ("ln2T", [128, 8 * OWN], BF16),
            ("bias", [1, 32], F32),
        ]:
            dbg[nm] = dt("dbg_" + nm, shp, dtp, kind="ExternalOutput").ap()

    with ExitStack() as st:
        P = Prog(nc, st)
        A = Arena(nc, st)
        PB = st.enter_context(nc.psum_tensor("PB", [128, 4096], F32))
        pb = [PB[:, i * 512:(i + 1) * 512] for i in range(8)]
        pbT = PB[:, 3584:4096].bitcast(BF16)
        L = P.lab
        bank_owner = {}

        def dma(dst, src):
            return lambda e: e.dma_start(out=dst, in_=src)

        c_ident = A.bf16(128)
        c_rot = A.bf16(128)
        c_ones128b = A.bf16(128)
        c_onescol = A.bf16(1)
        c_onesrow = A.f32(128, rows=1)
        c_wbt = A.bf16(H * DH)
        c_b1 = A.f32(NEXP * (HID // 128))
        c_b2 = A.bf16(D, rows=NEXP)
        c_wg = A.bf16(8 * NEXP)
        c_bg = A.f32(NEXP, rows=1)
        c_sel2 = A.bf16(128, rows=2)
        c_identf = A.f32(128)
        s_bias = A.f32(32)
        s_bv = A.f32(32, rows=1)
        s_mx = A.f32(32, rows=1)

        A.mark()
        a_x = [A.f32(D) for _ in range(2)]
        a_ln1 = A.f32(D)
        a_xn = [A.bf16(D) for _ in range(2)]
        a_st = A.f32(16)
        a_mv = A.f32(4)
        a_rs = A.f32(8)
        a_xnT = A.bf16(8 * CH)
        a_wstg = [A.bf16(4 * D) for _ in range(2)]
        a_scr = A.f32(2048)
        a_kraw = [a_scr[:, 0:256].bitcast(BF16), a_scr[:, 256:512].bitcast(BF16)]
        a_rt1 = [a_scr[:, 512:768].bitcast(BF16), a_scr[:, 768:1024].bitcast(BF16)]
        a_rt2 = [a_scr[:, 1024:1280].bitcast(BF16), a_scr[:, 1280:1536].bitcast(BF16)]
        a_qhat = a_scr[:, 1536:1792].bitcast(BF16)
        a_sq = a_scr[:, 1792:2048].bitcast(BF16)
        a_gbc = A.bf16(D)
        a_kT = A.bf16(8 * OWN)
        a_v = A.bf16(16 * VTW + 64)
        s_r64 = a_scr[:, 0:1024]
        a_qT = A.bf16(8 * OWN)
        a_aoT = A.bf16(8 * OWN)
        a_ptw = A.bf16(2048)
        a_pt2 = [a_ptw[:, 0:1024], a_ptw[:, 1024:2048]]
        a_cos = A.bf16(OWN)
        a_sin = A.bf16(OWN)
        s_sumsT = A.f32(256)
        s_sumb16 = A.bf16(512, rows=2)
        assert A.off <= SBUF_BYTES, A.off

        l_ident = P.dma("cs", "sync", dma(c_ident, identb[:]), serialize=True)
        l_rot = P.dma("cs", "sync", dma(c_rot, rotmat[:]), serialize=True)
        l_o128 = P.dma("cs", "sync", dma(c_ones128b, ones128b[:]), serialize=True)
        l_ocol = P.dma("cs", "sync", dma(c_onescol, onescol_b[:]), serialize=True)
        l_orow = P.dma("cs", "sync", dma(c_onesrow, onesrow_f[:]), serialize=True)
        l_bg = P.dma("cs", "sync", dma(c_bg, bg[:]), serialize=True)
        l_b1 = P.dma("cs", "sync", dma(c_b1, b1c[:]), serialize=True)
        l_gam = P.dma("cs", "sync", dma(a_x[0][0:1, 0:D], gamma[:]), serialize=True)
        l_wbt = P.dma("cg", "gsdma", dma(c_wbt, wbt2[:]), serialize=True)
        l_b2 = P.dma("cg", "gsdma", dma(c_b2, b2[:]), serialize=True)
        l_wg = P.dma("cg", "gsdma", dma(c_wg, wgt[:]), serialize=True)
        l_sel2 = P.dma("cg", "gsdma", dma(c_sel2, sel2[:]), serialize=True)
        l_idf = P.dma("cg", "gsdma", dma(c_identf, identf[:]), serialize=True)
        P.op("scalar", lambda e: e.activation(out=c_wbt, in_=c_wbt, func=AF.Copy, scale=0.125),
             waits=[l_wbt], signal="wbt_ok")
        for i in range(2):
            P.op("tensor", (lambda i: lambda e: e.matmul(
                pb[i], c_onesrow, a_x[0][0:1, i * 512:(i + 1) * 512],
                start=True, stop=True))(i),
                waits=[l_orow, l_gam], signal=f"gbcmm{i}")
            P.op("vector", (lambda i: lambda e: e.tensor_copy(
                a_gbc[:, i * 512:(i + 1) * 512], pb[i]))(i),
                waits=[f"gbcmm{i}"], signal=f"gbc{i}")
        P.op("vector", lambda e: e.memset(s_mx[0:1, :], -1e30), signal="mxinit")
        for tt in range(16):
            P.op("vector", (lambda tt: lambda e: e.memset(
                a_v[:, tt * VTW:(tt + 1) * VTW].rearrange(
                    "p (h u) -> p h u", u=DH + 1)[:, :, DH:DH + 1], 1.0))(tt),
                signal=f"vone{tt}")
        P.op("vector", lambda e: e.memset(a_v[:, 16 * VTW:16 * VTW + 64], 1.0),
             signal="vpad")
        P.op("scalar", lambda e: e.preload_act_table(AF.Sqrt), signal=False)
        P.emit_segment("s0")

        def ln_block(src, dst, deps, tag, parity, gamma_mul):
            rsb = parity * 4
            l1 = P.chain("vector", [
                lambda e: (e.bn_stats(a_st[:, 0:6], src[:, 0:512]),
                           e.bn_stats(a_st[:, 6:12], src[:, 512:1024]))[-1],
                lambda e: e.bn_aggr(a_mv[:, 0:2], a_st[:, 0:12]),
                lambda e: e.tensor_scalar(out=a_rs[:, rsb:rsb + 1], in0=a_mv[:, 1:2],
                                          scalar1=EPS, scalar2=None, op0=ALU.add),
                lambda e: e.reciprocal(a_rs[:, rsb:rsb + 1], a_rs[:, rsb:rsb + 1]),
            ], waits=deps)
            l2 = P.op("scalar", lambda e: e.activation(
                out=a_rs[:, rsb + 1:rsb + 2], in_=a_rs[:, rsb:rsb + 1], func=AF.Sqrt),
                waits=[l1])
            l2b = P.op("vector", lambda e: e.tensor_scalar(
                out=a_rs[:, rsb + 2:rsb + 3], in0=a_mv[:, 0:1],
                scalar1=a_rs[:, rsb + 1:rsb + 2], scalar2=-1.0,
                op0=ALU.mult, op1=ALU.mult), waits=[l2])
            l2c = P.op("scalar", lambda e: e.activation(
                out=a_ln1[:], in_=src, func=AF.Identity,
                scale=a_rs[:, rsb + 1:rsb + 2], bias=a_rs[:, rsb + 2:rsb + 3]),
                waits=[l2b])
            l3 = P.chain("vector", [
                lambda e: (e.bn_stats(a_st[:, 0:6], a_ln1[:, 0:512]),
                           e.bn_stats(a_st[:, 6:12], a_ln1[:, 512:1024]))[-1],
                lambda e: e.bn_aggr(a_mv[:, 2:4], a_st[:, 0:12]),
                lambda e: e.tensor_scalar(out=a_rs[:, rsb:rsb + 1], in0=a_mv[:, 3:4],
                                          scalar1=EPS, scalar2=None, op0=ALU.add),
                lambda e: e.reciprocal(a_rs[:, rsb:rsb + 1], a_rs[:, rsb:rsb + 1]),
            ], waits=[l2c])
            l4 = P.op("scalar", lambda e: e.activation(
                out=a_rs[:, rsb + 1:rsb + 2], in_=a_rs[:, rsb:rsb + 1], func=AF.Sqrt),
                waits=[l3])
            l4b = P.op("vector", lambda e: e.tensor_scalar(
                out=a_rs[:, rsb + 2:rsb + 3], in0=a_mv[:, 2:3],
                scalar1=a_rs[:, rsb + 1:rsb + 2], scalar2=-1.0,
                op0=ALU.mult, op1=ALU.mult), waits=[l4])
            l4c = P.op("scalar", lambda e: e.activation(
                out=dst, in_=a_ln1[:], func=AF.Identity,
                scale=a_rs[:, rsb + 1:rsb + 2], bias=a_rs[:, rsb + 2:rsb + 3]),
                waits=[l4b] + (gamma_mul or []))
            l5 = P.op("vector", lambda e: e.tensor_tensor(
                out=dst, in0=dst, in1=a_gbc, op=ALU.mult),
                waits=[l4c, "gbc0", "gbc1"])
            return l5

        kraw_free = [None, None]

        def rope(bank, toksl, dst_ap, buf, tag, l_cos, l_sin, dst_free=None):
            kraw, rt1, rt2 = a_kraw[buf], a_rt1[buf], a_rt2[buf]
            wev = [bank_owner.get(bank)] + (kraw_free[buf] or [])
            l_ev = P.op("scalar", lambda e: e.activation(out=kraw, in_=pb[bank],
                                                         func=AF.Copy),
                        waits=wev, signal=L(tag))
            l_ro = P.op("tensor", lambda e: e.matmul(pb[bank], c_rot, kraw,
                                                     start=True, stop=True),
                        waits=[l_ev, l_rot], signal=L(tag))
            l_t1 = P.op("vector", lambda e: e.tensor_tensor(
                out=rt1, in0=kraw, in1=a_cos[:, toksl], op=ALU.mult),
                waits=[l_ev, l_cos], signal=L(tag))
            l_t2 = P.op("vector", lambda e: e.tensor_tensor(
                out=rt2, in0=pb[bank], in1=a_sin[:, toksl], op=ALU.mult),
                waits=[l_ro, l_t1, l_sin], signal=L(tag))
            l_dst = P.op("vector", lambda e: e.tensor_tensor(
                out=dst_ap, in0=rt1, in1=rt2, op=ALU.add),
                waits=[l_t2] + ([dst_free] if dst_free else []), signal=L(tag))
            kraw_free[buf] = [l_ro, l_t1]
            bank_owner[bank] = l_t2
            return l_dst

        def warm_burst(n=24):
            def f(e):
                r = None
                for _ in range(n):
                    r = e.matmul(pb[7][0:1, :], c_ones128b[0:1, 0:1],
                                 c_b2[0:1, 0:512], start=True, stop=True)
                return r
            P.op("tensor", f, signal=False)

        prev_sq = [None]
        mx_last = [None]

        def normsq(src_ap, bank, mslot, deps, tag):
            l_sq = P.op("vector", lambda e: e.tensor_tensor(
                out=a_sq, in0=src_ap, in1=src_ap, op=ALU.mult),
                waits=list(deps) + ([prev_sq[0]] if prev_sq[0] else []), signal=L(tag))
            l_mm = P.op("tensor", lambda e: e.matmul(pb[bank][0:1, :], c_onescol, a_sq,
                                                     start=True, stop=True),
                        waits=[l_sq, l_ocol, bank_owner.get(bank)], signal=L(tag))
            prev_sq[0] = l_mm
            l_mx = P.chain("vector", [
                lambda e: e.tensor_reduce(s_mx[0:1, 31:32], pb[bank][0:1, :],
                                          axis=AX.X, op=ALU.max),
                lambda e: e.tensor_tensor(out=s_mx[0:1, mslot:mslot + 1],
                                          in0=s_mx[0:1, mslot:mslot + 1],
                                          in1=s_mx[0:1, 31:32], op=ALU.max),
            ], waits=[l_mm, "mxinit"])
            bank_owner[bank] = l_mx
            mx_last[0] = l_mx
            return l_mx

        def proj_pass(pass_id):
            tag = f"P{pass_id}"
            base = pass_id * OWN
            if pass_id == 1:
                P.op("scalar", lambda e: e.preload_act_table(AF.Sqrt), signal=False)
            l_cos = P.dma("tcos", "sync", dma(a_cos, costab[:, base:base + OWN]))
            l_sin = P.dma("tsin", "sync", dma(a_sin, sintab[:, base:base + OWN]))
            warm_burst()
            wstg_free = [None, None]
            x_free = [None, None]
            xn_free = [None, None]
            xnT_free = None
            qhat_free = None
            for tc in range(NCH):
                xn_labels = []
                tr_labels = []
                for t4 in range(4):
                    tile = tc * 4 + t4
                    xb = a_x[t4 % 2]
                    l_x = P.dma(f"x{t4 % 2}", "sync",
                                dma(xb[:], x_roll[base + tile * 128: base + tile * 128 + 128, :]),
                                waits=[x_free[t4 % 2]])
                    xn = a_xn[t4 % 2]
                    l_xn = ln_block(xb[:], xn[:], [l_x], tag, t4 % 2,
                                    gamma_mul=[xn_free[t4 % 2]] if xn_free[t4 % 2] else [])
                    x_free[t4 % 2] = l_xn
                    wtr = [l_xn, l_ident, bank_owner.get(7)]
                    l_tr = P.op("tensor", (lambda xn: lambda e: [
                        e.transpose(pbT[:, d * 128:(d + 1) * 128],
                                    xn[:, d * 128:(d + 1) * 128], c_ident)
                        for d in range(8)][-1])(xn),
                        waits=wtr, signal=L(tag))
                    bank_owner[7] = l_tr
                    xn_free[t4 % 2] = l_tr
                    tr_labels.append(l_tr)
                    l_ev = P.op("vector", (lambda t4: lambda e: e.tensor_copy(
                        a_xnT.rearrange("p (d c) -> p d c", d=8)[:, :, t4 * 128:(t4 + 1) * 128],
                        pbT.rearrange("p (d c) -> p d c", d=8)))(t4),
                        waits=[l_tr] + ([xnT_free] if (xnT_free and t4 == 0) else []),
                        signal=L(tag))
                    bank_owner[7] = l_ev
                    xn_labels.append(l_ev)

                def wload(which, half):
                    sbuf = a_wstg[half]
                    r0 = half * 512
                    if which == "k":
                        src = wkv[r0:r0 + 512, 0:D]
                    elif which == "v":
                        src = wkv[r0:r0 + 512, D:2 * D]
                    else:
                        src = wq[r0:r0 + 512, 0:D]
                    src = src.rearrange("(c p) f -> p c f", p=128)
                    dst = sbuf.rearrange("p (c f) -> p c f", f=D)
                    l_w = P.dma(f"ws{half}", "gsdma", dma(dst, src),
                                waits=[wstg_free[half]])
                    return sbuf, l_w

                def proj_mm(which, d, dd, sbuf, l_w, first_waits):
                    if which == "k" or which == "q":
                        def f(e, sbuf=sbuf, d=d, dd=dd):
                            r = None
                            for kf in range(8):
                                r = e.matmul(pb[kf],
                                             sbuf[:, dd * D + kf * 128: dd * D + (kf + 1) * 128],
                                             a_xnT[:, d * CH:(d + 1) * CH],
                                             start=(d == 0), stop=(d == 7))
                            return r
                    else:
                        def f(e, sbuf=sbuf, d=d, dd=dd):
                            r = None
                            for tt in range(4):
                                for vh in range(2):
                                    r = e.matmul(pb[tt * 2 + vh],
                                                 a_xnT[:, d * CH + tt * 128: d * CH + (tt + 1) * 128],
                                                 sbuf[:, dd * D + vh * 512: dd * D + (vh + 1) * 512],
                                                 start=(d == 0), stop=(d == 7))
                            return r
                    return P.op("tensor", f, waits=[l_w] + first_waits, signal=L(tag))

                def proj_sweep(which):
                    last_mm = None
                    for half in range(2):
                        sbuf, l_w = wload(which, half)
                        for dd in range(4):
                            d = half * 4 + dd
                            fw = []
                            if d == 0:
                                fw = xn_labels + [bank_owner.get(b) for b in range(8)]
                            if last_mm:
                                fw = fw + [last_mm]
                            last_mm = proj_mm(which, d, dd, sbuf, l_w, fw)
                        wstg_free[half] = last_mm
                    for b in range(8):
                        bank_owner[b] = last_mm
                    return last_mm

                toksl = slice(tc * CH, (tc + 1) * CH)

                last_mm = proj_sweep("k")
                for kf in range(8):
                    dst = a_kT[:, kf * OWN + tc * CH: kf * OWN + (tc + 1) * CH]
                    l_dst = rope(kf, toksl, dst, kf % 2, tag, l_cos, l_sin)
                    bank_owner[kf] = l_dst

                last_mm = proj_sweep("v")
                for tt in range(4):
                    for vh in range(2):
                        b = tt * 2 + vh
                        vtile = tc * 4 + tt
                        l_ev = P.op("vector", (lambda b, vtile, vh: lambda e: e.tensor_copy(
                            a_v[:, vtile * VTW + vh * 8 * (DH + 1):
                                vtile * VTW + (vh + 1) * 8 * (DH + 1)].rearrange(
                                "p (h u) -> p h u", u=DH + 1)[:, :, 0:DH],
                            pb[b][:].rearrange("p (h d) -> p h d", d=DH)))(b, vtile, vh),
                            waits=[bank_owner[b]], signal=L(tag))
                        bank_owner[b] = l_ev

                if pass_id == 0:
                    last_mm = proj_sweep("q")
                    for qf in range(8):
                        l_dst = rope(qf, toksl, a_qhat, qf % 2, tag, l_cos, l_sin,
                                     dst_free=qhat_free)
                        h0, h1 = 2 * qf, 2 * qf + 1
                        l_bl = P.op("tensor", (lambda qf, h0, h1: lambda e: (
                            e.matmul(pb[qf][0:64, :], c_wbt[0:64, h0 * 64:(h0 + 1) * 64],
                                     a_qhat[0:64, :], start=True, stop=True),
                            e.matmul(pb[qf][64:128, :], c_wbt[64:128, h1 * 64:(h1 + 1) * 64],
                                     a_qhat[64:128, :], start=True, stop=True))[-1])(qf, h0, h1),
                            waits=[l_dst, "wbt_ok", bank_owner.get(qf)], signal=L(tag))
                        qhat_free = l_bl
                        bank_owner[qf] = l_bl
                        dstq = a_qT[:, qf * OWN + tc * CH: qf * OWN + (tc + 1) * CH]
                        l_ev = P.op("vector", (lambda dstq, qf: lambda e: e.tensor_copy(
                            dstq, pb[qf][:, :]))(dstq, qf),
                            waits=[l_bl], signal=L(tag))
                        bank_owner[qf] = l_ev
                xnT_free = last_mm

        def norm_sweep(pass_id, tag):
            srcs = [("k", a_kT, 8 + 8 * pass_id)]
            if pass_id == 0:
                srcs.append(("q", a_qT, 0))
            for nm, buf, slot0 in srcs:
                for kf in range(8):
                    l_sq = P.op("vector", (lambda buf, kf: lambda e: e.tensor_tensor(
                        out=a_ptw, in0=buf[:, kf * OWN:(kf + 1) * OWN],
                        in1=buf[:, kf * OWN:(kf + 1) * OWN], op=ALU.mult))(buf, kf),
                        waits=[prev_sq[0], bank_owner.get(9)], signal=L(tag))
                    lms = []
                    for i in range(4):
                        lm = P.op("tensor", (lambda i: lambda e: e.matmul(
                            PB[0:1, i * 512:(i + 1) * 512], c_onescol,
                            a_ptw[:, i * 512:(i + 1) * 512],
                            start=True, stop=True))(i),
                            waits=[l_sq, l_ocol] +
                                  ([bank_owner.get(b) for b in range(4)] if True else []),
                            signal=L(tag))
                        bank_owner[i] = lm
                        lms.append(lm)
                    prev_sq[0] = lms[-1]
                    mslot = slot0 + kf
                    l_mx = P.chain("vector", [
                        lambda e: e.tensor_reduce(s_mx[0:1, 31:32], PB[0:1, 0:2048],
                                                  axis=AX.X, op=ALU.max),
                        (lambda mslot: lambda e: e.tensor_tensor(
                            out=s_mx[0:1, mslot:mslot + 1],
                            in0=s_mx[0:1, mslot:mslot + 1],
                            in1=s_mx[0:1, 31:32], op=ALU.max))(mslot),
                    ], waits=lms + ["mxinit"])
                    for i in range(4):
                        bank_owner[i] = l_mx
                    mx_last[0] = l_mx
                    bank_owner[9] = l_mx

        def bias_build(pass_id):
            tag = f"B{pass_id}"
            if pass_id == 0:
                l1 = P.op("vector", lambda e: e.tensor_tensor(
                    out=s_bv[0:1, 0:8], in0=s_mx[0:1, 0:8], in1=s_mx[0:1, 8:16], op=ALU.mult),
                    waits=[mx_last[0]], signal=L(tag))
                l2 = P.op("scalar", lambda e: e.activation(
                    out=s_bv[0:1, 0:8], in_=s_bv[0:1, 0:8], func=AF.Sqrt),
                    waits=[l1], signal=L(tag))
                l3 = P.op("vector", lambda e: e.tensor_scalar(
                    out=s_bv[0:1, 8:16], in0=s_bv[0:1, 0:8], scalar1=-1.0, scalar2=SHIFT,
                    op0=ALU.mult, op1=ALU.add), waits=[l2], signal=L(tag))
                lm = P.op("tensor", lambda e: e.matmul(
                    pb[0][:, 0:8], c_onesrow, s_bv[0:1, 8:16], start=True, stop=True),
                    waits=[l3, bank_owner.get(0)], signal=L(tag))
                bank_owner[0] = lm
                le = P.op("vector", lambda e: e.tensor_copy(s_bias[:, 0:8], pb[0][:, 0:8]),
                          waits=[lm], signal=L(tag))
                bank_owner[0] = le
                return le
            l1 = P.op("vector", lambda e: e.tensor_tensor(
                out=s_bv[0:1, 16:24], in0=s_mx[0:1, 0:8], in1=s_mx[0:1, 16:24], op=ALU.mult),
                waits=[mx_last[0]], signal=L(tag))
            l2 = P.op("scalar", lambda e: e.activation(
                out=s_bv[0:1, 16:24], in_=s_bv[0:1, 16:24], func=AF.Sqrt),
                waits=[l1], signal=L(tag))
            l3 = P.chain("vector", [
                lambda e: e.tensor_tensor(out=s_bv[0:1, 16:24], in0=s_bv[0:1, 16:24],
                                          in1=s_bv[0:1, 0:8], op=ALU.max),
                lambda e: e.tensor_scalar(out=s_bv[0:1, 24:32], in0=s_bv[0:1, 16:24],
                                          scalar1=-1.0, scalar2=SHIFT,
                                          op0=ALU.mult, op1=ALU.add),
                lambda e: e.tensor_tensor(out=s_bv[0:1, 16:24], in0=s_bv[0:1, 24:32],
                                          in1=s_bv[0:1, 8:16], op=ALU.subtract),
            ], waits=[l2])
            P.op("scalar", lambda e: e.preload_act_table(AF.Exp), waits=[l3],
                 signal=False)
            l4 = P.op("scalar", lambda e: e.activation(
                out=s_bv[0:1, 16:24], in_=s_bv[0:1, 16:24], func=AF.Exp),
                waits=[l3], signal=L(tag))
            lm = P.op("tensor", lambda e: (
                e.matmul(pb[0][:, 0:8], c_onesrow, s_bv[0:1, 24:32], start=True, stop=True),
                e.matmul(pb[1][:, 0:8], c_onesrow, s_bv[0:1, 16:24], start=True, stop=True))[-1],
                waits=[l4, bank_owner.get(0), bank_owner.get(1)], signal=L(tag))
            le = P.op("vector", lambda e: (
                e.tensor_copy(s_bias[:, 8:16], pb[0][:, 0:8]),
                e.tensor_copy(s_bias[:, 16:24], pb[1][:, 0:8]))[-1],
                waits=[lm], signal=L(tag))
            bank_owner[0] = le
            bank_owner[1] = le
            return le

        sums_last = [None]

        def sim_pass(pass_id, l_bias):
            tag = f"S{pass_id}"
            P.op("scalar", lambda e: e.preload_act_table(AF.Exp), waits=[l_bias],
                 signal=False)
            biascol = 0 if pass_id == 0 else 8
            warm_burst()
            for pr in range(NPAIR):
                h0, h1 = 2 * pr, 2 * pr + 1
                for qc in range(4):
                    qsl = slice(pr * OWN + qc * 512, pr * OWN + (qc + 1) * 512)
                    av_done = {}
                    exp_done = {}
                    def emit_sims(js):
                        for hp in (0, 1):
                            sb = (js % 2) * 2 + hp
                            ksl = slice(pr * OWN + js * 128, pr * OWN + (js + 1) * 128)
                            l_sim = P.op("tensor", (lambda sb, hp, ksl, qsl: lambda e: e.matmul(
                                pb[sb],
                                a_kT[hp * 64:(hp + 1) * 64, ksl],
                                a_qT[hp * 64:(hp + 1) * 64, qsl],
                                start=True, stop=True))(sb, hp, ksl, qsl),
                                waits=[bank_owner.get(sb)], signal=L(tag))
                            bank_owner[sb] = l_sim
                    def emit_exps(js):
                        par = js % 2
                        l_exp = P.op("scalar", (lambda par, pr: lambda e: e.activation(
                            out=a_pt2[par], in_=PB[:, par * 1024:(par + 1) * 1024],
                            func=AF.Exp,
                            bias=s_bias[:, biascol + pr: biascol + pr + 1]))(par, pr),
                            waits=[bank_owner[par * 2], bank_owner[par * 2 + 1],
                                   av_done.get((0, js - 2)), av_done.get((1, js - 2)),
                                   mx_last[0] if (pr == 0 and qc == 0 and js <= 1) else None],
                            signal=L(tag))
                        bank_owner[par * 2] = l_exp
                        bank_owner[par * 2 + 1] = l_exp
                        exp_done[(0, js)] = l_exp
                        exp_done[(1, js)] = l_exp
                    def emit_avs(js):
                        for hp, h in ((0, h0), (1, h1)):
                            ob = 4 + hp
                            pt = a_pt2[js % 2][:, hp * 512:(hp + 1) * 512]
                            wav = [exp_done[(hp, js)]]
                            if js == 0:
                                wav += [bank_owner.get(ob)]
                            l_av = P.op("tensor", (lambda ob, h, pt, js: lambda e:
                                e.matmul(pb[ob][0:128, :],
                                         a_v[:, js * VTW + h * (DH + 1):
                                             js * VTW + h * (DH + 1) + 128],
                                         pt, start=(js == 0), stop=(js == 15)))(
                                             ob, h, pt, js),
                                waits=wav, signal=L(tag))
                            av_done[(hp, js)] = l_av
                            if js == 15:
                                bank_owner[ob] = l_av
                    for js in range(16):
                        emit_sims(js)
                        emit_exps(js)
                        if js >= 1:
                            emit_avs(js - 1)
                    emit_avs(15)
                    l_e16 = P.op("vector", lambda e: (
                        e.tensor_copy(s_r64[64:65, 0:512], pb[4][64:65, :]),
                        e.tensor_copy(s_r64[64:65, 512:1024], pb[5][64:65, :]))[-1],
                        waits=[bank_owner[4], bank_owner[5], bank_owner.get(6)],
                        signal=L(tag))
                    bank_owner[4] = l_e16
                    bank_owner[5] = l_e16
                    l_tr = P.op("tensor", (lambda pr: lambda e: [
                        e.transpose(pb[6][:, qs * 16 + 2 * pr + hh: qs * 16 + 2 * pr + hh + 1],
                                    s_r64[64:65, hh * 512 + qs * 128: hh * 512 + (qs + 1) * 128],
                                    c_identf[64:65, 64:65])
                        for hh in range(2) for qs in range(4)][-1])(pr),
                        waits=[l_e16, l_idf, bank_owner[6]], signal=L(tag))
                    bank_owner[6] = l_tr
                    for hp, h in ((0, h0), (1, h1)):
                        ob = 4 + hp
                        asl = slice(pr * OWN + qc * 512, pr * OWN + (qc + 1) * 512)
                        dsta = a_aoT[hp * 64:hp * 64 + 64, asl]
                        if pass_id == 0:
                            l1 = P.op("vector", (lambda dsta, ob: lambda e: e.tensor_copy(
                                dsta, pb[ob][0:64, :]))(dsta, ob),
                                waits=[bank_owner[ob]], signal=L(tag))
                        else:
                            l1 = P.op("vector", (lambda dsta, ob, hp, pr: lambda e:
                                e.scalar_tensor_tensor(
                                    out=dsta, in0=dsta,
                                    scalar=s_bias[hp * 64:hp * 64 + 64, 16 + pr:17 + pr],
                                    in1=pb[ob][0:64, :], op0=ALU.mult, op1=ALU.add))(
                                        dsta, ob, hp, pr),
                                waits=[bank_owner[ob]], signal=L(tag))
                        bank_owner[ob] = l1
                    sTdst = s_sumsT.rearrange("p (s h) -> p s h", h=16)[
                        :, qc * 4:(qc + 1) * 4, 2 * pr:2 * pr + 2]
                    sTsrc = pb[6][:, 0:64].rearrange("p (s h) -> p s h", h=16)[
                        :, :, 2 * pr:2 * pr + 2]
                    if pass_id == 0:
                        l_sc = P.op("vector", (lambda sTdst, sTsrc: lambda e: e.tensor_copy(
                            sTdst, sTsrc))(sTdst, sTsrc),
                            waits=[l_tr], signal=L(tag))
                    else:
                        l_sc = P.op("vector", (lambda sTdst, sTsrc, pr: lambda e:
                            e.scalar_tensor_tensor(
                                out=sTdst, in0=sTdst,
                                scalar=s_bias[:, 16 + pr:17 + pr],
                                in1=sTsrc, op0=ALU.mult, op1=ALU.add))(sTdst, sTsrc, pr),
                            waits=[l_tr], signal=L(tag))
                    bank_owner[6] = l_sc
                    sums_last[0] = l_sc

        def attn_normalize():
            tag = "N"
            l_r = P.op("vector", lambda e: e.reciprocal(s_sumsT, s_sumsT),
                       waits=[sums_last[0]])
            for pr in range(NPAIR):
                for qc in range(4):
                    b = (pr * 4 + qc) % 4
                    asl = slice(pr * OWN + qc * 512, pr * OWN + (qc + 1) * 512)
                    src_r = s_sumsT.rearrange("p (s h) -> p s h", h=16)[
                        :, qc * 4:(qc + 1) * 4, 2 * pr:2 * pr + 2]
                    l_t = P.op("tensor", (lambda b, pr, qc: lambda e: [
                        e.transpose(pb[b][0:2, qs * 128:(qs + 1) * 128],
                                    s_sumsT[:, (qc * 4 + qs) * 16 + 2 * pr:
                                            (qc * 4 + qs) * 16 + 2 * pr + 2],
                                    c_identf)
                        for qs in range(4)][-1])(b, pr, qc),
                        waits=[l_r, l_idf, bank_owner.get(b)], signal=L(tag))
                    bank_owner[b] = l_t
                    l_c = P.op("vector", (lambda b: lambda e: e.tensor_copy(
                        s_sumb16[0:2, :], pb[b][0:2, :]))(b),
                        waits=[l_t], signal=L(tag))
                    bank_owner[b] = l_c
                    lm = P.op("tensor", (lambda b: lambda e: e.matmul(
                        pb[b], c_sel2[0:2, :], s_sumb16[0:2, :],
                        start=True, stop=True))(b),
                        waits=[l_c, l_sel2], signal=L(tag))
                    bank_owner[b] = lm
                    lt = P.op("vector", (lambda b, asl: lambda e: e.tensor_tensor(
                        out=a_aoT[:, asl], in0=a_aoT[:, asl], in1=pb[b],
                        op=ALU.mult))(b, asl),
                        waits=[lm], signal=L(tag))
                    bank_owner[b] = lt
            return lt

        import os as _os
        if not _os.environ.get("SKIP_A"):
            proj_pass(0)
            norm_sweep(0, "NSA")
            l_b = bias_build(0)
            if not _os.environ.get("SKIP_SIMA"):
                sim_pass(0, l_b)
        else:
            l_b = bias_build(0)
        if debug:
            P.op("sync", dma(dbg["kTA"][:], a_kT), waits=[l_b, sums_last[0]], out_store=True)
            ldbgA = P.op("sync", dma(dbg["vA"][:], a_v[:, 0:16 * VTW]),
                         waits=[l_b, sums_last[0]], out_store=True)
            P.raw_wait("sync", ldbgA)
        P.emit_segment("attnA")
        import os as _os
        proj_pass(1)
        norm_sweep(1, "NSB")
        l_b = bias_build(1)
        if not _os.environ.get("SKIP_SIMB"):
            sim_pass(1, l_b)
            l_fin = attn_normalize()
        else:
            l_fin = l_b
        if debug:
            P.op("sync", dma(dbg["kT"][:], a_kT), waits=[l_fin], out_store=True)
            P.op("sync", dma(dbg["qT"][:], a_qT), waits=[l_fin], out_store=True)
            P.op("sync", dma(dbg["v"][:], a_v[:, 0:16 * VTW]), waits=[l_fin], out_store=True)
            P.op("sync", dma(dbg["aoT"][:], a_aoT), waits=[l_fin], out_store=True)
            P.op("sync", dma(dbg["sums"][:], s_sumsT), waits=[l_fin], out_store=True)
            P.op("sync", dma(dbg["xnT"][:], a_xnT), waits=[l_fin], out_store=True)
            ldbg = P.op("sync", dma(dbg["bias"][:], s_bv[0:1, 0:32]), waits=[l_fin],
                        out_store=True)
            P.raw_wait("sync", ldbg)
        P.emit_segment("attnB")

        A.reset()
        A.mark()
        r_res = [A.f32(D) for _ in range(16)]
        f_w1 = A.bf16(8 * HID)
        r_wo = f_w1.bitcast(BF16)[:, 0:8 * D] if False else None
        r_wo = f_w1[:, 0:8 * D]
        f_ln2T = A.bf16(8 * OWN)
        f_w2 = A.bf16(16 * D)
        f_hT = A.bf16(16 * 512)
        f_comb = A.f32(128)
        f_combb = A.bf16(8)
        f_combT = A.bf16(OWN, rows=8)
        f_xn = A.bf16(D)
        f_g = A.f32(40)
        f_st = A.f32(16)
        f_mv = A.f32(4)
        f_rs = A.f32(8)
        assert A.off <= SBUF_BYTES, A.off

        bank_owner.clear()
        tag = "O"
        l_wo = P.dma("w1", "gsdma", dma(r_wo, wo[:]))
        res_labels = [None] * 16
        res_prev = [None] * 4
        warm_burst()
        for m in range(16):
            l_x = P.dma(f"rs{m % 4}", "sync",
                        dma(r_res[m][:], x_roll[m * 128:(m + 1) * 128, :]),
                        waits=[res_prev[m % 4]])
            for oh in range(2):
                b = (m * 2 + oh) % 4
                lm = P.op("tensor", (lambda m, oh, b: lambda e: [
                    e.matmul(pb[b],
                             a_aoT[:, c * OWN + m * 128: c * OWN + (m + 1) * 128],
                             r_wo[:, c * D + oh * 512: c * D + (oh + 1) * 512],
                             start=(c == 0), stop=(c == 7))
                    for c in range(8)][-1])(m, oh, b),
                    waits=[l_wo, bank_owner.get(b)], signal=L(tag))
                bank_owner[b] = lm
                le = P.op("vector", (lambda m, oh, b: lambda e: e.tensor_tensor(
                    out=r_res[m][:, oh * 512:(oh + 1) * 512],
                    in0=r_res[m][:, oh * 512:(oh + 1) * 512],
                    in1=pb[b], op=ALU.add))(m, oh, b),
                    waits=[lm, l_x], signal=L(tag))
                bank_owner[b] = le
                res_labels[m] = le
            res_prev[m % 4] = res_labels[m]
            oproj_last_mm = lm
        res_dump = [None] * 16
        if debug:
            for m in range(16):
                res_dump[m] = P.dma("dres", "sync",
                                    dma(dbg["res1"][m * 128:(m + 1) * 128, :], r_res[m][:]),
                                    waits=[res_labels[m]], serialize=True)

        tag = "F"
        ln_done = [None] * 16
        P.op("scalar", lambda e: e.preload_act_table(AF.Sqrt), signal=False)
        fxn_free = None
        for m in range(16):
            src = r_res[m][:]
            rsb = (m % 2) * 4
            l1 = P.chain("vector", [
                (lambda src: lambda e: (e.bn_stats(f_st[:, 0:6], src[:, 0:512]),
                                        e.bn_stats(f_st[:, 6:12], src[:, 512:1024]))[-1])(src),
                (lambda rsb: lambda e: e.bn_aggr(f_mv[:, 0:2], f_st[:, 0:12]))(rsb),
                (lambda rsb: lambda e: e.tensor_scalar(
                    out=f_rs[:, rsb:rsb + 1], in0=f_mv[:, 1:2], scalar1=EPS,
                    scalar2=None, op0=ALU.add))(rsb),
                (lambda rsb: lambda e: e.reciprocal(
                    f_rs[:, rsb:rsb + 1], f_rs[:, rsb:rsb + 1]))(rsb),
            ], waits=[res_labels[m]])
            l2 = P.op("scalar", (lambda rsb: lambda e: e.activation(
                out=f_rs[:, rsb + 1:rsb + 2], in_=f_rs[:, rsb:rsb + 1], func=AF.Sqrt))(rsb),
                waits=[l1], signal=L(tag))
            l3 = P.chain("vector", [
                (lambda rsb: lambda e: e.tensor_scalar(
                    out=f_rs[:, rsb + 2:rsb + 3], in0=f_mv[:, 0:1],
                    scalar1=f_rs[:, rsb + 1:rsb + 2], scalar2=-1.0,
                    op0=ALU.mult, op1=ALU.mult))(rsb),
                (lambda src, rsb: lambda e: e.tensor_scalar(
                    out=f_xn, in0=src, scalar1=f_rs[:, rsb + 1:rsb + 2],
                    scalar2=f_rs[:, rsb + 2:rsb + 3], op0=ALU.mult, op1=ALU.add))(src, rsb),
            ], waits=[l2] + ([fxn_free] if fxn_free else []))
            l_tr = P.op("tensor", lambda e: [
                e.transpose(pbT[:, d * 128:(d + 1) * 128],
                            f_xn[:, d * 128:(d + 1) * 128], c_ident)
                for d in range(8)][-1],
                waits=[l3, l_ident, bank_owner.get(7)], signal=L(tag))
            bank_owner[7] = l_tr
            fxn_free = l_tr
            l_ev = P.op("vector", (lambda m: lambda e: e.tensor_copy(
                f_ln2T.rearrange("p (d c) -> p d c", d=8)[:, :, m * 128:(m + 1) * 128],
                pbT.rearrange("p (d c) -> p d c", d=8)))(m),
                waits=[l_tr], signal=L(tag))
            bank_owner[7] = l_ev
            ln_done[m] = l_ev

        P.op("scalar", lambda e: e.preload_act_table(AF.Exp), signal=False)
        comb_ready = [None] * 16
        combT_ready = [None] * 16
        for m in range(16):
            b = m % 2
            lm = P.op("tensor", (lambda m, b: lambda e: (
                [e.matmul(pb[b][:, 0:8],
                          f_ln2T[:, d * OWN + m * 128: d * OWN + (m + 1) * 128],
                          c_wg[:, d * 8:(d + 1) * 8], start=(d == 0), stop=False)
                 for d in range(8)],
                e.matmul(pb[b][:, 0:8], c_onesrow, c_bg, start=False, stop=True))[1])(m, b),
                waits=[l_wg, l_bg, l_orow, ln_done[m], bank_owner.get(b)], signal=L(tag))
            bank_owner[b] = lm
            l_g1 = P.chain("vector", [
                (lambda b: lambda e: e.tensor_copy(f_g[:, 0:8], pb[b][:, 0:8]))(b),
                lambda e: e.tensor_reduce(f_g[:, 8:9], f_g[:, 0:8], axis=AX.X, op=ALU.max),
                lambda e: e.tensor_scalar(out=f_g[:, 16:24], in0=f_g[:, 0:8],
                                          scalar1=f_g[:, 8:9], scalar2=None,
                                          op0=ALU.is_equal),
                lambda e: e.scalar_tensor_tensor(out=f_g[:, 24:32], in0=f_g[:, 16:24],
                                                 scalar=-1e30, in1=f_g[:, 0:8],
                                                 op0=ALU.mult, op1=ALU.add),
                lambda e: e.tensor_reduce(f_g[:, 9:10], f_g[:, 24:32], axis=AX.X,
                                          op=ALU.max),
                lambda e: e.tensor_scalar(out=f_g[:, 10:11], in0=f_g[:, 8:9],
                                          scalar1=-1.0, scalar2=None, op0=ALU.mult),
            ], waits=[lm])
            bank_owner[b] = l_g1
            l_g2 = P.op("scalar", lambda e: e.activation(
                out=f_g[:, 11:12], in_=f_g[:, 9:10], func=AF.Exp, bias=f_g[:, 10:11]),
                waits=[l_g1], signal=L(tag))
            l_g3 = P.chain("vector", [
                lambda e: e.tensor_scalar(out=f_g[:, 12:13], in0=f_g[:, 11:12],
                                          scalar1=1.0, scalar2=None, op0=ALU.add),
                lambda e: e.reciprocal(f_g[:, 12:13], f_g[:, 12:13]),
                lambda e: e.tensor_tensor(out=f_g[:, 13:14], in0=f_g[:, 11:12],
                                          in1=f_g[:, 12:13], op=ALU.mult),
                lambda e: e.tensor_scalar(out=f_g[:, 24:32], in0=f_g[:, 0:8],
                                          scalar1=f_g[:, 9:10], scalar2=None,
                                          op0=ALU.is_equal),
                lambda e: e.tensor_scalar(out=f_g[:, 32:40], in0=f_g[:, 16:24],
                                          scalar1=-1.0, scalar2=1.0,
                                          op0=ALU.mult, op1=ALU.add),
                lambda e: e.tensor_tensor(out=f_g[:, 24:32], in0=f_g[:, 24:32],
                                          in1=f_g[:, 32:40], op=ALU.mult),
                lambda e: e.tensor_scalar(out=f_g[:, 16:24], in0=f_g[:, 16:24],
                                          scalar1=f_g[:, 12:13], scalar2=None,
                                          op0=ALU.mult),
                lambda e: e.tensor_scalar(out=f_g[:, 24:32], in0=f_g[:, 24:32],
                                          scalar1=f_g[:, 13:14], scalar2=None,
                                          op0=ALU.mult),
                (lambda m: lambda e: e.tensor_tensor(
                    out=f_comb[:, m * 8:(m + 1) * 8], in0=f_g[:, 16:24],
                    in1=f_g[:, 24:32], op=ALU.add))(m),
                (lambda m: lambda e: e.tensor_copy(
                    f_combb[:, 0:8], f_comb[:, m * 8:(m + 1) * 8]))(m),
            ], waits=[l_g2])
            comb_ready[m] = l_g3
            l_tc = P.op("tensor", lambda e: e.transpose(pbT[0:8, 0:128], f_combb[:, 0:8], c_ident),
                        waits=[l_g3, l_ident, bank_owner.get(7)], signal=L(tag))
            bank_owner[7] = l_tc
            l_te = P.op("vector", (lambda m: lambda e: e.tensor_copy(
                f_combT[0:8, m * 128:(m + 1) * 128], pbT[0:8, 0:128]))(m),
                waits=[l_tc], signal=L(tag))
            bank_owner[7] = l_te
            combT_ready[m] = l_te
        if debug:
            P.op("sync", dma(dbg["comb"][:], f_comb), waits=comb_ready, out_store=True)
            ldbg = P.op("sync", dma(dbg["ln2T"][:], f_ln2T), waits=ln_done, out_store=True)
            P.raw_wait("sync", ldbg)

        mm1_done = None
        mm2_done = None
        relu_done = {}
        store_done = [None] * 16
        for ex in range(NEXP):
            w1_ready = P.dma("w1", "gsdma", (lambda ex: lambda e: e.dma_start(
                out=f_w1, in_=w1t[ex]))(ex),
                waits=[mm1_done if ex else oproj_last_mm])
            w2_ready = P.dma("w2", "gsdma", (lambda ex: lambda e: e.dma_start(
                out=f_w2, in_=w2t[ex]))(ex), waits=[mm2_done])
            for tc in range(4):
                for hf in range(16):
                    b = hf % 4
                    wmm = [w1_ready, bank_owner.get(b)]
                    if ex == 0 and tc == 0 and hf < 4:
                        wmm += ln_done
                    lm = P.op("tensor", (lambda ex, tc, hf, b: lambda e: [
                        e.matmul(pb[b],
                                 f_w1[:, d * HID + hf * 128: d * HID + (hf + 1) * 128],
                                 f_ln2T[:, d * OWN + tc * 512: d * OWN + (tc + 1) * 512],
                                 start=(d == 0), stop=(d == 7))
                        for d in range(8)][-1])(ex, tc, hf, b),
                        waits=wmm, signal=L(tag))
                    bank_owner[b] = lm
                    lr = P.op("scalar", (lambda ex, hf, b: lambda e: e.activation(
                        out=f_hT[:, hf * 512:(hf + 1) * 512], in_=pb[b], func=AF.Relu,
                        bias=c_b1[:, ex * 16 + hf: ex * 16 + hf + 1]))(ex, hf, b),
                        waits=[lm, l_b1, mm2_done], signal=L(tag))
                    bank_owner[b] = lr
                    relu_done[hf] = lr
                mm1_done = relu_done[15]
                for m4 in range(4):
                    m = tc * 4 + m4
                    for oh in range(2):
                        b = 4 + (m4 * 2 + oh) % 4
                        wmm = [w2_ready, bank_owner.get(b)] + \
                              [relu_done[hf] for hf in range(16)]
                        if ex == 0:
                            wmm += [combT_ready[m], l_b2]
                        def mk_mm2(ex, m4, m, oh, b):
                            def f(e):
                                last = None
                                for hf in range(16):
                                    last = e.matmul(
                                        pb[b],
                                        f_hT[:, hf * 512 + m4 * 128: hf * 512 + (m4 + 1) * 128],
                                        f_w2[:, hf * D + oh * 512: hf * D + (oh + 1) * 512],
                                        start=(hf == 0), stop=(hf == 15 and ex != 0))
                                if ex == 0:
                                    last = e.matmul(
                                        pb[b], f_combT[0:8, m * 128:(m + 1) * 128],
                                        c_b2[0:8, oh * 512:(oh + 1) * 512],
                                        start=False, stop=True)
                                return last
                            return f
                        lm = P.op("tensor", mk_mm2(ex, m4, m, oh, b), waits=wmm,
                                  signal=L(tag))
                        bank_owner[b] = lm
                        le = P.op("vector", (lambda ex, m, oh, b: lambda e:
                            e.scalar_tensor_tensor(
                                out=r_res[m][:, oh * 512:(oh + 1) * 512],
                                in0=pb[b],
                                scalar=f_comb[:, m * 8 + ex: m * 8 + ex + 1],
                                in1=r_res[m][:, oh * 512:(oh + 1) * 512],
                                op0=ALU.mult, op1=ALU.add))(ex, m, oh, b),
                            waits=[lm, comb_ready[m], store_done[m]] +
                                  ([res_dump[m]] if ex == 0 else []),
                            signal=L(tag))
                        bank_owner[b] = le
                        res_labels[m] = le
                mm2_done = bank_owner[4 + 7 % 4]
            if ex == NEXP - 1:
                for m in range(16):
                    store_done[m] = P.op("sync", (lambda m: lambda e: e.dma_start(
                        out=out[m * 128:(m + 1) * 128, :], in_=r_res[m][:]))(m),
                        waits=[res_labels[m]], out_store=True)
        P.raw_wait("sync", store_done[15])
        P.emit_segment("ffn")

    return nc


def _host_constants():
    i = np.arange(0, DH, 2, dtype=np.float64) / DH
    theta = 1.0 / (10000.0 ** i)
    pos = np.arange(SEQ, dtype=np.float64)
    ang = pos[None, :] * theta[(np.arange(128) % 32)][:, None]
    cosfm = np.cos(ang).astype(np.float32)
    sinfm = np.sin(ang).astype(np.float32)
    identb = np.eye(128, dtype=np.float32)
    rot = np.zeros((128, 128), dtype=np.float32)
    for k in range(64):
        rot[2 * k + 1, 2 * k] = -1.0
        rot[2 * k, 2 * k + 1] = 1.0
    sel2 = np.zeros((2, 128), dtype=np.float32)
    sel2[0, 0:64] = 1.0
    sel2[1, 64:128] = 1.0
    bf = lambda a: np.ascontiguousarray(a.astype(ml_dtypes.bfloat16))
    return dict(costab=bf(cosfm), sintab=bf(sinfm), identb=bf(identb),
                rotmat=bf(rot), ones128b=bf(np.ones((128, 128), np.float32)),
                onescol_b=bf(np.ones((128, 1), np.float32)),
                onesrow_f=np.ones((1, 128), np.float32),
                sel2=bf(sel2), identf=identb.copy())


def make_in_maps(L0_attn_x, L0_attn_gamma, L0_attn_Wq, L0_attn_Wkv, L0_attn_W_bilinear,
                 L0_attn_Wo, L0_ffn_W1, L0_ffn_b1, L0_ffn_W2, L0_ffn_b2, L0_ffn_Wg,
                 L0_ffn_bg):
    C = np.ascontiguousarray
    x = np.asarray(L0_attn_x, dtype=np.float32)
    consts = _host_constants()
    A = np.asarray(L0_attn_W_bilinear, np.float32).transpose(0, 2, 1)
    wb_half = C(A.transpose(1, 0, 2).reshape(64, H * DH))
    wbt2 = np.concatenate([wb_half, wb_half], axis=0)
    shared = dict(
        gamma=C(np.asarray(L0_attn_gamma, np.float32)[None, :]),
        wq=C(np.asarray(L0_attn_Wq, np.float32)),
        wkv=C(np.asarray(L0_attn_Wkv, np.float32)),
        wbt2=C(wbt2),
        wo=C(np.asarray(L0_attn_Wo, np.float32).reshape(8, 128, D)
             .transpose(1, 0, 2).reshape(128, 8 * D)),
        w1t=C(np.asarray(L0_ffn_W1, np.float32).transpose(0, 2, 1)
              .reshape(NEXP, 8, 128, HID).transpose(0, 2, 1, 3)
              .reshape(NEXP, 128, 8 * HID)),
        w2t=C(np.asarray(L0_ffn_W2, np.float32).transpose(0, 2, 1)
              .reshape(NEXP, 16, 128, D).transpose(0, 2, 1, 3)
              .reshape(NEXP, 128, 16 * D)),
        b1c=C(np.asarray(L0_ffn_b1, np.float32).reshape(NEXP, HID // 128, 128)
              .transpose(2, 0, 1).reshape(128, NEXP * (HID // 128))),
        b2=C(np.asarray(L0_ffn_b2, np.float32)),
        wgt=C(np.asarray(L0_ffn_Wg, np.float32).T.reshape(8, 128, NEXP)
              .transpose(1, 0, 2).reshape(128, 8 * NEXP)),
        bg=C(np.asarray(L0_ffn_bg, np.float32)[None, :]),
        **consts,
    )
    in_maps = []
    for core in range(8):
        s, o = core // 2, core % 2
        xo = x[s, o * OWN:(o + 1) * OWN]
        xr = x[s, (1 - o) * OWN:(2 - o) * OWN]
        m = dict(shared)
        m["x_roll"] = C(np.concatenate([xo, xr], axis=0))
        own_sl = slice(o * OWN, (o + 1) * OWN)
        oth_sl = slice((1 - o) * OWN, (2 - o) * OWN)
        m["costab"] = C(np.concatenate([consts["costab"][:, own_sl],
                                        consts["costab"][:, oth_sl]], axis=1))
        m["sintab"] = C(np.concatenate([consts["sintab"][:, own_sl],
                                        consts["sintab"][:, oth_sl]], axis=1))
        in_maps.append(m)
    return in_maps


_NC_CACHE = {}


def kernel(**inputs):
    if "main" not in _NC_CACHE:
        _NC_CACHE["main"] = build_nc(debug=False)
    nc = _NC_CACHE["main"]
    in_maps = make_in_maps(**inputs)
    res = run_bass_kernel_spmd(nc, in_maps, core_ids=list(range(8)))
    outp = np.empty((4, SEQ, D), dtype=np.float32)
    for core in range(8):
        s, o = core // 2, core % 2
        outp[s, o * OWN:(o + 1) * OWN] = res.results[core]["out"]
    return outp


# version 10 (final)
# speedup vs baseline: 1.6503x; 1.0042x over previous
import numpy as np
import ml_dtypes
from contextlib import ExitStack

import concourse.bass as bass
import concourse.mybir as mybir
from concourse.bass_utils import run_bass_kernel_spmd

F32 = mybir.dt.float32
BF16 = mybir.dt.bfloat16
AF = mybir.ActivationFunctionType
ALU = mybir.AluOpType
AX = mybir.AxisListType

SEQ = 4096
OWN = 2048
CH = 512
NCH = OWN // CH
D = 1024
H = 16
DH = 64
NPAIR = 8
HID = 2048
NEXP = 8
EPS = 1e-5
SHIFT = 40.0

SBUF_BYTES = 207 * 1024
VTW = H * (DH + 1)


class Arena:

    def __init__(self, nc, st):
        self.words = SBUF_BYTES // 4
        self.t = st.enter_context(nc.sbuf_tensor("arena", [128, self.words], F32))
        self.off = 0
        self.marks = []

    def _alloc(self, bytes_pp):
        off = (self.off + 511) // 512 * 512
        assert off + bytes_pp <= SBUF_BYTES, f"arena overflow: {off}+{bytes_pp}"
        self.off = off + bytes_pp
        return off

    def f32(self, cols, rows=128):
        off = self._alloc(cols * 4)
        return self.t[0:rows, off // 4: off // 4 + cols]

    def bf16(self, cols, rows=128):
        off = self._alloc(cols * 2 + (cols * 2) % 4)
        nw = (cols * 2 + 3) // 4
        return self.t[0:rows, off // 4: off // 4 + nw].bitcast(BF16)[:, 0:cols]

    def mark(self):
        self.marks.append(self.off)

    def reset(self):
        self.off = self.marks.pop()


class Prog:
    QUEUES = ("sync", "gsdma", "tensor", "vector", "scalar")
    ENG = ("tensor", "vector", "scalar")

    def __init__(self, nc, st):
        self.nc = nc
        self.st = st
        self.sems = {q: st.enter_context(nc.semaphore("sem_" + q)) for q in self.ENG}
        self.sems["out"] = st.enter_context(nc.semaphore("sem_out"))
        self.ops = []
        self.labels = {}
        self.counts = {q: 0 for q in self.ENG}
        self.counts["out"] = 0
        self.watermarks = {}
        self.uid = 0

    def lab(self, tag="l"):
        self.uid += 1
        return f"{tag}{self.uid}"

    def _chan(self, name):
        if name not in self.sems:
            self.sems[name] = self.st.enter_context(self.nc.semaphore("sem_" + name))
            self.counts[name] = 0
        return name

    def op(self, queue, fn, waits=(), signal=None, out_store=False):
        if out_store:
            return self.dma("out", "sync", fn, waits=waits, signal=signal)
        assert queue in self.ENG, queue
        if signal is False:
            self.ops.append((queue, fn, tuple(w for w in waits if w), None, queue))
            return None
        if signal is None:
            signal = self.lab(queue)
        self.counts[queue] += 1
        assert signal not in self.labels, signal
        self.labels[signal] = (queue, self.counts[queue])
        self.ops.append((queue, fn, tuple(w for w in waits if w), signal, queue))
        return signal

    def dma(self, channel, queue, fn, waits=(), signal=None, serialize=False):
        assert queue in ("sync", "gsdma")
        ch = self._chan(channel)
        if signal is None:
            signal = self.lab(channel)
        self.counts[ch] += 16
        assert signal not in self.labels, signal
        self.labels[signal] = (ch, self.counts[ch])
        self.ops.append((queue, fn, tuple(w for w in waits if w), signal, ch))
        if serialize:
            self.raw_wait(queue, signal)
        return signal

    def chain(self, queue, fns, waits=()):
        lb = None
        for i, fn in enumerate(fns):
            w = list(waits) if i == 0 else [lb]
            lb = self.op(queue, fn, waits=w)
        return lb

    def raw_wait(self, queue, label):
        self.ops.append((queue, None, (label,), None, None))

    def emit_segment(self, name):
        per_q = {q: [] for q in self.QUEUES}
        for rec in self.ops:
            per_q[rec[0]].append(rec)
        self.ops = []
        active = [q for q in self.QUEUES if per_q[q]]

        with self.nc.Block(name) as blk:
            for q in active:
                def mk(qq):
                    def body(eng):
                        for (_, fn, waits, signal, semkey) in per_q[qq]:
                            for labw in waits:
                                pq, cnt = self.labels[labw]
                                key = (qq, pq)
                                if self.watermarks.get(key, 0) < cnt:
                                    eng.wait_ge(self.sems[pq], cnt)
                                    self.watermarks[key] = cnt
                            if fn is None:
                                continue
                            instr = fn(eng)
                            if signal is not None:
                                instr.then_inc(self.sems[semkey],
                                               16 if semkey not in self.ENG else 1)
                    return body
                getattr(blk, "gpsimd" if q == "gsdma" else q)(mk(q))
        for q in self.QUEUES:
            for pq in list(self.counts):
                self.watermarks[(q, pq)] = self.counts[pq]


def build_nc(debug=False):
    nc = bass.Bass("TRN2", target_bir_lowering=False)
    dt = nc.dram_tensor

    x_roll = dt("x_roll", [SEQ, D], F32, kind="ExternalInput").ap()
    gamma = dt("gamma", [1, D], F32, kind="ExternalInput").ap()
    wq = dt("wq", [D, D], F32, kind="ExternalInput").ap()
    wkv = dt("wkv", [D, 2 * D], F32, kind="ExternalInput").ap()
    wbt2 = dt("wbt2", [128, H * DH], F32, kind="ExternalInput").ap()
    wo = dt("wo", [128, 8 * D], F32, kind="ExternalInput").ap()
    w1t = dt("w1t", [NEXP, 128, 8 * HID], F32, kind="ExternalInput").ap()
    w2t = dt("w2t", [NEXP, 128, 16 * D], F32, kind="ExternalInput").ap()
    b1c = dt("b1c", [128, NEXP * (HID // 128)], F32, kind="ExternalInput").ap()
    b2 = dt("b2", [NEXP, D], F32, kind="ExternalInput").ap()
    wgt = dt("wgt", [128, 8 * NEXP], F32, kind="ExternalInput").ap()
    bg = dt("bg", [1, NEXP], F32, kind="ExternalInput").ap()
    costab = dt("costab", [128, SEQ], BF16, kind="ExternalInput").ap()
    sintab = dt("sintab", [128, SEQ], BF16, kind="ExternalInput").ap()
    identb = dt("identb", [128, 128], BF16, kind="ExternalInput").ap()
    rotmat = dt("rotmat", [128, 128], BF16, kind="ExternalInput").ap()
    ones128b = dt("ones128b", [128, 128], BF16, kind="ExternalInput").ap()
    onescol_b = dt("onescol_b", [128, 1], BF16, kind="ExternalInput").ap()
    onesrow_f = dt("onesrow_f", [1, 128], F32, kind="ExternalInput").ap()
    sel2 = dt("sel2", [2, 128], BF16, kind="ExternalInput").ap()
    identf = dt("identf", [128, 128], F32, kind="ExternalInput").ap()

    out = dt("out", [OWN, D], F32, kind="ExternalOutput").ap()
    dbg = {}
    if debug:
        for nm, shp, dtp in [
            ("kT", [128, 8 * OWN], BF16), ("qT", [128, 8 * OWN], BF16),
            ("v", [128, 16 * VTW], BF16), ("aoT", [128, 8 * OWN], BF16),
            ("res1", [OWN, D], F32), ("comb", [128, 128], F32),
            ("kTA", [128, 8 * OWN], BF16), ("vA", [128, 16 * VTW], BF16),
            ("xnT", [128, 8 * CH], BF16),
            ("sums", [128, 256], F32), ("ln2T", [128, 8 * OWN], BF16),
            ("bias", [1, 32], F32),
        ]:
            dbg[nm] = dt("dbg_" + nm, shp, dtp, kind="ExternalOutput").ap()

    with ExitStack() as st:
        P = Prog(nc, st)
        A = Arena(nc, st)
        PB = st.enter_context(nc.psum_tensor("PB", [128, 4096], F32))
        pb = [PB[:, i * 512:(i + 1) * 512] for i in range(8)]
        pbT = PB[:, 3584:4096].bitcast(BF16)
        L = P.lab
        bank_owner = {}

        def dma(dst, src):
            return lambda e: e.dma_start(out=dst, in_=src)

        c_ident = A.bf16(128)
        c_rot = A.bf16(128)
        c_ones128b = A.bf16(128)
        c_onescol = A.bf16(1)
        c_onesrow = A.f32(128, rows=1)
        c_wbt = A.bf16(H * DH)
        c_b1 = A.f32(NEXP * (HID // 128))
        c_b2 = A.bf16(D, rows=NEXP)
        c_wg = A.bf16(8 * NEXP)
        c_bg = A.f32(NEXP, rows=1)
        c_sel2 = A.bf16(128, rows=2)
        c_identf = A.f32(128)
        s_bias = A.f32(32)
        s_bv = A.f32(32, rows=1)
        s_mx = A.f32(32, rows=1)

        A.mark()
        a_x = [A.f32(D) for _ in range(2)]
        a_ln1 = A.f32(D)
        a_xn = [A.bf16(D) for _ in range(2)]
        a_st = A.f32(16)
        a_mv = A.f32(4)
        a_rs = A.f32(8)
        a_xnT = A.bf16(8 * CH)
        a_wstg = [A.bf16(4 * D) for _ in range(2)]
        a_scr = A.f32(2048)
        a_kraw = [a_scr[:, 0:256].bitcast(BF16), a_scr[:, 256:512].bitcast(BF16)]
        a_rt1 = [a_scr[:, 512:768].bitcast(BF16), a_scr[:, 768:1024].bitcast(BF16)]
        a_rt2 = [a_scr[:, 1024:1280].bitcast(BF16), a_scr[:, 1280:1536].bitcast(BF16)]
        a_qhat = a_scr[:, 1536:1792].bitcast(BF16)
        a_sq = a_scr[:, 1792:2048].bitcast(BF16)
        a_gbc = A.bf16(D)
        a_kT = A.bf16(8 * OWN)
        a_v = A.bf16(16 * VTW + 64)
        s_r64 = a_scr[:, 0:1024]
        a_qT = A.bf16(8 * OWN)
        a_aoT = A.bf16(8 * OWN)
        a_ptw = A.bf16(2048)
        a_pt2 = [a_ptw[:, 0:1024], a_ptw[:, 1024:2048]]
        a_cos = A.bf16(OWN)
        a_sin = A.bf16(OWN)
        s_sumsT = A.f32(256)
        s_sumb16 = A.bf16(512, rows=2)
        assert A.off <= SBUF_BYTES, A.off

        l_ident = P.dma("cs", "sync", dma(c_ident, identb[:]), serialize=True)
        l_rot = P.dma("cs", "sync", dma(c_rot, rotmat[:]), serialize=True)
        l_o128 = P.dma("cs", "sync", dma(c_ones128b, ones128b[:]), serialize=True)
        l_ocol = P.dma("cs", "sync", dma(c_onescol, onescol_b[:]), serialize=True)
        l_orow = P.dma("cs", "sync", dma(c_onesrow, onesrow_f[:]), serialize=True)
        l_bg = P.dma("cs", "sync", dma(c_bg, bg[:]), serialize=True)
        l_b1 = P.dma("cs", "sync", dma(c_b1, b1c[:]), serialize=True)
        l_gam = P.dma("cs", "sync", dma(a_x[0][0:1, 0:D], gamma[:]), serialize=True)
        l_wbt = P.dma("cg", "gsdma", dma(c_wbt, wbt2[:]), serialize=True)
        l_b2 = P.dma("cg", "gsdma", dma(c_b2, b2[:]), serialize=True)
        l_wg = P.dma("cg", "gsdma", dma(c_wg, wgt[:]), serialize=True)
        l_sel2 = P.dma("cg", "gsdma", dma(c_sel2, sel2[:]), serialize=True)
        l_idf = P.dma("cg", "gsdma", dma(c_identf, identf[:]), serialize=True)
        P.op("scalar", lambda e: e.activation(out=c_wbt, in_=c_wbt, func=AF.Copy, scale=0.125),
             waits=[l_wbt], signal="wbt_ok")
        for i in range(2):
            P.op("tensor", (lambda i: lambda e: e.matmul(
                pb[i], c_onesrow, a_x[0][0:1, i * 512:(i + 1) * 512],
                start=True, stop=True))(i),
                waits=[l_orow, l_gam], signal=f"gbcmm{i}")
            P.op("vector", (lambda i: lambda e: e.tensor_copy(
                a_gbc[:, i * 512:(i + 1) * 512], pb[i]))(i),
                waits=[f"gbcmm{i}"], signal=f"gbc{i}")
        P.op("vector", lambda e: e.memset(s_mx[0:1, :], -1e30), signal="mxinit")
        for tt in range(16):
            P.op("vector", (lambda tt: lambda e: e.memset(
                a_v[:, tt * VTW:(tt + 1) * VTW].rearrange(
                    "p (h u) -> p h u", u=DH + 1)[:, :, DH:DH + 1], 1.0))(tt),
                signal=f"vone{tt}")
        P.op("vector", lambda e: e.memset(a_v[:, 16 * VTW:16 * VTW + 64], 1.0),
             signal="vpad")
        P.op("scalar", lambda e: e.preload_act_table(AF.Sqrt), signal=False)
        P.emit_segment("s0")

        def ln_block(src, dst, deps, tag, parity, gamma_mul):
            rsb = parity * 4
            l1 = P.chain("vector", [
                lambda e: (e.bn_stats(a_st[:, 0:6], src[:, 0:512]),
                           e.bn_stats(a_st[:, 6:12], src[:, 512:1024]))[-1],
                lambda e: e.bn_aggr(a_mv[:, 0:2], a_st[:, 0:12]),
                lambda e: e.tensor_scalar(out=a_rs[:, rsb:rsb + 1], in0=a_mv[:, 1:2],
                                          scalar1=EPS, scalar2=None, op0=ALU.add),
                lambda e: e.reciprocal(a_rs[:, rsb:rsb + 1], a_rs[:, rsb:rsb + 1]),
            ], waits=deps)
            l2 = P.op("scalar", lambda e: e.activation(
                out=a_rs[:, rsb + 1:rsb + 2], in_=a_rs[:, rsb:rsb + 1], func=AF.Sqrt),
                waits=[l1])
            l2b = P.op("vector", lambda e: e.tensor_scalar(
                out=a_rs[:, rsb + 2:rsb + 3], in0=a_mv[:, 0:1],
                scalar1=a_rs[:, rsb + 1:rsb + 2], scalar2=-1.0,
                op0=ALU.mult, op1=ALU.mult), waits=[l2])
            l2c = P.op("scalar", lambda e: e.activation(
                out=a_ln1[:], in_=src, func=AF.Identity,
                scale=a_rs[:, rsb + 1:rsb + 2], bias=a_rs[:, rsb + 2:rsb + 3]),
                waits=[l2b])
            l3 = P.chain("vector", [
                lambda e: (e.bn_stats(a_st[:, 0:6], a_ln1[:, 0:512]),
                           e.bn_stats(a_st[:, 6:12], a_ln1[:, 512:1024]))[-1],
                lambda e: e.bn_aggr(a_mv[:, 2:4], a_st[:, 0:12]),
                lambda e: e.tensor_scalar(out=a_rs[:, rsb:rsb + 1], in0=a_mv[:, 3:4],
                                          scalar1=EPS, scalar2=None, op0=ALU.add),
                lambda e: e.reciprocal(a_rs[:, rsb:rsb + 1], a_rs[:, rsb:rsb + 1]),
            ], waits=[l2c])
            l4 = P.op("scalar", lambda e: e.activation(
                out=a_rs[:, rsb + 1:rsb + 2], in_=a_rs[:, rsb:rsb + 1], func=AF.Sqrt),
                waits=[l3])
            l4b = P.op("vector", lambda e: e.tensor_scalar(
                out=a_rs[:, rsb + 2:rsb + 3], in0=a_mv[:, 2:3],
                scalar1=a_rs[:, rsb + 1:rsb + 2], scalar2=-1.0,
                op0=ALU.mult, op1=ALU.mult), waits=[l4])
            l4c = P.op("scalar", lambda e: e.activation(
                out=dst, in_=a_ln1[:], func=AF.Identity,
                scale=a_rs[:, rsb + 1:rsb + 2], bias=a_rs[:, rsb + 2:rsb + 3]),
                waits=[l4b] + (gamma_mul or []))
            l5 = P.op("vector", lambda e: e.tensor_tensor(
                out=dst, in0=dst, in1=a_gbc, op=ALU.mult),
                waits=[l4c, "gbc0", "gbc1"])
            return l5

        kraw_free = [None, None]

        def bcast4(ap512):
            return ap512.unsqueeze(1).broadcast_to((128, 4, 512))

        def rope4(g, toksl, tag, l_cos, l_sin, kdst_tc=None):
            kraw_w = a_ptw
            rt1_w = a_scr[:, 0:1024].bitcast(BF16)
            rt2_w = a_scr[:, 1024:2048].bitcast(BF16)
            grp = PB[:, g * 2048:(g + 1) * 2048]
            wev = [bank_owner.get(g * 4), bank_owner.get(9), bank_owner.get(10)]
            l_ev = P.op("scalar", lambda e: e.activation(out=kraw_w, in_=grp,
                                                         func=AF.Copy),
                        waits=wev, signal=L(tag))
            l_ro = P.op("tensor", lambda e: [
                e.matmul(pb[g * 4 + i], c_rot, kraw_w[:, i * 512:(i + 1) * 512],
                         start=True, stop=True) for i in range(4)][-1],
                waits=[l_ev, l_rot], signal=L(tag))
            bank_owner[10] = l_ro
            l_t1 = P.op("vector", lambda e: e.tensor_tensor(
                out=rt1_w.rearrange("p (c t) -> p c t", t=512),
                in0=kraw_w.rearrange("p (c t) -> p c t", t=512),
                in1=bcast4(a_cos[:, toksl]), op=ALU.mult),
                waits=[l_ev, l_cos], signal=L(tag))
            l_t2 = P.op("vector", lambda e: e.tensor_tensor(
                out=rt2_w.rearrange("p (c t) -> p c t", t=512),
                in0=grp.rearrange("p (c t) -> p c t", t=512),
                in1=bcast4(a_sin[:, toksl]), op=ALU.mult),
                waits=[l_ro, l_t1, l_sin], signal=L(tag))
            for i in range(4):
                bank_owner[g * 4 + i] = l_t2
            if kdst_tc is not None:
                dst = a_kT.rearrange("p (kf t) -> p kf t", t=OWN)[
                    :, g * 4:(g + 1) * 4, kdst_tc * CH:(kdst_tc + 1) * CH]
                l_add = P.op("vector", lambda e: e.tensor_tensor(
                    out=dst, in0=rt1_w.rearrange("p (c t) -> p c t", t=512),
                    in1=rt2_w.rearrange("p (c t) -> p c t", t=512), op=ALU.add),
                    waits=[l_t2], signal=L(tag))
            else:
                l_add = P.op("vector", lambda e: e.tensor_tensor(
                    out=rt1_w, in0=rt1_w, in1=rt2_w, op=ALU.add),
                    waits=[l_t2], signal=L(tag))
            bank_owner[9] = l_add
            return l_add, l_t2

        def warm_burst(n=24):
            def f(e):
                r = None
                for _ in range(n):
                    r = e.matmul(pb[7][0:1, :], c_ones128b[0:1, 0:1],
                                 c_b2[0:1, 0:512], start=True, stop=True)
                return r
            P.op("tensor", f, signal=False)

        prev_sq = [None]
        mx_last = [None]

        def normsq(src_ap, bank, mslot, deps, tag):
            l_sq = P.op("vector", lambda e: e.tensor_tensor(
                out=a_sq, in0=src_ap, in1=src_ap, op=ALU.mult),
                waits=list(deps) + ([prev_sq[0]] if prev_sq[0] else []), signal=L(tag))
            l_mm = P.op("tensor", lambda e: e.matmul(pb[bank][0:1, :], c_onescol, a_sq,
                                                     start=True, stop=True),
                        waits=[l_sq, l_ocol, bank_owner.get(bank)], signal=L(tag))
            prev_sq[0] = l_mm
            l_mx = P.chain("vector", [
                lambda e: e.tensor_reduce(s_mx[0:1, 31:32], pb[bank][0:1, :],
                                          axis=AX.X, op=ALU.max),
                lambda e: e.tensor_tensor(out=s_mx[0:1, mslot:mslot + 1],
                                          in0=s_mx[0:1, mslot:mslot + 1],
                                          in1=s_mx[0:1, 31:32], op=ALU.max),
            ], waits=[l_mm, "mxinit"])
            bank_owner[bank] = l_mx
            mx_last[0] = l_mx
            return l_mx

        def proj_pass(pass_id):
            tag = f"P{pass_id}"
            base = pass_id * OWN
            if pass_id == 1:
                P.op("scalar", lambda e: e.preload_act_table(AF.Sqrt), signal=False)
            l_cos = P.dma("tcos", "sync", dma(a_cos, costab[:, base:base + OWN]))
            l_sin = P.dma("tsin", "sync", dma(a_sin, sintab[:, base:base + OWN]))
            warm_burst()
            wstg_free = [None, None]
            x_free = [None, None]
            xn_free = [None, None]
            xnT_free = None
            qhat_free = None
            for tc in range(NCH):
                xn_labels = []
                tr_labels = []
                for t4 in range(4):
                    tile = tc * 4 + t4
                    xb = a_x[t4 % 2]
                    l_x = P.dma(f"x{t4 % 2}", "sync",
                                dma(xb[:], x_roll[base + tile * 128: base + tile * 128 + 128, :]),
                                waits=[x_free[t4 % 2]])
                    xn = a_xn[t4 % 2]
                    l_xn = ln_block(xb[:], xn[:], [l_x], tag, t4 % 2,
                                    gamma_mul=[xn_free[t4 % 2]] if xn_free[t4 % 2] else [])
                    x_free[t4 % 2] = l_xn
                    wtr = [l_xn, l_ident, bank_owner.get(7)]
                    l_tr = P.op("tensor", (lambda xn: lambda e: [
                        e.transpose(pbT[:, d * 128:(d + 1) * 128],
                                    xn[:, d * 128:(d + 1) * 128], c_ident)
                        for d in range(8)][-1])(xn),
                        waits=wtr, signal=L(tag))
                    bank_owner[7] = l_tr
                    xn_free[t4 % 2] = l_tr
                    tr_labels.append(l_tr)
                    l_ev = P.op("vector", (lambda t4: lambda e: e.tensor_copy(
                        a_xnT.rearrange("p (d c) -> p d c", d=8)[:, :, t4 * 128:(t4 + 1) * 128],
                        pbT.rearrange("p (d c) -> p d c", d=8)))(t4),
                        waits=[l_tr] + ([xnT_free] if (xnT_free and t4 == 0) else []),
                        signal=L(tag))
                    bank_owner[7] = l_ev
                    xn_labels.append(l_ev)

                def wload(which, half):
                    sbuf = a_wstg[half]
                    r0 = half * 512
                    if which == "k":
                        src = wkv[r0:r0 + 512, 0:D]
                    elif which == "v":
                        src = wkv[r0:r0 + 512, D:2 * D]
                    else:
                        src = wq[r0:r0 + 512, 0:D]
                    src = src.rearrange("(c p) f -> p c f", p=128)
                    dst = sbuf.rearrange("p (c f) -> p c f", f=D)
                    l_w = P.dma(f"ws{half}", "gsdma", dma(dst, src),
                                waits=[wstg_free[half]])
                    return sbuf, l_w

                def proj_mm(which, d, dd, sbuf, l_w, first_waits):
                    if which == "k" or which == "q":
                        def f(e, sbuf=sbuf, d=d, dd=dd):
                            r = None
                            for kf in range(8):
                                r = e.matmul(pb[kf],
                                             sbuf[:, dd * D + kf * 128: dd * D + (kf + 1) * 128],
                                             a_xnT[:, d * CH:(d + 1) * CH],
                                             start=(d == 0), stop=(d == 7))
                            return r
                    else:
                        def f(e, sbuf=sbuf, d=d, dd=dd):
                            r = None
                            for tt in range(4):
                                for vh in range(2):
                                    r = e.matmul(pb[tt * 2 + vh],
                                                 a_xnT[:, d * CH + tt * 128: d * CH + (tt + 1) * 128],
                                                 sbuf[:, dd * D + vh * 512: dd * D + (vh + 1) * 512],
                                                 start=(d == 0), stop=(d == 7))
                            return r
                    return P.op("tensor", f, waits=[l_w] + first_waits, signal=L(tag))

                def proj_sweep(which):
                    last_mm = None
                    for half in range(2):
                        sbuf, l_w = wload(which, half)
                        for dd in range(4):
                            d = half * 4 + dd
                            fw = []
                            if d == 0:
                                fw = xn_labels + [bank_owner.get(b) for b in range(8)]
                            if last_mm:
                                fw = fw + [last_mm]
                            last_mm = proj_mm(which, d, dd, sbuf, l_w, fw)
                        wstg_free[half] = last_mm
                    for b in range(8):
                        bank_owner[b] = last_mm
                    return last_mm

                toksl = slice(tc * CH, (tc + 1) * CH)

                last_mm = proj_sweep("k")
                for g in range(2):
                    rope4(g, toksl, tag, l_cos, l_sin, kdst_tc=tc)

                last_mm = proj_sweep("v")
                for tt in range(4):
                    for vh in range(2):
                        b = tt * 2 + vh
                        vtile = tc * 4 + tt
                        l_ev = P.op("vector", (lambda b, vtile, vh: lambda e: e.tensor_copy(
                            a_v[:, vtile * VTW + vh * 8 * (DH + 1):
                                vtile * VTW + (vh + 1) * 8 * (DH + 1)].rearrange(
                                "p (h u) -> p h u", u=DH + 1)[:, :, 0:DH],
                            pb[b][:].rearrange("p (h d) -> p h d", d=DH)))(b, vtile, vh),
                            waits=[bank_owner[b]], signal=L(tag))
                        bank_owner[b] = l_ev

                if pass_id == 0:
                    last_mm = proj_sweep("q")
                    rt1_w = a_scr[:, 0:1024].bitcast(BF16)
                    for g in range(2):
                        l_add, l_t2 = rope4(g, toksl, tag, l_cos, l_sin)
                        l_bl = P.op("tensor", (lambda g, rt1_w: lambda e: [
                            e.matmul(pb[qf][hp * 64:hp * 64 + 64, :],
                                     c_wbt[hp * 64:hp * 64 + 64,
                                           (2 * qf + hp) * 64:(2 * qf + hp + 1) * 64],
                                     rt1_w[hp * 64:hp * 64 + 64,
                                           (qf - g * 4) * 512:(qf - g * 4 + 1) * 512],
                                     start=True, stop=True)
                            for qf in range(g * 4, g * 4 + 4) for hp in range(2)][-1])(
                                g, rt1_w),
                            waits=[l_add, "wbt_ok", l_t2], signal=L(tag))
                        bank_owner[10] = l_bl
                        dstq = a_qT.rearrange("p (kf t) -> p kf t", t=OWN)[
                            :, g * 4:(g + 1) * 4, tc * CH:(tc + 1) * CH]
                        l_ev = P.op("vector", (lambda g, dstq: lambda e: e.tensor_copy(
                            dstq, PB[:, g * 2048:(g + 1) * 2048].rearrange(
                                "p (c t) -> p c t", t=512)))(g, dstq),
                            waits=[l_bl], signal=L(tag))
                        for i in range(4):
                            bank_owner[g * 4 + i] = l_ev
                xnT_free = last_mm

        def norm_sweep(pass_id, tag):
            srcs = [("k", a_kT, 8 + 8 * pass_id)]
            if pass_id == 0:
                srcs.append(("q", a_qT, 0))
            for nm, buf, slot0 in srcs:
                for kf in range(8):
                    l_sq = P.op("vector", (lambda buf, kf: lambda e: e.tensor_tensor(
                        out=a_ptw, in0=buf[:, kf * OWN:(kf + 1) * OWN],
                        in1=buf[:, kf * OWN:(kf + 1) * OWN], op=ALU.mult))(buf, kf),
                        waits=[prev_sq[0], bank_owner.get(9), bank_owner.get(10)],
                        signal=L(tag))
                    lms = []
                    for i in range(4):
                        lm = P.op("tensor", (lambda i: lambda e: e.matmul(
                            PB[0:1, i * 512:(i + 1) * 512], c_onescol,
                            a_ptw[:, i * 512:(i + 1) * 512],
                            start=True, stop=True))(i),
                            waits=[l_sq, l_ocol] +
                                  ([bank_owner.get(b) for b in range(4)] if True else []),
                            signal=L(tag))
                        bank_owner[i] = lm
                        lms.append(lm)
                    prev_sq[0] = lms[-1]
                    mslot = slot0 + kf
                    l_mx = P.chain("vector", [
                        lambda e: e.tensor_reduce(s_mx[0:1, 31:32], PB[0:1, 0:2048],
                                                  axis=AX.X, op=ALU.max),
                        (lambda mslot: lambda e: e.tensor_tensor(
                            out=s_mx[0:1, mslot:mslot + 1],
                            in0=s_mx[0:1, mslot:mslot + 1],
                            in1=s_mx[0:1, 31:32], op=ALU.max))(mslot),
                    ], waits=lms + ["mxinit"])
                    for i in range(4):
                        bank_owner[i] = l_mx
                    mx_last[0] = l_mx
                    bank_owner[9] = l_mx

        def bias_build(pass_id):
            tag = f"B{pass_id}"
            if pass_id == 0:
                l1 = P.op("vector", lambda e: e.tensor_tensor(
                    out=s_bv[0:1, 0:8], in0=s_mx[0:1, 0:8], in1=s_mx[0:1, 8:16], op=ALU.mult),
                    waits=[mx_last[0]], signal=L(tag))
                l2 = P.op("scalar", lambda e: e.activation(
                    out=s_bv[0:1, 0:8], in_=s_bv[0:1, 0:8], func=AF.Sqrt),
                    waits=[l1], signal=L(tag))
                l3 = P.op("vector", lambda e: e.tensor_scalar(
                    out=s_bv[0:1, 8:16], in0=s_bv[0:1, 0:8], scalar1=-1.0, scalar2=SHIFT,
                    op0=ALU.mult, op1=ALU.add), waits=[l2], signal=L(tag))
                lm = P.op("tensor", lambda e: e.matmul(
                    pb[0][:, 0:8], c_onesrow, s_bv[0:1, 8:16], start=True, stop=True),
                    waits=[l3, bank_owner.get(0)], signal=L(tag))
                bank_owner[0] = lm
                le = P.op("vector", lambda e: e.tensor_copy(s_bias[:, 0:8], pb[0][:, 0:8]),
                          waits=[lm], signal=L(tag))
                bank_owner[0] = le
                return le
            l1 = P.op("vector", lambda e: e.tensor_tensor(
                out=s_bv[0:1, 16:24], in0=s_mx[0:1, 0:8], in1=s_mx[0:1, 16:24], op=ALU.mult),
                waits=[mx_last[0]], signal=L(tag))
            l2 = P.op("scalar", lambda e: e.activation(
                out=s_bv[0:1, 16:24], in_=s_bv[0:1, 16:24], func=AF.Sqrt),
                waits=[l1], signal=L(tag))
            l3 = P.chain("vector", [
                lambda e: e.tensor_tensor(out=s_bv[0:1, 16:24], in0=s_bv[0:1, 16:24],
                                          in1=s_bv[0:1, 0:8], op=ALU.max),
                lambda e: e.tensor_scalar(out=s_bv[0:1, 24:32], in0=s_bv[0:1, 16:24],
                                          scalar1=-1.0, scalar2=SHIFT,
                                          op0=ALU.mult, op1=ALU.add),
                lambda e: e.tensor_tensor(out=s_bv[0:1, 16:24], in0=s_bv[0:1, 24:32],
                                          in1=s_bv[0:1, 8:16], op=ALU.subtract),
            ], waits=[l2])
            P.op("scalar", lambda e: e.preload_act_table(AF.Exp), waits=[l3],
                 signal=False)
            l4 = P.op("scalar", lambda e: e.activation(
                out=s_bv[0:1, 16:24], in_=s_bv[0:1, 16:24], func=AF.Exp),
                waits=[l3], signal=L(tag))
            lm = P.op("tensor", lambda e: (
                e.matmul(pb[0][:, 0:8], c_onesrow, s_bv[0:1, 24:32], start=True, stop=True),
                e.matmul(pb[1][:, 0:8], c_onesrow, s_bv[0:1, 16:24], start=True, stop=True))[-1],
                waits=[l4, bank_owner.get(0), bank_owner.get(1)], signal=L(tag))
            le = P.op("vector", lambda e: (
                e.tensor_copy(s_bias[:, 8:16], pb[0][:, 0:8]),
                e.tensor_copy(s_bias[:, 16:24], pb[1][:, 0:8]))[-1],
                waits=[lm], signal=L(tag))
            bank_owner[0] = le
            bank_owner[1] = le
            return le

        sums_last = [None]

        def sim_pass(pass_id, l_bias):
            tag = f"S{pass_id}"
            P.op("scalar", lambda e: e.preload_act_table(AF.Exp), waits=[l_bias],
                 signal=False)
            biascol = 0 if pass_id == 0 else 8
            warm_burst()
            for pr in range(NPAIR):
                h0, h1 = 2 * pr, 2 * pr + 1
                for qc in range(4):
                    qsl = slice(pr * OWN + qc * 512, pr * OWN + (qc + 1) * 512)
                    av_done = {}
                    exp_done = {}
                    def emit_sims(js):
                        for hp in (0, 1):
                            sb = (js % 2) * 2 + hp
                            ksl = slice(pr * OWN + js * 128, pr * OWN + (js + 1) * 128)
                            l_sim = P.op("tensor", (lambda sb, hp, ksl, qsl: lambda e: e.matmul(
                                pb[sb],
                                a_kT[hp * 64:(hp + 1) * 64, ksl],
                                a_qT[hp * 64:(hp + 1) * 64, qsl],
                                start=True, stop=True))(sb, hp, ksl, qsl),
                                waits=[bank_owner.get(sb)], signal=L(tag))
                            bank_owner[sb] = l_sim
                    def emit_exps(js):
                        par = js % 2
                        l_exp = P.op("scalar", (lambda par, pr: lambda e: e.activation(
                            out=a_pt2[par], in_=PB[:, par * 1024:(par + 1) * 1024],
                            func=AF.Exp,
                            bias=s_bias[:, biascol + pr: biascol + pr + 1]))(par, pr),
                            waits=[bank_owner[par * 2], bank_owner[par * 2 + 1],
                                   av_done.get((0, js - 2)), av_done.get((1, js - 2)),
                                   mx_last[0] if (pr == 0 and qc == 0 and js <= 1) else None],
                            signal=L(tag))
                        bank_owner[par * 2] = l_exp
                        bank_owner[par * 2 + 1] = l_exp
                        exp_done[(0, js)] = l_exp
                        exp_done[(1, js)] = l_exp
                    def emit_avs(js):
                        for hp, h in ((0, h0), (1, h1)):
                            ob = 4 + hp
                            pt = a_pt2[js % 2][:, hp * 512:(hp + 1) * 512]
                            wav = [exp_done[(hp, js)]]
                            if js == 0:
                                wav += [bank_owner.get(ob)]
                            l_av = P.op("tensor", (lambda ob, h, pt, js: lambda e:
                                e.matmul(pb[ob][0:128, :],
                                         a_v[:, js * VTW + h * (DH + 1):
                                             js * VTW + h * (DH + 1) + 128],
                                         pt, start=(js == 0), stop=(js == 15)))(
                                             ob, h, pt, js),
                                waits=wav, signal=L(tag))
                            av_done[(hp, js)] = l_av
                            if js == 15:
                                bank_owner[ob] = l_av
                    for js in range(16):
                        emit_sims(js)
                        emit_exps(js)
                        if js >= 1:
                            emit_avs(js - 1)
                    emit_avs(15)
                    l_e16 = P.op("vector", lambda e: (
                        e.tensor_copy(s_r64[64:65, 0:512], pb[4][64:65, :]),
                        e.tensor_copy(s_r64[64:65, 512:1024], pb[5][64:65, :]))[-1],
                        waits=[bank_owner[4], bank_owner[5], bank_owner.get(6)],
                        signal=L(tag))
                    bank_owner[4] = l_e16
                    bank_owner[5] = l_e16
                    l_tr = P.op("tensor", (lambda pr: lambda e: [
                        e.transpose(pb[6][:, qs * 16 + 2 * pr + hh: qs * 16 + 2 * pr + hh + 1],
                                    s_r64[64:65, hh * 512 + qs * 128: hh * 512 + (qs + 1) * 128],
                                    c_identf[64:65, 64:65])
                        for hh in range(2) for qs in range(4)][-1])(pr),
                        waits=[l_e16, l_idf, bank_owner[6]], signal=L(tag))
                    bank_owner[6] = l_tr
                    for hp, h in ((0, h0), (1, h1)):
                        ob = 4 + hp
                        asl = slice(pr * OWN + qc * 512, pr * OWN + (qc + 1) * 512)
                        dsta = a_aoT[hp * 64:hp * 64 + 64, asl]
                        if pass_id == 0:
                            l1 = P.op("vector", (lambda dsta, ob: lambda e: e.tensor_copy(
                                dsta, pb[ob][0:64, :]))(dsta, ob),
                                waits=[bank_owner[ob]], signal=L(tag))
                        else:
                            l1 = P.op("vector", (lambda dsta, ob, hp, pr: lambda e:
                                e.scalar_tensor_tensor(
                                    out=dsta, in0=dsta,
                                    scalar=s_bias[hp * 64:hp * 64 + 64, 16 + pr:17 + pr],
                                    in1=pb[ob][0:64, :], op0=ALU.mult, op1=ALU.add))(
                                        dsta, ob, hp, pr),
                                waits=[bank_owner[ob]], signal=L(tag))
                        bank_owner[ob] = l1
                    sTdst = s_sumsT.rearrange("p (s h) -> p s h", h=16)[
                        :, qc * 4:(qc + 1) * 4, 2 * pr:2 * pr + 2]
                    sTsrc = pb[6][:, 0:64].rearrange("p (s h) -> p s h", h=16)[
                        :, :, 2 * pr:2 * pr + 2]
                    if pass_id == 0:
                        l_sc = P.op("vector", (lambda sTdst, sTsrc: lambda e: e.tensor_copy(
                            sTdst, sTsrc))(sTdst, sTsrc),
                            waits=[l_tr], signal=L(tag))
                    else:
                        l_sc = P.op("vector", (lambda sTdst, sTsrc, pr: lambda e:
                            e.scalar_tensor_tensor(
                                out=sTdst, in0=sTdst,
                                scalar=s_bias[:, 16 + pr:17 + pr],
                                in1=sTsrc, op0=ALU.mult, op1=ALU.add))(sTdst, sTsrc, pr),
                            waits=[l_tr], signal=L(tag))
                    bank_owner[6] = l_sc
                    sums_last[0] = l_sc

        def attn_normalize():
            tag = "N"
            l_r = P.op("vector", lambda e: e.reciprocal(s_sumsT, s_sumsT),
                       waits=[sums_last[0]])
            for pr in range(NPAIR):
                for qc in range(4):
                    b = (pr * 4 + qc) % 4
                    asl = slice(pr * OWN + qc * 512, pr * OWN + (qc + 1) * 512)
                    src_r = s_sumsT.rearrange("p (s h) -> p s h", h=16)[
                        :, qc * 4:(qc + 1) * 4, 2 * pr:2 * pr + 2]
                    l_t = P.op("tensor", (lambda b, pr, qc: lambda e: [
                        e.transpose(pb[b][0:2, qs * 128:(qs + 1) * 128],
                                    s_sumsT[:, (qc * 4 + qs) * 16 + 2 * pr:
                                            (qc * 4 + qs) * 16 + 2 * pr + 2],
                                    c_identf)
                        for qs in range(4)][-1])(b, pr, qc),
                        waits=[l_r, l_idf, bank_owner.get(b)], signal=L(tag))
                    bank_owner[b] = l_t
                    l_c = P.op("vector", (lambda b: lambda e: e.tensor_copy(
                        s_sumb16[0:2, :], pb[b][0:2, :]))(b),
                        waits=[l_t], signal=L(tag))
                    bank_owner[b] = l_c
                    lm = P.op("tensor", (lambda b: lambda e: e.matmul(
                        pb[b], c_sel2[0:2, :], s_sumb16[0:2, :],
                        start=True, stop=True))(b),
                        waits=[l_c, l_sel2], signal=L(tag))
                    bank_owner[b] = lm
                    lt = P.op("vector", (lambda b, asl: lambda e: e.tensor_tensor(
                        out=a_aoT[:, asl], in0=a_aoT[:, asl], in1=pb[b],
                        op=ALU.mult))(b, asl),
                        waits=[lm], signal=L(tag))
                    bank_owner[b] = lt
            return lt

        import os as _os
        if not _os.environ.get("SKIP_A"):
            proj_pass(0)
            norm_sweep(0, "NSA")
            l_b = bias_build(0)
            if not _os.environ.get("SKIP_SIMA"):
                sim_pass(0, l_b)
        else:
            l_b = bias_build(0)
        if debug:
            P.op("sync", dma(dbg["kTA"][:], a_kT), waits=[l_b, sums_last[0]], out_store=True)
            ldbgA = P.op("sync", dma(dbg["vA"][:], a_v[:, 0:16 * VTW]),
                         waits=[l_b, sums_last[0]], out_store=True)
            P.raw_wait("sync", ldbgA)
        P.emit_segment("attnA")
        import os as _os
        proj_pass(1)
        norm_sweep(1, "NSB")
        l_b = bias_build(1)
        if not _os.environ.get("SKIP_SIMB"):
            sim_pass(1, l_b)
            l_fin = attn_normalize()
        else:
            l_fin = l_b
        if debug:
            P.op("sync", dma(dbg["kT"][:], a_kT), waits=[l_fin], out_store=True)
            P.op("sync", dma(dbg["qT"][:], a_qT), waits=[l_fin], out_store=True)
            P.op("sync", dma(dbg["v"][:], a_v[:, 0:16 * VTW]), waits=[l_fin], out_store=True)
            P.op("sync", dma(dbg["aoT"][:], a_aoT), waits=[l_fin], out_store=True)
            P.op("sync", dma(dbg["sums"][:], s_sumsT), waits=[l_fin], out_store=True)
            P.op("sync", dma(dbg["xnT"][:], a_xnT), waits=[l_fin], out_store=True)
            ldbg = P.op("sync", dma(dbg["bias"][:], s_bv[0:1, 0:32]), waits=[l_fin],
                        out_store=True)
            P.raw_wait("sync", ldbg)
        P.emit_segment("attnB")

        A.reset()
        A.mark()
        r_res = [A.f32(D) for _ in range(16)]
        f_w1 = A.bf16(8 * HID)
        r_wo = f_w1.bitcast(BF16)[:, 0:8 * D] if False else None
        r_wo = f_w1[:, 0:8 * D]
        f_ln2T = A.bf16(8 * OWN)
        f_w2 = A.bf16(16 * D)
        f_hT = A.bf16(16 * 512)
        f_comb = A.f32(128)
        f_combb = A.bf16(8)
        f_combT = A.bf16(OWN, rows=8)
        f_xn = A.bf16(D)
        f_g = A.f32(40)
        f_st = A.f32(16)
        f_mv = A.f32(4)
        f_rs = A.f32(8)
        assert A.off <= SBUF_BYTES, A.off

        bank_owner.clear()
        tag = "O"
        l_wo = P.dma("w1", "gsdma", dma(r_wo, wo[:]))
        res_labels = [None] * 16
        res_prev = [None] * 4
        warm_burst()
        for m in range(16):
            l_x = P.dma(f"rs{m % 4}", "sync",
                        dma(r_res[m][:], x_roll[m * 128:(m + 1) * 128, :]),
                        waits=[res_prev[m % 4]])
            for oh in range(2):
                b = (m * 2 + oh) % 4
                lm = P.op("tensor", (lambda m, oh, b: lambda e: [
                    e.matmul(pb[b],
                             a_aoT[:, c * OWN + m * 128: c * OWN + (m + 1) * 128],
                             r_wo[:, c * D + oh * 512: c * D + (oh + 1) * 512],
                             start=(c == 0), stop=(c == 7))
                    for c in range(8)][-1])(m, oh, b),
                    waits=[l_wo, bank_owner.get(b)], signal=L(tag))
                bank_owner[b] = lm
                le = P.op("vector", (lambda m, oh, b: lambda e: e.tensor_tensor(
                    out=r_res[m][:, oh * 512:(oh + 1) * 512],
                    in0=r_res[m][:, oh * 512:(oh + 1) * 512],
                    in1=pb[b], op=ALU.add))(m, oh, b),
                    waits=[lm, l_x], signal=L(tag))
                bank_owner[b] = le
                res_labels[m] = le
            res_prev[m % 4] = res_labels[m]
            oproj_last_mm = lm
        res_dump = [None] * 16
        if debug:
            for m in range(16):
                res_dump[m] = P.dma("dres", "sync",
                                    dma(dbg["res1"][m * 128:(m + 1) * 128, :], r_res[m][:]),
                                    waits=[res_labels[m]], serialize=True)

        tag = "F"
        ln_done = [None] * 16
        P.op("scalar", lambda e: e.preload_act_table(AF.Sqrt), signal=False)
        fxn_free = None
        for m in range(16):
            src = r_res[m][:]
            rsb = (m % 2) * 4
            l1 = P.chain("vector", [
                (lambda src: lambda e: (e.bn_stats(f_st[:, 0:6], src[:, 0:512]),
                                        e.bn_stats(f_st[:, 6:12], src[:, 512:1024]))[-1])(src),
                (lambda rsb: lambda e: e.bn_aggr(f_mv[:, 0:2], f_st[:, 0:12]))(rsb),
                (lambda rsb: lambda e: e.tensor_scalar(
                    out=f_rs[:, rsb:rsb + 1], in0=f_mv[:, 1:2], scalar1=EPS,
                    scalar2=None, op0=ALU.add))(rsb),
                (lambda rsb: lambda e: e.reciprocal(
                    f_rs[:, rsb:rsb + 1], f_rs[:, rsb:rsb + 1]))(rsb),
            ], waits=[res_labels[m]])
            l2 = P.op("scalar", (lambda rsb: lambda e: e.activation(
                out=f_rs[:, rsb + 1:rsb + 2], in_=f_rs[:, rsb:rsb + 1], func=AF.Sqrt))(rsb),
                waits=[l1], signal=L(tag))
            l3a = P.op("vector", (lambda rsb: lambda e: e.tensor_scalar(
                out=f_rs[:, rsb + 2:rsb + 3], in0=f_mv[:, 0:1],
                scalar1=f_rs[:, rsb + 1:rsb + 2], scalar2=-1.0,
                op0=ALU.mult, op1=ALU.mult))(rsb), waits=[l2])
            l3 = P.op("scalar", (lambda src, rsb: lambda e: e.activation(
                out=f_xn, in_=src, func=AF.Identity,
                scale=f_rs[:, rsb + 1:rsb + 2], bias=f_rs[:, rsb + 2:rsb + 3]))(src, rsb),
                waits=[l3a] + ([fxn_free] if fxn_free else []))
            l_tr = P.op("tensor", lambda e: [
                e.transpose(pbT[:, d * 128:(d + 1) * 128],
                            f_xn[:, d * 128:(d + 1) * 128], c_ident)
                for d in range(8)][-1],
                waits=[l3, l_ident, bank_owner.get(7)], signal=L(tag))
            bank_owner[7] = l_tr
            fxn_free = l_tr
            l_ev = P.op("vector", (lambda m: lambda e: e.tensor_copy(
                f_ln2T.rearrange("p (d c) -> p d c", d=8)[:, :, m * 128:(m + 1) * 128],
                pbT.rearrange("p (d c) -> p d c", d=8)))(m),
                waits=[l_tr], signal=L(tag))
            bank_owner[7] = l_ev
            ln_done[m] = l_ev

        P.op("scalar", lambda e: e.preload_act_table(AF.Exp), signal=False)
        comb_ready = [None] * 16
        combT_ready = [None] * 16
        for m in range(16):
            b = m % 2
            lm = P.op("tensor", (lambda m, b: lambda e: (
                [e.matmul(pb[b][:, 0:8],
                          f_ln2T[:, d * OWN + m * 128: d * OWN + (m + 1) * 128],
                          c_wg[:, d * 8:(d + 1) * 8], start=(d == 0), stop=False)
                 for d in range(8)],
                e.matmul(pb[b][:, 0:8], c_onesrow, c_bg, start=False, stop=True))[1])(m, b),
                waits=[l_wg, l_bg, l_orow, ln_done[m], bank_owner.get(b)], signal=L(tag))
            bank_owner[b] = lm
            l_g1 = P.chain("vector", [
                (lambda b: lambda e: e.tensor_copy(f_g[:, 0:8], pb[b][:, 0:8]))(b),
                lambda e: e.tensor_reduce(f_g[:, 8:9], f_g[:, 0:8], axis=AX.X, op=ALU.max),
                lambda e: e.tensor_scalar(out=f_g[:, 16:24], in0=f_g[:, 0:8],
                                          scalar1=f_g[:, 8:9], scalar2=None,
                                          op0=ALU.is_equal),
                lambda e: e.scalar_tensor_tensor(out=f_g[:, 24:32], in0=f_g[:, 16:24],
                                                 scalar=-1e30, in1=f_g[:, 0:8],
                                                 op0=ALU.mult, op1=ALU.add),
                lambda e: e.tensor_reduce(f_g[:, 9:10], f_g[:, 24:32], axis=AX.X,
                                          op=ALU.max),
                lambda e: e.tensor_scalar(out=f_g[:, 10:11], in0=f_g[:, 8:9],
                                          scalar1=-1.0, scalar2=None, op0=ALU.mult),
            ], waits=[lm])
            bank_owner[b] = l_g1
            l_g2 = P.op("scalar", lambda e: e.activation(
                out=f_g[:, 11:12], in_=f_g[:, 9:10], func=AF.Exp, bias=f_g[:, 10:11]),
                waits=[l_g1], signal=L(tag))
            l_g3 = P.chain("vector", [
                lambda e: e.tensor_scalar(out=f_g[:, 12:13], in0=f_g[:, 11:12],
                                          scalar1=1.0, scalar2=None, op0=ALU.add),
                lambda e: e.reciprocal(f_g[:, 12:13], f_g[:, 12:13]),
                lambda e: e.tensor_tensor(out=f_g[:, 13:14], in0=f_g[:, 11:12],
                                          in1=f_g[:, 12:13], op=ALU.mult),
                lambda e: e.tensor_scalar(out=f_g[:, 24:32], in0=f_g[:, 0:8],
                                          scalar1=f_g[:, 9:10], scalar2=None,
                                          op0=ALU.is_equal),
                lambda e: e.tensor_scalar(out=f_g[:, 32:40], in0=f_g[:, 16:24],
                                          scalar1=-1.0, scalar2=1.0,
                                          op0=ALU.mult, op1=ALU.add),
                lambda e: e.tensor_tensor(out=f_g[:, 24:32], in0=f_g[:, 24:32],
                                          in1=f_g[:, 32:40], op=ALU.mult),
                lambda e: e.tensor_scalar(out=f_g[:, 16:24], in0=f_g[:, 16:24],
                                          scalar1=f_g[:, 12:13], scalar2=None,
                                          op0=ALU.mult),
                lambda e: e.tensor_scalar(out=f_g[:, 24:32], in0=f_g[:, 24:32],
                                          scalar1=f_g[:, 13:14], scalar2=None,
                                          op0=ALU.mult),
                (lambda m: lambda e: e.tensor_tensor(
                    out=f_comb[:, m * 8:(m + 1) * 8], in0=f_g[:, 16:24],
                    in1=f_g[:, 24:32], op=ALU.add))(m),
                (lambda m: lambda e: e.tensor_copy(
                    f_combb[:, 0:8], f_comb[:, m * 8:(m + 1) * 8]))(m),
            ], waits=[l_g2])
            comb_ready[m] = l_g3
            l_tc = P.op("tensor", lambda e: e.transpose(pbT[0:8, 0:128], f_combb[:, 0:8], c_ident),
                        waits=[l_g3, l_ident, bank_owner.get(7)], signal=L(tag))
            bank_owner[7] = l_tc
            l_te = P.op("vector", (lambda m: lambda e: e.tensor_copy(
                f_combT[0:8, m * 128:(m + 1) * 128], pbT[0:8, 0:128]))(m),
                waits=[l_tc], signal=L(tag))
            bank_owner[7] = l_te
            combT_ready[m] = l_te
        if debug:
            P.op("sync", dma(dbg["comb"][:], f_comb), waits=comb_ready, out_store=True)
            ldbg = P.op("sync", dma(dbg["ln2T"][:], f_ln2T), waits=ln_done, out_store=True)
            P.raw_wait("sync", ldbg)

        mm1_done = None
        mm2_done = None
        relu_done = {}
        store_done = [None] * 16
        for ex in range(NEXP):
            w1_ready = P.dma("w1", "gsdma", (lambda ex: lambda e: e.dma_start(
                out=f_w1, in_=w1t[ex]))(ex),
                waits=[mm1_done if ex else oproj_last_mm])
            w2_ready = P.dma("w2", "gsdma", (lambda ex: lambda e: e.dma_start(
                out=f_w2, in_=w2t[ex]))(ex), waits=[mm2_done])
            for tc in range(4):
                for hf in range(16):
                    b = hf % 4
                    wmm = [w1_ready, bank_owner.get(b)]
                    if ex == 0 and tc == 0 and hf < 4:
                        wmm += ln_done
                    lm = P.op("tensor", (lambda ex, tc, hf, b: lambda e: [
                        e.matmul(pb[b],
                                 f_w1[:, d * HID + hf * 128: d * HID + (hf + 1) * 128],
                                 f_ln2T[:, d * OWN + tc * 512: d * OWN + (tc + 1) * 512],
                                 start=(d == 0), stop=(d == 7))
                        for d in range(8)][-1])(ex, tc, hf, b),
                        waits=wmm, signal=L(tag))
                    bank_owner[b] = lm
                    lr = P.op("scalar", (lambda ex, hf, b: lambda e: e.activation(
                        out=f_hT[:, hf * 512:(hf + 1) * 512], in_=pb[b], func=AF.Relu,
                        bias=c_b1[:, ex * 16 + hf: ex * 16 + hf + 1]))(ex, hf, b),
                        waits=[lm, l_b1, mm2_done], signal=L(tag))
                    bank_owner[b] = lr
                    relu_done[hf] = lr
                mm1_done = relu_done[15]
                for m4 in range(4):
                    m = tc * 4 + m4
                    for oh in range(2):
                        b = 4 + (m4 * 2 + oh) % 4
                        wmm = [w2_ready, bank_owner.get(b)] + \
                              [relu_done[hf] for hf in range(16)]
                        if ex == 0:
                            wmm += [combT_ready[m], l_b2]
                        def mk_mm2(ex, m4, m, oh, b):
                            def f(e):
                                last = None
                                for hf in range(16):
                                    last = e.matmul(
                                        pb[b],
                                        f_hT[:, hf * 512 + m4 * 128: hf * 512 + (m4 + 1) * 128],
                                        f_w2[:, hf * D + oh * 512: hf * D + (oh + 1) * 512],
                                        start=(hf == 0), stop=(hf == 15 and ex != 0))
                                if ex == 0:
                                    last = e.matmul(
                                        pb[b], f_combT[0:8, m * 128:(m + 1) * 128],
                                        c_b2[0:8, oh * 512:(oh + 1) * 512],
                                        start=False, stop=True)
                                return last
                            return f
                        lm = P.op("tensor", mk_mm2(ex, m4, m, oh, b), waits=wmm,
                                  signal=L(tag))
                        bank_owner[b] = lm
                        le = P.op("vector", (lambda ex, m, oh, b: lambda e:
                            e.scalar_tensor_tensor(
                                out=r_res[m][:, oh * 512:(oh + 1) * 512],
                                in0=pb[b],
                                scalar=f_comb[:, m * 8 + ex: m * 8 + ex + 1],
                                in1=r_res[m][:, oh * 512:(oh + 1) * 512],
                                op0=ALU.mult, op1=ALU.add))(ex, m, oh, b),
                            waits=[lm, comb_ready[m], store_done[m]] +
                                  ([res_dump[m]] if ex == 0 else []),
                            signal=L(tag))
                        bank_owner[b] = le
                        res_labels[m] = le
                mm2_done = bank_owner[4 + 7 % 4]
            if ex == NEXP - 1:
                for m in range(16):
                    store_done[m] = P.op("sync", (lambda m: lambda e: e.dma_start(
                        out=out[m * 128:(m + 1) * 128, :], in_=r_res[m][:]))(m),
                        waits=[res_labels[m]], out_store=True)
        P.raw_wait("sync", store_done[15])
        P.emit_segment("ffn")

    return nc


def _host_constants():
    i = np.arange(0, DH, 2, dtype=np.float64) / DH
    theta = 1.0 / (10000.0 ** i)
    pos = np.arange(SEQ, dtype=np.float64)
    ang = pos[None, :] * theta[(np.arange(128) % 32)][:, None]
    cosfm = np.cos(ang).astype(np.float32)
    sinfm = np.sin(ang).astype(np.float32)
    identb = np.eye(128, dtype=np.float32)
    rot = np.zeros((128, 128), dtype=np.float32)
    for k in range(64):
        rot[2 * k + 1, 2 * k] = -1.0
        rot[2 * k, 2 * k + 1] = 1.0
    sel2 = np.zeros((2, 128), dtype=np.float32)
    sel2[0, 0:64] = 1.0
    sel2[1, 64:128] = 1.0
    bf = lambda a: np.ascontiguousarray(a.astype(ml_dtypes.bfloat16))
    return dict(costab=bf(cosfm), sintab=bf(sinfm), identb=bf(identb),
                rotmat=bf(rot), ones128b=bf(np.ones((128, 128), np.float32)),
                onescol_b=bf(np.ones((128, 1), np.float32)),
                onesrow_f=np.ones((1, 128), np.float32),
                sel2=bf(sel2), identf=identb.copy())


def make_in_maps(L0_attn_x, L0_attn_gamma, L0_attn_Wq, L0_attn_Wkv, L0_attn_W_bilinear,
                 L0_attn_Wo, L0_ffn_W1, L0_ffn_b1, L0_ffn_W2, L0_ffn_b2, L0_ffn_Wg,
                 L0_ffn_bg):
    C = np.ascontiguousarray
    x = np.asarray(L0_attn_x, dtype=np.float32)
    consts = _host_constants()
    A = np.asarray(L0_attn_W_bilinear, np.float32).transpose(0, 2, 1)
    wb_half = C(A.transpose(1, 0, 2).reshape(64, H * DH))
    wbt2 = np.concatenate([wb_half, wb_half], axis=0)
    shared = dict(
        gamma=C(np.asarray(L0_attn_gamma, np.float32)[None, :]),
        wq=C(np.asarray(L0_attn_Wq, np.float32)),
        wkv=C(np.asarray(L0_attn_Wkv, np.float32)),
        wbt2=C(wbt2),
        wo=C(np.asarray(L0_attn_Wo, np.float32).reshape(8, 128, D)
             .transpose(1, 0, 2).reshape(128, 8 * D)),
        w1t=C(np.asarray(L0_ffn_W1, np.float32).transpose(0, 2, 1)
              .reshape(NEXP, 8, 128, HID).transpose(0, 2, 1, 3)
              .reshape(NEXP, 128, 8 * HID)),
        w2t=C(np.asarray(L0_ffn_W2, np.float32).transpose(0, 2, 1)
              .reshape(NEXP, 16, 128, D).transpose(0, 2, 1, 3)
              .reshape(NEXP, 128, 16 * D)),
        b1c=C(np.asarray(L0_ffn_b1, np.float32).reshape(NEXP, HID // 128, 128)
              .transpose(2, 0, 1).reshape(128, NEXP * (HID // 128))),
        b2=C(np.asarray(L0_ffn_b2, np.float32)),
        wgt=C(np.asarray(L0_ffn_Wg, np.float32).T.reshape(8, 128, NEXP)
              .transpose(1, 0, 2).reshape(128, 8 * NEXP)),
        bg=C(np.asarray(L0_ffn_bg, np.float32)[None, :]),
        **consts,
    )
    in_maps = []
    for core in range(8):
        s, o = core // 2, core % 2
        xo = x[s, o * OWN:(o + 1) * OWN]
        xr = x[s, (1 - o) * OWN:(2 - o) * OWN]
        m = dict(shared)
        m["x_roll"] = C(np.concatenate([xo, xr], axis=0))
        own_sl = slice(o * OWN, (o + 1) * OWN)
        oth_sl = slice((1 - o) * OWN, (2 - o) * OWN)
        m["costab"] = C(np.concatenate([consts["costab"][:, own_sl],
                                        consts["costab"][:, oth_sl]], axis=1))
        m["sintab"] = C(np.concatenate([consts["sintab"][:, own_sl],
                                        consts["sintab"][:, oth_sl]], axis=1))
        in_maps.append(m)
    return in_maps


_NC_CACHE = {}


def kernel(**inputs):
    if "main" not in _NC_CACHE:
        _NC_CACHE["main"] = build_nc(debug=False)
    nc = _NC_CACHE["main"]
    in_maps = make_in_maps(**inputs)
    res = run_bass_kernel_spmd(nc, in_maps, core_ids=list(range(8)))
    outp = np.empty((4, SEQ, D), dtype=np.float32)
    for core in range(8):
        s, o = core // 2, core % 2
        outp[s, o * OWN:(o + 1) * OWN] = res.results[core]["out"]
    return outp
